# version 06
# baseline (speedup 1.0000x reference)
import numpy as np
from contextlib import ExitStack
import concourse.bass as bass
import concourse.mybir as mybir
from concourse.bass_utils import run_bass_kernel_spmd

F32 = mybir.dt.float32
BF16 = mybir.dt.bfloat16
I32 = mybir.dt.int32
U32 = mybir.dt.uint32
AF = mybir.ActivationFunctionType
ALU = mybir.AluOpType
AX = mybir.AxisListType

D = 1024
NCH = 8
SEQ = 2048
NSEQ = 2
EPS = 1e-5
DFF = 4096
NEXP = 8

ENGINES = ("sync", "gpsimd", "tensor", "vector", "scalar")



_UNIQ = [0]


def _sb(nc, name, shape, dt, **kw):
    _UNIQ[0] += 1
    return nc.sbuf_tensor(f"{name}_{_UNIQ[0]}", shape, dt, **kw)


def _psm(nc, name, shape, dt):
    _UNIQ[0] += 1
    return nc.psum_tensor(f"{name}_{_UNIQ[0]}", shape, dt)


class Prog:

    def __init__(self, nc):
        self.nc = nc
        self.phases = []
        self.cur = None
        self.sems = {}
        self.cnt = {}
        self.stack = ExitStack()
        self.new_phase()

    def sem(self, name):
        if name not in self.sems:
            self.sems[name] = self.stack.enter_context(self.nc.semaphore(name))
            self.cnt[name] = 0
        return self.sems[name]

    def new_phase(self):
        self.cur = {e: [] for e in ENGINES}
        self.phases.append(self.cur)

    def emit(self, eng, fn):
        self.cur[eng].append(fn)

    def inc(self, sem_name, n=1):
        self.sem(sem_name)
        self.cnt[sem_name] += n
        return self.cnt[sem_name]

    def wait(self, eng, sem_name, val):
        h = self.sem(sem_name)
        self.emit(eng, lambda e, h=h, v=val: e.wait_ge(h, v))

    serialize = ()

    def op(self, eng, method, *args, sem=None, n=1, **kw):
        ser = eng in self.serialize and eng in ("vector", "scalar", "gpsimd",
                                                "tensor")
        if ser:
            last = getattr(self, "_last_op", {}).get(eng)
            if last is not None:
                self.wait(eng, last[0], last[1])
            if sem is None:
                sem, n = "_ser_" + eng, 1
        t = None
        h = None
        if sem is not None:
            t = self.inc(sem, n)
            h = self.sems[sem]

        def fn(e, method=method, args=args, kw=kw, h=h, n=n):
            instr = getattr(e, method)(*args, **kw)
            if h is not None:
                instr.then_inc(h, n)
        self.emit(eng, fn)
        if ser:
            if not hasattr(self, "_last_op"):
                self._last_op = {}
            self._last_op[eng] = (sem, t)
        return t

    def dma(self, q, out, in_, sem="dma", **kw):
        return self.op(q, "dma_start", sem=sem, n=16, out=out, in_=in_, **kw)

    def build(self):
        for ph in self.phases:
            if not any(ph[e] for e in ENGINES):
                continue
            with self.nc.Block() as block:
                for eng in ENGINES:
                    if ph[eng]:
                        def body(E, ops=tuple(ph[eng])):
                            for f in ops:
                                f(E)
                        getattr(block, eng)(body)
        self.stack.close()


class Commons:

    def __init__(self, P):
        nc, st = P.nc, P.stack
        self.ident_f = st.enter_context(_sb(nc, "c_identf", [128, 128], F32))
        self.ident_b = st.enter_context(_sb(nc, "c_identb", [128, 128], BF16))
        self.ones_b = st.enter_context(_sb(nc, "c_onesb", [128, 1], BF16))
        self.ones_f = st.enter_context(_sb(nc, "c_onesf", [1, 128], F32))
        self.iota_f = st.enter_context(_sb(nc, "c_iotaf", [128, 128], F32))
        self.iota_p = st.enter_context(_sb(nc, "c_iotap", [128, 128], F32))
        self.eps_c = st.enter_context(_sb(nc, "c_eps", [128, 1], F32))
        P.op("gpsimd", "memset", self.eps_c[:], EPS)
        P.op("gpsimd", "iota", self.iota_f[:], [[1, 128]], channel_multiplier=0,
             allow_small_or_imprecise_dtypes=True)
        P.op("gpsimd", "iota", self.iota_p[:], [[0, 128]], channel_multiplier=1,
             allow_small_or_imprecise_dtypes=True)
        P.op("gpsimd", "memset", self.ones_b[:], 1.0)
        P.op("gpsimd", "memset", self.ones_f[:], 1.0, sem="pool")
        P.wait("vector", "pool", P.cnt["pool"])
        P.op("vector", "tensor_tensor", self.ident_f[:], self.iota_f[:],
             self.iota_p[:], ALU.is_equal)
        P.op("vector", "tensor_copy", self.ident_b[:], self.ident_f[:], sem="dve")
        self.ready = P.cnt["dve"]


def load_pp(P, com, st, hbm_2d, n, name, ps=None, guard=0):
    nc = P.nc
    dest = st.enter_context(_sb(nc, name, [128, n], F32))
    td = P.dma("sync", dest[:], hbm_2d.rearrange("n p -> p n"),
               allow_slow_non_contiguous=True)
    return dest, td



def emit_prep(P, com, io):
    nc = P.nc
    P.new_phase()
    st = ExitStack()
    x_in = io["x_in"]
    x_fm = io["x_fm"]

    if "W1c" in io:
        for e in range(NEXP):
            for src_t, dst_t in (("W1", "W1c"), ("W2", "W2c"),
                                 ("W3", "W3c")):
                a, b = io[src_t][e], io[dst_t][e]
                if a.shape[-1] > 2048:
                    a = a.rearrange("a (b f) -> a b f", f=2048)
                    b = b.rearrange("a (b f) -> a b f", f=2048)
                P.dma("gpsimd", b, a, sem="dmag")
    if "wqkvo_c" in io:
        for i, nm in enumerate(("W_Q", "W_K", "W_V", "W_o")):
            P.dma("gpsimd", io["wqkvo_c"][i], io[nm], sem="dmag")
        P.dma("gpsimd", io["wr_c"], io["Wr"], sem="dmag")
    t_wcache = P.cnt.get("dmag", 0)

    xt = st.enter_context(_sb(nc, "pr_xt", [128, 1024], F32))
    xo = st.enter_context(_sb(nc, "pr_xo", [128, 8, 128], F32))
    pst = [st.enter_context(_psm(nc, f"pr_ps{i}", [128, 128], F32))
           for i in range(2)]
    t_st = 0
    for s in range(NSEQ):
        for tc in range(16):
            P.wait("sync", "dve", P.cnt.get("dve", 0))
            td = P.dma("sync", xt[:], x_in[s, tc * 128:(tc + 1) * 128, :])
            P.wait("tensor", "dma", td)
            P.wait("tensor", "dve", com.ready)
            for dc in range(NCH):
                pb = pst[dc % 2]
                tp = P.op("tensor", "transpose", pb[:],
                          xt[:, dc * 128:(dc + 1) * 128], com.ident_f[:],
                          sem="pe")
                P.wait("vector", "pe", tp)
                if dc == 0:
                    P.wait("vector", "st", t_st)
                tv = P.op("vector", "tensor_copy", xo[:, dc, :], pb[:],
                          sem="dve")
                P.wait("tensor", "dve", tv)
            P.wait("sync", "dve", P.cnt["dve"])
            t_st = P.dma("sync",
                         x_fm[s, :, :, tc * 128:(tc + 1) * 128]
                         .rearrange("c p t -> p c t"), xo[:], sem="st")

    wrow = st.enter_context(_sb(nc, "pr_wrow", [128, 1024], F32))
    wbf = st.enter_context(_sb(nc, "pr_wbf", [128, 1024], BF16))
    qsb = st.enter_context(_sb(nc, "pr_qsb", [128, 1024], I32))
    scl = st.enter_context(_sb(nc, "pr_scl", [128, 64], F32))
    psb = [st.enter_context(_psm(nc, f"pr_psb{i}", [128, 128], BF16))
           for i in range(2)]
    wtbig = st.enter_context(_sb(nc, "pr_wtbig", [128, 8, 4096], BF16))

    def transpose_mat(src_rows_get, n_out, out_ap3):
        nonlocal t_st
        for oc in range(n_out // 128):
            tready = src_rows_get(oc)
            P.wait("tensor", "dve", tready)
            for dc in range(NCH):
                pb = psb[dc % 2]
                tp = P.op("tensor", "transpose", pb[:],
                          wbf[:, dc * 128:(dc + 1) * 128], com.ident_b[:],
                          sem="pe")
                P.wait("vector", "pe", tp)
                if dc == 0 and oc == 0:
                    P.wait("vector", "st", t_st)
                tv = P.op("vector", "tensor_copy",
                          wtbig[:, dc, oc * 128:(oc + 1) * 128], pb[:],
                          sem="dve")
                P.wait("tensor", "dve", tv)
        P.wait("sync", "dve", P.cnt["dve"])
        t_st = P.dma("sync", out_ap3, wtbig[:, :, 0:n_out], sem="st")

    if "q" in io:
        qw, scales, wt_out = io["q"], io["scales"], io["wt_ffn0"]
        for i in range(6):
            def get_rows(oc, i=i):
                P.wait("sync", "dve", P.cnt["dve"])
                td = P.dma("sync", qsb[:], qw[i, oc * 128:(oc + 1) * 128, :])
                ts_ = P.dma("sync", scl[:], scales[i, oc * 128:(oc + 1) * 128, :])
                P.wait("vector", "dma", max(td, ts_))
                P.op("vector", "tensor_scalar", wrow[:], qsb[:],
                     1.0 / 7.5, -1.0, ALU.mult, op1=ALU.add)
                scl_b = bass.AP(scl.ap().tensor, 0, [[64, 128], [1, 64], [0, 16]])
                return P.op("vector", "tensor_tensor", wbf[:], wrow[:],
                            scl_b, ALU.mult, sem="dve")
            transpose_mat(get_rows, 1024,
                          wt_out[i].rearrange("(c p) o -> p c o", p=128))

    def plain_rows(src2d):
        def get_rows(oc):
            P.wait("sync", "dve", P.cnt["dve"])
            td = P.dma("sync", wrow[:], src2d[oc * 128:(oc + 1) * 128, :])
            P.wait("vector", "dma", td)
            return P.op("vector", "tensor_copy", wbf[:], wrow[:], sem="dve")
        return get_rows

    if "w_proj" in io:
        transpose_mat(plain_rows(io["w_proj"]), 4096,
                      io["wprojT"].rearrange("(c p) o -> p c o", p=128))
        transpose_mat(plain_rows(io["w_delta"]), 1024,
                      io["wdeltaT"].rearrange("(c p) o -> p c o", p=128))

        wdT = st.enter_context(_sb(nc, "pr_wdT", [128, 8, 1024], BF16))
        wpr = st.enter_context(_sb(nc, "pr_wpr", [128, 8, 1024], BF16))
        P.wait("sync", "st", t_st)
        td = P.dma("sync", wdT[:],
                   io["wdeltaT"].rearrange("(c p) o -> p c o", p=128))
        for o1c in range(8):
            P.wait("sync", "dve", P.cnt["dve"])
            tw = P.dma("sync", wrow[:],
                       io["w_proj"][o1c * 128:(o1c + 1) * 128, :])
            P.wait("vector", "dma", tw)
            P.op("vector", "tensor_copy", wpr[:, o1c, :], wrow[:], sem="dve")
        tcast = P.cnt["dve"]
        P.wait("tensor", "dma", td)
        P.wait("tensor", "dve", tcast)
        pcmb = [st.enter_context(_psm(nc, f"pr_pc{i}", [128, 512], F32))
                for i in range(2)]
        ev_t = [0, 0]
        P.wait("vector", "st", t_st)
        for dc in range(8):
            for nt in range(2):
                k = (dc * 2 + nt) % 2
                P.wait("tensor", "dve", ev_t[k])
                tm = None
                for o1c in range(8):
                    tm = P.op("tensor", "matmul", pcmb[k][:],
                              wpr[:, o1c, dc * 128:(dc + 1) * 128],
                              wdT[:, o1c, nt * 512:(nt + 1) * 512],
                              start=(o1c == 0), stop=(o1c == 7),
                              sem="pe" if o1c == 7 else None)
                P.wait("vector", "pe", tm)
                ev_t[k] = P.op("vector", "tensor_copy",
                               wtbig[:, dc, nt * 512:(nt + 1) * 512],
                               pcmb[k][:], sem="dve")
        P.wait("sync", "dve", P.cnt["dve"])
        t_st = P.dma("sync",
                     io["wcombT"].rearrange("(c p) o -> p c o", p=128),
                     wtbig[:, :, 0:1024], sem="st")

    if "lora_a" in io:
        la, lb, laT_h, lbT_h = io["lora_a"], io["lora_b"], io["laT"], io["lbT"]
        lasb = st.enter_context(_sb(nc, "pr_lasb", [32, 1024], F32))
        lasbb = st.enter_context(_sb(nc, "pr_lasbb", [32, 1024], BF16))
        laacc = st.enter_context(_sb(nc, "pr_laacc", [128, 8, 32], BF16))
        for i in range(6):
            P.wait("sync", "dve", P.cnt["dve"])
            td = P.dma("sync", lasb[:], la[i])
            P.wait("vector", "dma", td)
            tv = P.op("vector", "tensor_copy", lasbb[:], lasb[:], sem="dve")
            P.wait("tensor", "dve", tv)
            for dc in range(NCH):
                tp = P.op("tensor", "transpose", psb[dc % 2][:, 0:32],
                          lasbb[:, dc * 128:(dc + 1) * 128],
                          com.ident_b[0:32, 0:32], sem="pe")
                P.wait("vector", "pe", tp)
                if dc == 0:
                    P.wait("vector", "st", t_st)
                tv = P.op("vector", "tensor_copy", laacc[:, dc, :],
                          psb[dc % 2][:, 0:32], sem="dve")
                P.wait("tensor", "dve", tv)
            P.wait("sync", "dve", P.cnt["dve"])
            t_st = P.dma("sync", laT_h[:, i, :, :], laacc[:], sem="st")

        lbsb = st.enter_context(_sb(nc, "pr_lbsb", [128, 32], F32))
        lbsbb = st.enter_context(_sb(nc, "pr_lbsbb", [128, 32], BF16))
        lbacc = st.enter_context(_sb(nc, "pr_lbacc", [32, 8, 128], BF16))
        for i in range(6):
            for oc in range(NCH):
                P.wait("sync", "dve", P.cnt["dve"])
                td = P.dma("sync", lbsb[:], lb[i, oc * 128:(oc + 1) * 128, :])
                P.wait("vector", "dma", td)
                tv = P.op("vector", "tensor_copy", lbsbb[:], lbsb[:], sem="dve")
                P.wait("tensor", "dve", tv)
                tp = P.op("tensor", "transpose", psb[oc % 2][0:32, :],
                          lbsbb[:], com.ident_b[:], sem="pe")
                P.wait("vector", "pe", tp)
                if oc == 0:
                    P.wait("vector", "st", t_st)
                tv = P.op("vector", "tensor_copy", lbacc[:, oc, :],
                          psb[oc % 2][0:32, :], sem="dve")
                P.wait("tensor", "dve", tv)
            P.wait("sync", "dve", P.cnt["dve"])
            t_st = P.dma("sync", lbT_h[:, i, :, :], lbacc[:], sem="st")

    P.wait("sync", "st", P.cnt["st"])
    st.close()



def emit_ln_stats(P, com, src3d, T, xb, xsq, rows, stat_ps, bcast_mb, bcast_rb,
                  wait_act=None):
    nh = T // 512
    P.wait("scalar", "pe", P.cnt.get("pe", 0))
    if wait_act is not None:
        P.wait("scalar", wait_act[0], wait_act[1])
    P.op("scalar", "activation", xb[:, :, 0:T], src3d, AF.Copy)
    ta = P.op("scalar", "activation", xsq[:, :, 0:T], src3d, AF.Square,
              sem="act")
    P.wait("tensor", "act", ta)
    P.wait("tensor", "dve", com.ready)
    tp = None
    for h in range(nh):
        for c in range(NCH):
            P.op("tensor", "matmul", stat_ps[h], com.ones_b[:],
                 xb[:, c, h * 512:(h + 1) * 512],
                 start=(c == 0), stop=(c == NCH - 1))
        for c in range(NCH):
            last = (h == nh - 1 and c == NCH - 1)
            tp = P.op("tensor", "matmul", stat_ps[nh + h], com.ones_b[:],
                      xsq[:, c, h * 512:(h + 1) * 512],
                      start=(c == 0), stop=(c == NCH - 1),
                      sem="pe" if last else None)
    P.wait("vector", "pe", tp)
    for h in range(nh):
        P.op("vector", "tensor_scalar", rows[0:1, 0, h * 512:(h + 1) * 512],
             stat_ps[h], -1.0 / D, None, ALU.mult)
        P.op("vector", "tensor_scalar", rows[0:1, 1, h * 512:(h + 1) * 512],
             stat_ps[nh + h], 1.0 / D, None, ALU.mult)
    P.op("vector", "tensor_tensor", rows[0:1, 2, 0:T], rows[0:1, 0, 0:T],
         rows[0:1, 0, 0:T], ALU.mult)
    tv = P.op("vector", "tensor_tensor", rows[0:1, 2, 0:T], rows[0:1, 1, 0:T],
              rows[0:1, 2, 0:T], ALU.subtract, sem="dve")
    P.wait("scalar", "dve", tv)
    P.op("scalar", "activation", rows[0:1, 3, 0:T], rows[0:1, 2, 0:T],
         AF.Ln, bias=com.eps_c[0:1, :])
    ta = P.op("scalar", "activation", rows[0:1, 3, 0:T], rows[0:1, 3, 0:T],
              AF.Exp, scale=-0.5, sem="act")
    P.wait("tensor", "act", ta)
    tb = None
    for h in range(nh):
        P.op("tensor", "matmul", bcast_mb[h], com.ones_f[:],
             rows[0:1, 0, h * 512:(h + 1) * 512], start=True, stop=True)
        tb = P.op("tensor", "matmul", bcast_rb[h], com.ones_f[:],
                  rows[0:1, 3, h * 512:(h + 1) * 512], start=True, stop=True,
                  sem="pe" if h == nh - 1 else None)
    return tb, tv



def emit_ffn0(P, com, io, n_groups=4, n_blocks=3):
    nc = P.nc
    P.new_phase()
    st = ExitStack()
    x_in = io["x_fm_in"]
    x_out = io["x_fm_out"]
    wt_h = io["wt_ffn0"]
    laT_h = io["laT"]
    lbT_h = io["lbT"]
    T = 1024

    laT = st.enter_context(_sb(nc, "f_laT", [128, 2, 8, 32], BF16))
    lbT = st.enter_context(_sb(nc, "f_lbT", [32, 2, 8, 128], BF16))
    t0 = 0
    psS = [st.enter_context(_psm(nc, f"f_psS{i}", [128, 512], F32))
           for i in range(4)]
    bias_pp, tb1 = load_pp(P, com, st, io["biases"].rearrange("i (c p) -> (i c) p", p=128), 48, "f_bias")
    gam_pp, tb2 = load_pp(P, com, st, io["ln_gamma"].rearrange("i (c p) -> (i c) p", p=128), 24, "f_gam")
    bet_pp, tb3 = load_pp(P, com, st, io["ln_beta"].rearrange("i (c p) -> (i c) p", p=128), 24, "f_bet")
    pp_ready = max(tb1, tb2, tb3)

    wt = [st.enter_context(_sb(nc, f"f_wt{i}", [128, 8, 1024], BF16))
          for i in range(2)]
    xg = st.enter_context(_sb(nc, "f_xg", [128, 8, T], F32))
    hh = st.enter_context(_sb(nc, "f_h", [128, 8, T], F32))
    hb = st.enter_context(_sb(nc, "f_hb", [128, 8, T], BF16))
    yy = st.enter_context(_sb(nc, "f_y", [128, 8, T], F32))
    xb = st.enter_context(_sb(nc, "f_xb", [128, 8, T], BF16))
    xsq = st.enter_context(_sb(nc, "f_xsq", [128, 8, T], BF16))
    gb = xb
    rows = st.enter_context(_sb(nc, "f_rows", [1, 4, T], F32))
    u1 = st.enter_context(_sb(nc, "f_u1", [32, T], BF16))

    psY = [st.enter_context(_psm(nc, f"f_psY{i}", [128, 512], F32))
           for i in range(2)]
    psU = st.enter_context(_psm(nc, "f_psU", [32, 1024], F32))

    P.wait("scalar", "dma", pp_ready)

    y_guard = [("dve", 0), ("dve", 0)]
    t_wt = [0, 0]
    wt_done = [0, 0]
    t_store = 0

    def qmm(mat_i, wtbuf, rhs3d, rhs_ready, u_guard, evac):
        P.wait("tensor", rhs_ready[0], rhs_ready[1])
        P.wait("tensor", "dma", t_wt[wtbuf])
        P.wait("tensor", u_guard[0], u_guard[1])
        tpe = None
        for h2 in range(2):
            for c in range(NCH):
                last = (h2 == 1 and c == NCH - 1)
                tpe = P.op("tensor", "matmul",
                           psU[0:32, h2 * 512:(h2 + 1) * 512],
                           laT[:, mat_i % 2, c, :],
                           rhs3d[:, c, h2 * 512:(h2 + 1) * 512],
                           start=(c == 0), stop=(c == NCH - 1),
                           sem="pe" if last else None)
        P.wait("vector", "pe", tpe)
        u_evac = P.op("vector", "tensor_copy", u1[:], psU[0:32, :], sem="dve")
        P.wait("tensor", "dve", u_evac)
        tpe = 0
        for oc in range(NCH):
            for h2 in range(2):
                k = (oc * 2 + h2) % 2
                pb = psY[k]
                P.wait("tensor", y_guard[k][0], y_guard[k][1])
                for dc in range(NCH):
                    P.op("tensor", "matmul", pb[:],
                         wt[wtbuf][:, dc, oc * 128:(oc + 1) * 128],
                         rhs3d[:, dc, h2 * 512:(h2 + 1) * 512],
                         start=(dc == 0), stop=False)
                tpe = P.op("tensor", "matmul", pb[:],
                           lbT[:, mat_i % 2, oc, :],
                           u1[0:32, h2 * 512:(h2 + 1) * 512],
                           start=False, stop=True, sem="pe")
                evac(oc, h2, pb, tpe, k)
        wt_done[wtbuf] = tpe
        return ("dve", u_evac)

    u_guard = ("dve", 0)
    for g in range(n_groups):
        s, t0g = g // 2, (g % 2) * T
        P.wait("sync", "dve", P.cnt["dve"])
        P.wait("sync", "st", t_store)
        tx = P.dma("sync", xg[:], x_in[s, :, :, t0g:t0g + T]
                   .rearrange("c p t -> p c t"))
        P.wait("sync", "pe", wt_done[0])
        t_wt[0] = P.dma("sync", wt[0][:],
                        wt_h[0].rearrange("(c p) o -> p c o", p=128))

        tpb, _ = emit_ln_stats(P, com, xg[:, :, 0:T], T, xb, xsq, rows,
                               [psS[0][0:1, :], psS[1][0:1, :],
                                psS[2][0:1, :], psS[3][0:1, :]],
                               [psS[0][:], psS[1][:]], [psS[2][:], psS[3][:]],
                               wait_act=("dma", tx))
        P.wait("vector", "pe", tpb)
        for c in range(NCH):
            for h2 in range(2):
                sl = slice(h2 * 512, (h2 + 1) * 512)
                P.op("vector", "tensor_tensor", hh[:, c, sl], xg[:, c, sl],
                     psS[h2][:], ALU.add)
                P.op("vector", "tensor_tensor", hh[:, c, sl], hh[:, c, sl],
                     psS[2 + h2][:], ALU.mult)
            t_hbv = P.op("vector", "tensor_copy", hb[:, c, :], hh[:, c, :],
                         sem="dve" if c == NCH - 1 else None)
        t_hb = ("dve", t_hbv)

        for blk in range(n_blocks):
            i0, i1 = 2 * blk, 2 * blk + 1
            P.wait("sync", "pe", wt_done[1])
            t_wt[1] = P.dma("sync", wt[1][:],
                            wt_h[i1].rearrange("(c p) o -> p c o", p=128))
            P.dma("sync", laT[:], laT_h[:, i0:i0 + 2, :, :])
            t_lo = P.dma("sync", lbT[:], lbT_h[:, i0:i0 + 2, :, :])
            P.wait("tensor", "dma", t_lo)

            def evac1(oc, h2, pb, tpe, k, i0=i0):
                P.wait("scalar", "pe", tpe)
                ta = P.op("scalar", "add",
                          yy[:, oc, h2 * 512:(h2 + 1) * 512], pb[:],
                          bias_pp[:, i0 * 8 + oc:i0 * 8 + oc + 1], sem="act")
                y_guard[k] = ("act", ta)
            u_guard = qmm(i0, 0, hb, t_hb, u_guard, evac1)

            P.wait("tensor", y_guard[0][0], y_guard[0][1])
            P.wait("tensor", y_guard[1][0], y_guard[1][1])
            P.wait("tensor", u_guard[0], u_guard[1])
            tpb, trows = emit_ln_stats(
                P, com, yy[:, :, 0:T], T, xb, xsq, rows,
                [psY[0][0:1, :], psY[1][0:1, :],
                 psU[0:1, 0:512], psU[0:1, 512:1024]],
                [psS[0][:], psS[1][:]], [psS[2][:], psS[3][:]],
                wait_act=None)
            P.wait("vector", "pe", tpb)
            glast = 0
            for c in range(NCH):
                for h2 in range(2):
                    sl = slice(h2 * 512, (h2 + 1) * 512)
                    P.op("vector", "tensor_tensor", yy[:, c, sl], yy[:, c, sl],
                         psS[h2][:], ALU.add)
                    tv = P.op("vector", "tensor_tensor", yy[:, c, sl],
                              yy[:, c, sl], psS[2 + h2][:], ALU.mult,
                              sem="dve")
                    P.wait("scalar", "dve", tv)
                    col = blk * 8 + c
                    glast = P.op("scalar", "activation", gb[:, c, sl],
                                 yy[:, c, sl], AF.Gelu,
                                 bias=bet_pp[:, col:col + 1],
                                 scale=gam_pp[:, col:col + 1], sem="act")

            if blk < n_blocks - 1:
                P.wait("sync", "pe", wt_done[0])
                t_wt[0] = P.dma("sync", wt[0][:],
                                wt_h[2 * blk + 2]
                                .rearrange("(c p) o -> p c o", p=128))

            P.wait("tensor", "dve", trows)
            y_guard[0] = y_guard[1] = ("dve", trows)
            u_guard = ("dve", trows)
            def evac2(oc, h2, pb, tpe, k, i1=i1):
                P.wait("vector", "pe", tpe)
                col = i1 * 8 + oc
                tv = P.op("vector", "scalar_tensor_tensor",
                          hh[:, oc, h2 * 512:(h2 + 1) * 512], pb[:],
                          bias_pp[:, col:col + 1],
                          hh[:, oc, h2 * 512:(h2 + 1) * 512],
                          ALU.add, ALU.add, sem="dve")
                y_guard[k] = ("dve", tv)
            u_guard = qmm(i1, 1, gb, ("act", glast), u_guard, evac2)
            if blk < n_blocks - 1:
                for c in range(NCH):
                    t_hbv = P.op("vector", "tensor_copy", hb[:, c, :],
                                 hh[:, c, :],
                                 sem="dve" if c == NCH - 1 else None)
                t_hb = ("dve", t_hbv)

        tv = P.op("vector", "scalar_tensor_tensor", xg[:, :, 0:T],
                  hh[:, :, 0:T], 0.5, xg[:, :, 0:T], ALU.mult, ALU.add,
                  sem="dve")
        P.wait("sync", "dve", tv)
        t_store = P.dma("sync", x_out[s, :, :, t0g:t0g + T]
                        .rearrange("c p t -> p c t"), xg[:], sem="st")
    P.wait("sync", "st", P.cnt["st"])
    st.close()



def emit_mixer(P, com, io):
    nc = P.nc
    P.new_phase()
    st = ExitStack()
    x_in = io["x_fm_in"]
    x_out = io["x_fm_out"]
    wprojT = io["wprojT"]
    wcombT = io["wcombT"]

    psA = st.enter_context(_psm(nc, "m_psA", [128, 2048], F32))
    psB = st.enter_context(_psm(nc, "m_psB", [128, 2048], F32))

    a_pp, t1 = load_pp(P, com, st, io["A"].rearrange("one1 one2 (c p) -> (one1 one2 c) p", p=128), 8, "m_A")
    b1_pp, t2 = load_pp(P, com, st, io["Bp"].rearrange("one1 (c p) one2 -> (one1 c) (p one2)", p=128), 8, "m_B1")
    bd_pp, t3 = load_pp(P, com, st, io["b_delta"].rearrange("(c p) -> c p", p=128), 8, "m_bd")
    d_pp = st.enter_context(_sb(nc, "m_dpp", [128, 1], F32))
    td = P.dma("sync", d_pp[:],
               bass.AP(io["D"].tensor, 0, [[0, 128], [1, 1]]),
               allow_slow_non_contiguous=True)
    pp_ready = max(t1, t2, t3, td)

    wcomb = st.enter_context(_sb(nc, "m_wcomb", [128, 8, 1024], BF16))
    wpart = st.enter_context(_sb(nc, "m_wpart", [128, 8, 2048], BF16))
    x_mix = st.enter_context(_sb(nc, "m_xmix", [128, 8, 2048], BF16))
    xg = st.enter_context(_sb(nc, "m_xg", [128, 8, 1024], F32))
    xb = st.enter_context(_sb(nc, "m_xb", [128, 8, 1024], BF16))
    xsq = st.enter_context(_sb(nc, "m_xsq", [128, 8, 1024], BF16))
    rows = st.enter_context(_sb(nc, "m_rows", [1, 4, 1024], F32))
    dump = st.enter_context(_sb(nc, "m_dump", [128, 2048], F32))
    scr1 = st.enter_context(_sb(nc, "m_scr1", [128, 2048], F32))
    scr2 = st.enter_context(_sb(nc, "m_scr2", [128, 2048], F32))
    dlt = st.enter_context(_sb(nc, "m_dlt", [128, 2048], F32))
    xc = st.enter_context(_sb(nc, "m_xc", [128, 2048], F32))
    uc_col = st.enter_context(_sb(nc, "m_uc", [128, 8], F32))
    cs_col = st.enter_context(_sb(nc, "m_cs", [128, 8], F32))
    pp1 = st.enter_context(_sb(nc, "m_pp1", [128, 8], F32))

    twc = P.dma("sync", wcomb[:], wcombT.rearrange("(c p) o -> p c o", p=128))
    twp = P.dma("sync", wpart[:, :, 0:1024],
                wprojT[:, 2048:3072].rearrange("(c p) o -> p c o", p=128))
    twp = P.dma("sync", wpart[:, :, 1024:2048],
                wprojT[:, 3072:4096].rearrange("(c p) o -> p c o", p=128))

    P.wait("scalar", "dma", pp_ready)
    P.wait("vector", "dma", pp_ready)
    t_store = 0
    ev_cs = 0
    last_y = 0

    for s in range(NSEQ):
        P.wait("tensor", "act", P.cnt.get("act", P.inc("act", 0)))
        P.wait("tensor", "dve", P.cnt["dve"])
        for grp in range(2):
            sl = slice(grp * 1024, (grp + 1) * 1024)
            P.wait("sync", "dve", P.cnt["dve"])
            P.wait("sync", "st", t_store)
            tx = P.dma("sync", xg[:], x_in[s, :, :, sl]
                       .rearrange("c p t -> p c t"))
            tpb, _ = emit_ln_stats(
                P, com, xg[:, :, 0:1024], 1024, xb, xsq, rows,
                [psA[0:1, 0:512], psA[0:1, 512:1024],
                 psA[0:1, 1024:1536], psA[0:1, 1536:2048]],
                [psA[:, 0:512], psA[:, 512:1024]],
                [psA[:, 1024:1536], psA[:, 1536:2048]],
                wait_act=("dma", tx))
            P.wait("vector", "pe", tpb)
            for c in range(NCH):
                for h2 in range(2):
                    s2 = slice(h2 * 512, (h2 + 1) * 512)
                    P.op("vector", "tensor_tensor", xg[:, c, s2], xg[:, c, s2],
                         psA[:, h2 * 512:(h2 + 1) * 512], ALU.add)
                    P.op("vector", "tensor_tensor",
                         x_mix[:, c, grp * 1024 + h2 * 512:
                               grp * 1024 + (h2 + 1) * 512],
                         xg[:, c, s2],
                         psA[:, 1024 + h2 * 512:1024 + (h2 + 1) * 512],
                         ALU.mult,
                         sem="dve" if (c == NCH - 1 and h2 == 1) else None)
        t_xmix = P.cnt["dve"]

        P.wait("tensor", "dve", t_xmix)
        P.wait("tensor", "dma", twp)
        for oc in range(NCH):
            P.wait("tensor", "dve", ev_cs)
            tm = None
            for nt in range(4):
                for dc in range(NCH):
                    tm = P.op("tensor", "matmul",
                              psB[:, nt * 512:(nt + 1) * 512],
                              wpart[:, dc, oc * 128:(oc + 1) * 128],
                              x_mix[:, dc, nt * 512:(nt + 1) * 512],
                              start=(dc == 0), stop=(dc == NCH - 1),
                              sem="pe" if (nt == 3 and dc == NCH - 1) else None)
            P.wait("vector", "pe", tm)
            P.op("vector", "scalar_tensor_tensor", dump[:], psB[:], 0.0,
                 x_mix[:, oc, :], ALU.add, ALU.mult,
                 accum_out=uc_col[:, oc:oc + 1])
            ev_cs = P.op("vector", "reduce_sum", cs_col[:, oc:oc + 1], psB[:],
                         AX.X, sem="dve")
        tv = P.op("vector", "tensor_tensor", pp1[:], b1_pp[:], uc_col[:],
                  ALU.mult, sem="dve")
        P.wait("vector", "dve", tv)
        P.wait("scalar", "dve", tv)
        if "dbg_uc" in io and s == 0:
            P.wait("sync", "dve", tv)
            P.dma("sync", io["dbg_uc"], uc_col[:], sem="st")
            P.dma("sync", io["dbg_cs"], cs_col[:], sem="st")
            P.dma("sync", io["dbg_bd1"], bd_pp[:], sem="st")

        P.wait("tensor", "dma", twc)
        ev_a = ("dve", t_xmix)
        ev_b = ("dve", ev_cs)
        for oc in range(NCH):
            P.wait("tensor", ev_a[0], ev_a[1])
            tm = None
            for nt in range(4):
                for dc in range(NCH):
                    tm = P.op("tensor", "matmul",
                              psA[:, nt * 512:(nt + 1) * 512],
                              wcomb[:, dc, oc * 128:(oc + 1) * 128],
                              x_mix[:, dc, nt * 512:(nt + 1) * 512],
                              start=(dc == 0), stop=(dc == NCH - 1),
                              sem="pe" if (nt == 3 and dc == NCH - 1) else None)
            P.wait("scalar", "pe", tm)
            P.wait("scalar", "dve", last_y)
            for nt in range(4):
                ta = P.op("scalar", "add",
                          scr2[:, nt * 512:(nt + 1) * 512],
                          psA[:, nt * 512:(nt + 1) * 512],
                          bd_pp[:, oc:oc + 1], sem="act")
            ev_a = ("act", ta)
            if "dbg_uc" in io and s == 0 and oc == 0:
                P.wait("sync", "act", ta)
                tst_ = P.dma("sync", io["dbg_pre"], scr2[:], sem="st")
                P.wait("vector", "st", tst_)
                P.wait("scalar", "st", tst_)
            P.wait("vector", "act", ta)
            tabs_ = P.op("vector", "scalar_tensor_tensor", scr1[:], scr2[:],
                         -1.0, scr2[:], ALU.mult, ALU.max, sem="dve")
            if "dbg_uc" in io and s == 0 and oc == 0:
                P.wait("sync", "dve", tabs_)
                tst_ = P.dma("sync", io["dbg_abs"], scr1[:], sem="st")
                P.wait("vector", "st", tst_)
                P.wait("scalar", "st", tst_)
            P.wait("scalar", "dve", P.cnt["dve"])
            P.op("scalar", "activation", scr1[:], scr1[:], AF.Exp, scale=-1.0)
            ta = P.op("scalar", "activation", scr1[:], scr1[:], AF.Ln,
                      bias=1.0, sem="act")
            if "dbg_uc" in io and s == 0 and oc == 0:
                P.wait("sync", "act", ta)
                tst_ = P.dma("sync", io["dbg_ln"], scr1[:], sem="st")
                P.wait("vector", "st", tst_)
                P.wait("scalar", "st", tst_)
            P.wait("vector", "act", ta)
            tdd = P.op("vector", "scalar_tensor_tensor", dlt[:], scr2[:], 0.0,
                       scr1[:], ALU.max, ALU.add,
                       sem="dve" if ("dbg_uc" in io and s == 0 and oc == 0) else None)
            if "dbg_uc" in io and s == 0 and oc == 0:
                P.wait("sync", "dve", tdd)
                tst_ = P.dma("sync", io["dbg_dlt"], dlt[:], sem="st")
                P.wait("vector", "st", tst_)
                P.wait("scalar", "st", tst_)
            tvs = P.op("vector", "tensor_scalar", scr1[:], dlt[:],
                       b1_pp[:, oc:oc + 1], None, ALU.mult, sem="dve")
            P.wait("vector", "dve", tvs)
            P.op("vector", "tensor_tensor_scan", scr1[:], scr1[:], scr1[:],
                 0.0, ALU.add, ALU.bypass)
            tv = P.op("vector", "tensor_scalar", scr1[:], scr1[:],
                      cs_col[:, oc:oc + 1], None, ALU.mult, sem="dve")
            if "dbg_uc" in io and s == 0 and oc == 0:
                P.wait("sync", "dve", tv)
                tst_ = P.dma("sync", io["dbg_scan"], scr1[:], sem="st")
                P.wait("vector", "st", tst_)
                P.wait("scalar", "st", tst_)
            P.wait("scalar", "dve", tv)
            ta = P.op("scalar", "activation", dlt[:], dlt[:], AF.Exp,
                      scale=a_pp[:, oc:oc + 1], sem="act")
            P.wait("vector", "act", ta)
            tv = P.op("vector", "scalar_tensor_tensor", scr1[:], dlt[:],
                      pp1[:, oc:oc + 1], scr1[:], ALU.mult, ALU.add, sem="dve")
            last_y = tv
            P.wait("tensor", ev_b[0], ev_b[1])
            tm = None
            for nt in range(4):
                for dc in range(NCH):
                    tm = P.op("tensor", "matmul",
                              psB[:, nt * 512:(nt + 1) * 512],
                              wpart[:, dc, 1024 + oc * 128:
                                    1024 + (oc + 1) * 128],
                              x_mix[:, dc, nt * 512:(nt + 1) * 512],
                              start=(dc == 0), stop=(dc == NCH - 1),
                              sem="pe" if (nt == 3 and dc == NCH - 1) else None)
            P.wait("vector", "pe", tm)
            P.wait("vector", "dve", tv)
            tv = P.op("vector", "scalar_tensor_tensor", scr1[:], psB[:],
                      d_pp[:, 0:1], scr1[:], ALU.mult, ALU.add, sem="dve")
            ev_b = ("dve", tv)
            P.wait("sync", "st", t_store)
            txc = P.dma("sync", xc[:], x_in[s, oc, :, :])
            P.wait("vector", "dma", txc)
            tv = P.op("vector", "tensor_tensor", xc[:], xc[:], scr1[:],
                      ALU.add, sem="dve")
            P.wait("sync", "dve", tv)
            t_store = P.dma("sync", x_out[s, oc, :, :], xc[:], sem="st")

    if "dbg_uc" in io:
        P.dma("sync", io["dbg_bd2"], bd_pp[:], sem="st")
    P.wait("sync", "st", P.cnt["st"])
    st.close()



def emit_attn(P, com, io):
    nc = P.nc
    P.new_phase()
    st = ExitStack()
    x_in = io["x_fm_in"]
    x_out = io["x_fm_out"]
    NH = 16
    DH = 64

    psA = st.enter_context(_psm(nc, "a_psA", [128, 2048], F32))
    psB = st.enter_context(_psm(nc, "a_psB", [128, 2048], F32))

    wq = st.enter_context(_sb(nc, "a_wq", [128, 8, 1024], BF16))
    wx = st.enter_context(_sb(nc, "a_wx", [128, 8, 1024], BF16))
    kf = st.enter_context(_sb(nc, "a_kf", [128, 8, 2048], BF16))
    vt = st.enter_context(_sb(nc, "a_vt", [128, 16, 1024], BF16))
    xat = st.enter_context(_sb(nc, "a_xat", [128, 8, 2048], BF16))
    qg = st.enter_context(_sb(nc, "a_qg", [128, 8, 512], BF16))
    pt = st.enter_context(_sb(nc, "a_pt", [128, 2, 512], BF16))
    att = st.enter_context(_sb(nc, "a_att", [128, 8, 512], BF16))
    msk = st.enter_context(_sb(nc, "a_msk", [128, 4, 512], BF16))
    lb = st.enter_context(_sb(nc, "a_lb", [128, 2, 512], F32))
    lrow = st.enter_context(_sb(nc, "a_lrow", [1, 512], F32))
    xg = st.enter_context(_sb(nc, "a_xg", [128, 8, 512], F32))
    xb = st.enter_context(_sb(nc, "a_xb", [128, 8, 512], BF16))
    xsq = st.enter_context(_sb(nc, "a_xsq", [128, 8, 512], BF16))
    rows = st.enter_context(_sb(nc, "a_rows", [1, 4, 512], F32))
    xc = st.enter_context(_sb(nc, "a_xc", [128, 512], F32))

    mi = st.enter_context(_sb(nc, "a_mi", [128, 512], F32))
    for j in range(4):
        P.op("gpsimd", "iota", mi[:], [[1, 512]], base=-128 * j,
             channel_multiplier=-1, allow_small_or_imprecise_dtypes=True,
             sem="pool")
        P.wait("vector", "pool", P.cnt["pool"])
        P.op("vector", "tensor_scalar", msk[:, j, :], mi[:], 0.0, -1e30,
             ALU.is_lt, op1=ALU.mult, sem="dve")
        P.wait("gpsimd", "dve", P.cnt["dve"])
    wsrc = (io["wqkvo_c"] if "wqkvo_c" in io else
            None)

    def _wload(buf, i, nm):
        if wsrc is not None:
            return P.dma("sync", buf,
                         wsrc[i].rearrange("(c p) o -> p c o", p=128),
                         sem="dma")
        return P.dma("gpsimd", buf,
                     io[nm].rearrange("(c p) o -> p c o", p=128), sem="dmag")

    wq_sem = "dma" if wsrc is not None else "dmag"
    t_wq = _wload(wq[:], 0, "W_Q")
    t_wx = _wload(wx[:], 1, "W_K")

    t_store = 0
    SC = 1.0 / (DH ** 0.5)
    sguard = [0, 0]
    ptg = [0, 0]
    lg_guard = ("dve", 0)
    lbg = ("dve", 0)
    att_guard = [("dve", 0), ("dve", 0)]

    for s in range(NSEQ):
        P.wait("tensor", "act", P.cnt.get("act", P.inc("act", 0)))
        P.wait("tensor", "dve", P.cnt["dve"])

        for grp in range(4):
            sl = slice(grp * 512, (grp + 1) * 512)
            P.wait("sync", "dve", P.cnt["dve"])
            P.wait("sync", "st", t_store)
            tx = P.dma("sync", xg[:], x_in[s, :, :, sl]
                       .rearrange("c p t -> p c t"))
            tpb, _ = emit_ln_stats(
                P, com, xg[:, :, 0:512], 512, xb, xsq, rows,
                [psA[0:1, 0:512], psA[0:1, 512:1024]],
                [psA[:, 0:512]], [psA[:, 512:1024]],
                wait_act=("dma", tx))
            P.wait("vector", "pe", tpb)
            for c in range(NCH):
                P.op("vector", "tensor_tensor", xg[:, c, :], xg[:, c, :],
                     psA[:, 0:512], ALU.add)
                P.op("vector", "tensor_tensor", xat[:, c, sl], xg[:, c, :],
                     psA[:, 512:1024], ALU.mult,
                     sem="dve" if c == NCH - 1 else None)
        t_xat = P.cnt["dve"]

        P.wait("tensor", "dve", t_xat)
        P.wait("tensor", wq_sem, t_wx)
        ev = [("dve", 0)] * 4
        for oc in range(NCH):
            for nt in range(4):
                k = nt
                pb = psB[:, k * 512:(k + 1) * 512]
                P.wait("tensor", ev[k][0], ev[k][1])
                tm = None
                for dc in range(NCH):
                    tm = P.op("tensor", "matmul", pb,
                              wx[:, dc, oc * 128:(oc + 1) * 128],
                              xat[:, dc, nt * 512:(nt + 1) * 512],
                              start=(dc == 0), stop=(dc == NCH - 1),
                              sem="pe" if dc == NCH - 1 else None)
                P.wait("scalar", "pe", tm)
                ta = P.op("scalar", "activation",
                          kf[:, oc, nt * 512:(nt + 1) * 512], pb, AF.Copy,
                          scale=SC, sem="act")
                ev[k] = ("act", ta)
        P.wait("sync" if wsrc is not None else "gpsimd", "act",
               P.cnt["act"])
        t_wx = _wload(wx[:], 2, "W_V")
        P.wait("tensor", wq_sem, t_wx)
        for kt in range(16):
            for nt2 in range(2):
                k = (kt * 2 + nt2) % 4
                pb = psB[:, k * 512:(k + 1) * 512]
                P.wait("tensor", ev[k][0], ev[k][1])
                tm = None
                for dc in range(NCH):
                    tm = P.op("tensor", "matmul", pb,
                              xat[:, dc, kt * 128:(kt + 1) * 128],
                              wx[:, dc, nt2 * 512:(nt2 + 1) * 512],
                              start=(dc == 0), stop=(dc == NCH - 1),
                              sem="pe" if dc == NCH - 1 else None)
                P.wait("scalar", "pe", tm)
                ta = P.op("scalar", "activation",
                          vt[:, kt, nt2 * 512:(nt2 + 1) * 512], pb, AF.Copy,
                          sem="act")
                ev[k] = ("act", ta)
        P.wait("sync" if wsrc is not None else "gpsimd", "act",
               P.cnt["act"])
        t_wx = _wload(wx[:], 3, "W_o")

        for g in range(4):
            gsl = slice(g * 512, (g + 1) * 512)
            P.wait("tensor", wq_sem, t_wq)
            P.wait("tensor", "act", P.cnt["act"])
            P.wait("tensor", "dve", P.cnt["dve"])
            for oc in range(NCH):
                k = oc % 2
                pb = psA[:, k * 512:(k + 1) * 512]
                P.wait("tensor", ev[k][0], ev[k][1])
                tm = None
                for dc in range(NCH):
                    tm = P.op("tensor", "matmul", pb,
                              wq[:, dc, oc * 128:(oc + 1) * 128],
                              xat[:, dc, gsl],
                              start=(dc == 0), stop=(dc == NCH - 1),
                              sem="pe" if dc == NCH - 1 else None)
                P.wait("scalar", "pe", tm)
                ta = P.op("scalar", "activation", qg[:, oc, :], pb, AF.Copy,
                          sem="act")
                ev[k] = ("act", ta)
            t_qg = ("act", P.cnt["act"])

            nkt = 4 * g + 4
            for h in range(NH):
                base = (h % 2) * 64
                hc = h // 2
                ob = h % 2
                P.wait("tensor", t_qg[0], t_qg[1])
                pv_t = {}
                for kt in range(nkt):
                    kb = kt % 2
                    pb = psA[:, kb * 512:(kb + 1) * 512]
                    P.wait("tensor", "act", sguard[kb])
                    tm = P.op("tensor", "matmul", pb,
                              kf[base:base + 64, hc,
                                 kt * 128:(kt + 1) * 128],
                              qg[base:base + 64, hc, :],
                              start=True, stop=True, sem="pe")
                    if kt >= 4 * g:
                        j = kt - 4 * g
                        P.wait("vector", "pe", tm)
                        tm2 = P.op("vector", "tensor_tensor", pb, pb,
                                   msk[:, j, :], ALU.add, sem="dve")
                        P.wait("scalar", "dve", tm2)
                    else:
                        P.wait("scalar", "pe", tm)
                    P.wait("scalar", "pe", ptg[kb])
                    sguard[kb] = P.op("scalar", "activation", pt[:, kb, :],
                                      pb, AF.Exp, sem="act")
                    P.wait("tensor", "act", sguard[kb])
                    if kt == 0:
                        P.wait("tensor", lg_guard[0], lg_guard[1])
                        P.wait("tensor", att_guard[ob][0], att_guard[ob][1])
                    P.op("tensor", "matmul", psB[0:1, 1024:1536],
                         com.ones_b[:], pt[:, kb, :],
                         start=(kt == 0), stop=(kt == nkt - 1))
                    pv_t[kt] = P.op("tensor", "matmul",
                                    psB[base:base + 64,
                                        ob * 512:ob * 512 + 512],
                                    vt[:, kt, h * 64:h * 64 + 64],
                                    pt[:, kb, :],
                                    start=(kt == 0), stop=(kt == nkt - 1),
                                    tile_position=(0, base), sem="pe")
                    ptg[kb] = pv_t[kt]
                P.wait("vector", "pe", pv_t[nkt - 1])
                tv = P.op("vector", "tensor_copy", lrow[:],
                          psB[0:1, 1024:1536], sem="dve")
                lg_guard = ("dve", tv)
                P.wait("scalar", "dve", tv)
                P.op("scalar", "activation", lrow[:], lrow[:], AF.Ln)
                ta = P.op("scalar", "activation", lrow[:], lrow[:], AF.Exp,
                          scale=-1.0, sem="act")
                P.wait("tensor", "act", ta)
                P.wait("tensor", lbg[0], lbg[1])
                tm = P.op("tensor", "matmul", psB[:, 1536:2048], com.ones_f[:],
                          lrow[:], start=True, stop=True, sem="pe")
                P.wait("vector", "pe", tm)
                tv = P.op("vector", "tensor_copy", lb[:, h % 2, :],
                          psB[:, 1536:2048], sem="dve")
                lbg = ("dve", tv)
                P.wait("vector", "dve", tv)
                td = P.op("vector", "tensor_tensor",
                          att[base:base + 64, hc, :],
                          psB[base:base + 64, ob * 512:ob * 512 + 512],
                          lb[base:base + 64, h % 2, :],
                          ALU.mult, sem="dve")
                att_guard[ob] = ("dve", td)

            P.wait("tensor", att_guard[0][0], att_guard[0][1])
            P.wait("tensor", att_guard[1][0], att_guard[1][1])
            P.wait("tensor", "act", P.cnt["act"])
            P.wait("tensor", "dve", P.cnt["dve"])
            P.wait("tensor", wq_sem, t_wx)
            for oc in range(NCH):
                k = oc % 2
                pb = psA[:, k * 512:(k + 1) * 512]
                P.wait("tensor", ev[k][0], ev[k][1])
                tm = None
                for dc in range(NCH):
                    tm = P.op("tensor", "matmul", pb,
                              wx[:, dc, oc * 128:(oc + 1) * 128],
                              att[:, dc, :],
                              start=(dc == 0), stop=(dc == NCH - 1),
                              sem="pe" if dc == NCH - 1 else None)
                P.wait("sync", "st", t_store)
                txc = P.dma("sync", xc[:], x_in[s, oc, :, gsl])
                P.wait("vector", "dma", txc)
                P.wait("vector", "pe", tm)
                tv = P.op("vector", "tensor_tensor", xc[:], xc[:], pb,
                          ALU.add, sem="dve")
                ev[k] = ("dve", tv)
                P.wait("sync", "dve", tv)
                t_store = P.dma("sync", x_out[s, oc, :, gsl], xc[:], sem="st")
    P.wait("sync", "st", P.cnt["st"])
    st.close()



def emit_ppT(P, com, st, src2d, R, Cc, name, ps, tmp, guard=0):
    nc = P.nc
    dest = st.enter_context(_sb(nc, name, [128, R * Cc], F32))
    P.wait("sync", "pe", guard)
    td = P.dma("sync", tmp[0:R, 0:Cc * 128], src2d)
    P.wait("tensor", "dma", td)
    P.wait("tensor", "dve", com.ready)
    t = 0
    for cc in range(Cc):
        pb = ps[cc % 2]
        if cc >= 2:
            P.wait("tensor", "dve", t)
        tp = P.op("tensor", "transpose", pb, tmp[:, cc * 128:(cc + 1) * 128],
                  com.ident_f[0:R, 0:R], sem="pe")
        P.wait("vector", "pe", tp)
        t = P.op("vector", "tensor_copy",
                 bass.AP(dest.ap().tensor, cc, [[R * Cc, 128], [Cc, R]]),
                 pb, sem="dve")
    return dest, t


def emit_moe(P, com, io):
    nc = P.nc
    P.new_phase()
    st = ExitStack()
    x_in = io["x_fm_in"]
    x_out = io["x_fm_out"]
    if "W1c" in io:
        W1h, W2h, W3h = io["W1c"], io["W2c"], io["W3c"]
    else:
        W1h, W2h, W3h = io["W1"], io["W2"], io["W3"]
    cw_scr = io["cw_scr"]

    psA = st.enter_context(_psm(nc, "e_psA", [128, 2048], F32))
    psB = st.enter_context(_psm(nc, "e_psB", [128, 2048], F32))

    xln = st.enter_context(_sb(nc, "e_xln", [128, 8, 1024], BF16))
    yy = st.enter_context(_sb(nc, "e_y", [128, 8, 1024], F32))
    hh = st.enter_context(_sb(nc, "e_h", [128, 32, 1024], BF16))
    w1q = st.enter_context(_sb(nc, "e_w1q", [128, 2, 8, 512], BF16))
    w2q = st.enter_context(_sb(nc, "e_w2q", [128, 2, 8, 512], BF16))
    xg = st.enter_context(_sb(nc, "e_xg", [128, 8, 512], F32))
    rows = st.enter_context(_sb(nc, "e_rows", [1, 4, 512], F32))
    cwb = st.enter_context(_sb(nc, "e_cwb", [128, 1024], F32))
    s1 = st.enter_context(_sb(nc, "e_s1", [128, 2, 512], BF16))
    tt1 = st.enter_context(_sb(nc, "e_tt1", [128, 512], F32))
    wr_sb = st.enter_context(_sb(nc, "e_wr", [128, 8, 8], BF16))
    brb = st.enter_context(_sb(nc, "e_brb", [128, 8], F32))
    iota8 = st.enter_context(_sb(nc, "e_io8", [128, 8], F32))
    lg = st.enter_context(_sb(nc, "e_lg", [128, 32, 8], F32))
    mxall = st.enter_context(_sb(nc, "e_mx", [128, 32, 8], F32))
    miall = st.enter_context(_sb(nc, "e_mi", [128, 32, 8], U32))
    i1f = st.enter_context(_sb(nc, "e_i1f", [128, 2, 32], F32))
    wgt = st.enter_context(_sb(nc, "e_wgt", [128, 5, 32], F32))
    oh = st.enter_context(_sb(nc, "e_oh", [128, 8], F32))
    cw = st.enter_context(_sb(nc, "e_cw", [128, 32, 8], F32))
    cwTt = st.enter_context(_sb(nc, "e_cwTt", [8, 2, 128], F32))

    xb = hh[:, 0:8, 0:512]
    xsq = hh[:, 8:16, 0:512]

    _pp = [psA[:, 0:8], psA[:, 512:520]]
    _tmp = st.enter_context(_sb(nc, "e_ppt", [8, 4096], F32))
    b1_pp, tp1 = emit_ppT(P, com, st, io["b1"], 8, 32, "e_b1", _pp, _tmp)
    b2_pp, tp2 = emit_ppT(P, com, st, io["b2"], 8, 32, "e_b2", _pp, _tmp,
                          guard=P.cnt["pe"])
    b3_pp, tp3 = emit_ppT(P, com, st, io["b3"], 8, 8, "e_b3", _pp, _tmp,
                          guard=P.cnt["pe"])
    P.op("gpsimd", "iota", iota8[:], [[1, 8]], channel_multiplier=0,
         allow_small_or_imprecise_dtypes=True, sem="pool")
    t_io8 = P.cnt["pool"]
    if "wr_c" in io:
        tw = P.dma("sync", wr_sb[:],
                   io["wr_c"].rearrange("(c p) e -> p c e", p=128), sem="dma")
        wr_sem = "dma"
    else:
        tw = P.dma("gpsimd", wr_sb[:],
                   io["Wr"].rearrange("(c p) e -> p c e", p=128), sem="dmag")
        wr_sem = "dmag"
    tb = P.dma("sync", brb[:],
               bass.AP(io["br"].tensor, 0, [[0, 128], [1, 8]]),
               allow_slow_non_contiguous=True)
    pp_ready = max(tp1, tp2, tp3)

    t_store = 0
    ev_ps = ("dve", 0)
    for g4 in range(4):
        for sub in range(2):
            t0g = g4 * 1024 + sub * 512
            s_, lo = t0g // SEQ, t0g % SEQ
            P.wait("sync", "dve", P.cnt["dve"])
            tx = P.dma("sync", xg[:], x_in[s_, :, :, lo:lo + 512]
                       .rearrange("c p t -> p c t"))
            tpb, _ = emit_ln_stats(
                P, com, xg[:, :, 0:512], 512, xb, xsq, rows,
                [psA[0:1, 0:512], psA[0:1, 512:1024]],
                [psA[:, 0:512]], [psA[:, 512:1024]],
                wait_act=("dma", tx))
            P.wait("vector", "pe", tpb)
            for c in range(NCH):
                P.op("vector", "tensor_tensor", xg[:, c, :], xg[:, c, :],
                     psA[:, 0:512], ALU.add)
                P.op("vector", "tensor_tensor",
                     xln[:, c, sub * 512:(sub + 1) * 512], xg[:, c, :],
                     psA[:, 512:1024], ALU.mult,
                     sem="dve" if c == NCH - 1 else None)
        t_xln = P.cnt["dve"]
        P.wait("tensor", "dve", t_xln)
        P.wait("tensor", wr_sem, tw)
        for tt2 in range(8):
            tt = g4 * 8 + tt2
            k = tt2 % 2
            pb = psA[:, 1024 + k * 8:1024 + (k + 1) * 8]
            P.wait("tensor", ev_ps[0], ev_ps[1])
            tm = None
            for dc in range(NCH):
                tm = P.op("tensor", "matmul", pb,
                          xln[:, dc, tt2 * 128:(tt2 + 1) * 128],
                          wr_sb[:, dc, :], start=(dc == 0),
                          stop=(dc == NCH - 1),
                          sem="pe" if dc == NCH - 1 else None)
            P.wait("vector", "pe", tm)
            tv = P.op("vector", "tensor_copy", lg[:, tt, :], pb, sem="dve")
            ev_ps = ("dve", tv)

    P.wait("vector", "dma", tb)
    brb_b = bass.AP(brb.ap().tensor, 0, [[8, 128], [0, 32], [1, 8]])
    tv = P.op("vector", "tensor_tensor", lg[:], lg[:], brb_b, ALU.add,
              sem="dve")
    for tt in range(32):
        P.wait("vector", "dve", P.cnt["dve"])
        P.op("vector", "max", mxall[:, tt, :], lg[:, tt, :], sem="dve")
        P.wait("vector", "dve", P.cnt["dve"])
        P.op("vector", "max_index", miall[:, tt, :], mxall[:, tt, :],
             lg[:, tt, :], sem="dve")
    m1 = bass.AP(mxall.ap().tensor, 0, [[256, 128], [8, 32]])
    m2 = bass.AP(mxall.ap().tensor, 1, [[256, 128], [8, 32]])
    i1u = bass.AP(miall.ap().tensor, 0, [[256, 128], [8, 32]])
    i2u = bass.AP(miall.ap().tensor, 1, [[256, 128], [8, 32]])
    P.wait("vector", "dve", P.cnt["dve"])
    P.op("vector", "tensor_tensor", wgt[:, 0, :], m2, m1, ALU.subtract)
    P.op("vector", "tensor_copy", i1f[:, 0, :], i1u)
    tv = P.op("vector", "tensor_copy", i1f[:, 1, :], i2u, sem="dve")
    P.wait("scalar", "dve", tv)
    ta = P.op("scalar", "activation", wgt[:, 0, :], wgt[:, 0, :], AF.Exp,
              sem="act")
    P.wait("vector", "act", ta)
    tv = P.op("vector", "tensor_scalar", wgt[:, 1, :], wgt[:, 0, :], 1.0,
              None, ALU.add, sem="dve")
    P.wait("vector", "dve", tv)
    tv = P.op("vector", "reciprocal", wgt[:, 4, :], wgt[:, 1, :], sem="dve")
    P.wait("vector", "dve", tv)
    P.op("vector", "tensor_scalar", wgt[:, 2, :], wgt[:, 4, :], 0.5, None,
         ALU.mult)
    tv = P.op("vector", "scalar_tensor_tensor", wgt[:, 3, :], wgt[:, 0, :],
              0.5, wgt[:, 4, :], ALU.mult, ALU.mult, sem="dve")
    P.wait("vector", "dve", tv)
    P.wait("vector", "pool", t_io8)
    for tt in range(32):
        P.op("vector", "tensor_scalar", oh[:], iota8[:],
             i1f[:, 0, tt:tt + 1], None, ALU.is_equal)
        P.op("vector", "tensor_scalar", cw[:, tt, :], oh[:],
             wgt[:, 2, tt:tt + 1], None, ALU.mult)
        P.op("vector", "tensor_scalar", oh[:], iota8[:],
             i1f[:, 1, tt:tt + 1], None, ALU.is_equal)
        P.op("vector", "scalar_tensor_tensor", cw[:, tt, :], oh[:],
             wgt[:, 3, tt:tt + 1], cw[:, tt, :], ALU.mult, ALU.add,
             sem="dve" if tt == 31 else None)
    t_cw = P.cnt["dve"]
    P.wait("tensor", "dve", t_cw)
    t = 0
    t_cws = 0
    cwv = cw_scr.rearrange("e (tt t) -> e tt t", t=128)
    for tt in range(32):
        pb = psA[:, 1024 + (tt % 2) * 128:1024 + (tt % 2) * 128 + 128]
        if tt >= 2:
            P.wait("tensor", "dve", t)
        tp = P.op("tensor", "transpose", pb[0:8, 0:128], cw[:, tt, :],
                  com.ident_f[:], sem="pe")
        P.wait("vector", "pe", tp)
        if tt >= 2:
            P.wait("vector", "st", t_cws)
        t = P.op("vector", "tensor_copy", cwTt[:, tt % 2, :], pb[0:8, 0:128],
                 sem="dve")
        P.wait("sync", "dve", t)
        t_cws = P.dma("sync", cwv[:, tt, :], cwTt[:, tt % 2, :], sem="st")
    if "dbg_cw" in io:
        P.wait("sync", "dve", P.cnt["dve"])
        P.wait("sync", "st", t_cws)
        P.dma("sync", io["dbg_cw"], cw[:], sem="st")
        P.dma("sync", io["dbg_lg"], lg[:], sem="st")
        P.dma("sync", io["dbg_mx"], mxall[:], sem="st")
        P.dma("sync", io["dbg_mi"], miall[:], sem="st")
        P.dma("sync", io["dbg_wgt"], wgt[:, 0:4, :], sem="st")
        tdb = P.dma("sync", io["dbg_i1f"], i1f[:], sem="st")
        P.wait("vector", "st", tdb)
        P.wait("scalar", "st", tdb)
        P.wait("tensor", "st", tdb)

    w1_free = [0, 0]
    w23_free = [0, 0]
    hg = [("dve", 0)] * 4
    yg = [("dve", 0)] * 8
    t_cwb = 0
    for g in range(4):
        P.wait("vector", "dve", P.cnt["dve"])
        tv = P.op("vector", "memset", yy[:], 0.0, sem="dve")
        t_y0 = tv
        P.wait("scalar", "pe", P.cnt["pe"])
        for sub in range(2):
            t0g = g * 1024 + sub * 512
            s_, lo = t0g // SEQ, t0g % SEQ
            P.wait("sync", "dve", P.cnt["dve"])
            tx = P.dma("sync", xg[:], x_in[s_, :, :, lo:lo + 512]
                       .rearrange("c p t -> p c t"))
            tpb, _ = emit_ln_stats(
                P, com, xg[:, :, 0:512], 512, xb, xsq, rows,
                [psA[0:1, 0:512], psA[0:1, 512:1024]],
                [psA[:, 0:512]], [psA[:, 512:1024]],
                wait_act=("dma", tx))
            P.wait("vector", "pe", tpb)
            for c in range(NCH):
                P.op("vector", "tensor_tensor", xg[:, c, :], xg[:, c, :],
                     psA[:, 0:512], ALU.add)
                P.op("vector", "tensor_tensor",
                     xln[:, c, sub * 512:(sub + 1) * 512], xg[:, c, :],
                     psA[:, 512:1024], ALU.mult,
                     sem="dve" if c == NCH - 1 else None)
        t_xln = P.cnt["dve"]
        hg = [("dve", t_xln)] * 4

        for e in range(NEXP):
            P.wait("sync", "st", t_cws)
            P.wait("sync", "dve", P.cnt["dve"])
            t_cwb = P.dma("sync", cwb[:],
                          bass.AP(cw_scr.tensor, e * 4096 + g * 1024,
                                  [[0, 128], [1, 1024]]),
                          allow_slow_non_contiguous=True)
            slab_t = [0, 0]

            def issue_a(q8, e=e):
                slot = q8 % 2
                P.wait("gpsimd", "pe", w1_free[slot])
                P.dma("gpsimd", w1q[:, slot],
                      W1h[e][:, q8 * 512:(q8 + 1) * 512]
                      .rearrange("(c p) f -> p c f", p=128), sem="dmag")
                P.wait("gpsimd", "pe", w23_free[slot])
                slab_t[slot] = P.dma(
                    "gpsimd", w2q[:, slot],
                    W2h[e][:, q8 * 512:(q8 + 1) * 512]
                    .rearrange("(c p) f -> p c f", p=128), sem="dmag")

            issue_a(0)
            issue_a(1)
            P.wait("tensor", "dve", t_xln)
            for q8 in range(8):
                slot = q8 % 2
                P.wait("tensor", "dmag", slab_t[slot])
                for cq in range(4):
                    c = q8 * 4 + cq
                    for nh in range(2):
                        k = (cq * 2 + nh) % 2
                        p1 = psA[:, k * 512:(k + 1) * 512]
                        p2 = psA[:, 1024 + k * 512:1024 + (k + 1) * 512]
                        P.wait("tensor", hg[k][0], hg[k][1])
                        P.wait("tensor", hg[2 + k][0], hg[2 + k][1])
                        for dc in range(NCH):
                            P.op("tensor", "matmul", p1,
                                 w1q[:, slot, dc, cq * 128:(cq + 1) * 128],
                                 xln[:, dc, nh * 512:(nh + 1) * 512],
                                 start=(dc == 0), stop=(dc == NCH - 1),
                                 sem="pe" if dc == NCH - 1 else None)
                        tm1 = P.cnt["pe"]
                        for dc in range(NCH):
                            P.op("tensor", "matmul", p2,
                                 w2q[:, slot, dc, cq * 128:(cq + 1) * 128],
                                 xln[:, dc, nh * 512:(nh + 1) * 512],
                                 start=(dc == 0), stop=(dc == NCH - 1),
                                 sem="pe" if dc == NCH - 1 else None)
                        tm2 = P.cnt["pe"]
                        P.wait("scalar", "pe", tm1)
                        col = e * 32 + c
                        ta = P.op("scalar", "activation", s1[:, k, :], p1,
                                  AF.Silu, bias=b1_pp[:, col:col + 1],
                                  sem="act")
                        hg[k] = ("act", ta)
                        P.wait("vector", "pe", tm2)
                        P.wait("vector", "act", ta)
                        tv = P.op("vector", "scalar_tensor_tensor",
                                  hh[:, c, nh * 512:(nh + 1) * 512], p2,
                                  b2_pp[:, col:col + 1], s1[:, k, :],
                                  ALU.add, ALU.mult, sem="dve")
                        hg[2 + k] = ("dve", tv)
                w1_free[slot] = w23_free[slot] = P.cnt["pe"]
                if q8 + 2 < 8:
                    issue_a(q8 + 2)
            t_h = ("dve", P.cnt["dve"])

            w3st = [0, 0]

            def w3_lhsT(slot, rc, dc):
                off = slot * 4096 + rc * 1024 + dc * 128
                return bass.AP(w2q.ap().tensor, off, [[8192, 128], [1, 128]])

            def w3_dst(slot):
                return bass.AP(w2q.ap().tensor, slot * 4096,
                               [[8192, 128], [1024, 4], [1, 1024]])

            def issue_w3(step, e=e):
                q8 = step % 8
                slot = step % 2
                P.wait("gpsimd", "pe", w23_free[slot])
                w3st[slot] = P.dma(
                    "gpsimd", w3_dst(slot),
                    W3h[e][q8 * 512:(q8 + 1) * 512, :]
                    .rearrange("(c p) f -> p c f", p=128), sem="dmag")

            issue_w3(0)
            issue_w3(1)
            for step in range(16):
                th, q8 = step // 8, step % 8
                slot = step % 2
                P.wait("tensor", "dmag", w3st[slot])
                if step == 0:
                    P.wait("tensor", t_h[0], t_h[1])
                    for dc in range(8):
                        P.wait("tensor", yg[dc][0], yg[dc][1])
                tm = None
                for rc in range(4):
                    c = q8 * 4 + rc
                    for dc in range(8):
                        reg = (psA if dc < 4 else psB)[
                            :, (dc % 4) * 512:(dc % 4 + 1) * 512]
                        tm = P.op("tensor", "matmul", reg,
                                  w3_lhsT(slot, rc, dc),
                                  hh[:, c, th * 512:(th + 1) * 512],
                                  start=(c == 0), stop=(c == 31),
                                  sem="pe" if (rc == 3 and dc == 7) else None)
                w23_free[slot] = tm
                if step + 2 < 16:
                    issue_w3(step + 2)
                if q8 == 7:
                    P.wait("vector", "pe", tm)
                    P.wait("vector", "dma", t_cwb)
                    P.wait("vector", "dve", t_y0)
                    for dc in range(8):
                        reg = (psA if dc < 4 else psB)[
                            :, (dc % 4) * 512:(dc % 4 + 1) * 512]
                        col = e * 8 + dc
                        P.op("vector", "scalar_tensor_tensor", tt1[:], reg,
                             b3_pp[:, col:col + 1],
                             cwb[:, th * 512:(th + 1) * 512], ALU.add,
                             ALU.mult)
                        tv = P.op("vector", "tensor_tensor",
                                  yy[:, dc, th * 512:(th + 1) * 512],
                                  yy[:, dc, th * 512:(th + 1) * 512], tt1[:],
                                  ALU.add, sem="dve")
                        yg[dc] = ("dve", tv)
                    if th == 0:
                        for dc in range(8):
                            P.wait("tensor", yg[dc][0], yg[dc][1])
            hg = [("dve", P.cnt["dve"])] * 4

        for sub in range(2):
            t0g = g * 1024 + sub * 512
            s_, lo = t0g // SEQ, t0g % SEQ
            P.wait("sync", "dve", P.cnt["dve"])
            P.wait("sync", "st", t_store)
            tx = P.dma("sync", xg[:], x_in[s_, :, :, lo:lo + 512]
                       .rearrange("c p t -> p c t"))
            P.wait("vector", "dma", tx)
            tv = P.op("vector", "tensor_tensor", xg[:],
                      xg[:], yy[:, :, sub * 512:(sub + 1) * 512], ALU.add,
                      sem="dve")
            P.wait("sync", "dve", tv)
            t_store = P.dma("sync", x_out[s_, :, :, lo:lo + 512]
                            .rearrange("c p t -> p c t"), xg[:], sem="st")
    P.wait("sync", "st", P.cnt["st"])
    st.close()



def emit_output(P, com, io):
    nc = P.nc
    P.new_phase()
    st = ExitStack()
    x_in = io["x_fm_in"]
    out = io["out"]
    xi = st.enter_context(_sb(nc, "o_xi", [128, 8, 128], F32))
    xo = st.enter_context(_sb(nc, "o_xo", [128, 8, 128], F32))
    pst = [st.enter_context(_psm(nc, f"o_ps{i}", [128, 128], F32))
           for i in range(2)]
    t_st = 0
    for s in range(NSEQ):
        for tc in range(16):
            P.wait("sync", "dve", P.cnt.get("dve", 0))
            td = P.dma("sync", xi[:],
                       x_in[s, :, :, tc * 128:(tc + 1) * 128]
                       .rearrange("c p t -> p c t"))
            P.wait("tensor", "dma", td)
            P.wait("tensor", "dve", com.ready)
            for dc in range(NCH):
                pb = pst[dc % 2]
                tp = P.op("tensor", "transpose", pb[:], xi[:, dc, :],
                          com.ident_f[:], sem="pe")
                P.wait("vector", "pe", tp)
                if dc == 0:
                    P.wait("vector", "st", t_st)
                tv = P.op("vector", "tensor_copy", xo[:, dc, :], pb[:],
                          sem="dve")
                P.wait("tensor", "dve", tv)
            P.wait("sync", "dve", P.cnt["dve"])
            t_st = P.dma("sync",
                         out[s, tc * 128:(tc + 1) * 128, :]
                         .rearrange("t (c p) -> t c p", p=128),
                         xo[:], sem="st")
    P.wait("sync", "st", P.cnt["st"])
    st.close()



def _din(nc, name, shape, dt):
    return nc.dram_tensor(name, list(shape), dt, kind="ExternalInput").ap()


def _dout(nc, name, shape, dt):
    return nc.dram_tensor(name, list(shape), dt, kind="ExternalOutput").ap()


XFM_SHAPE = (NSEQ, 8, 128, 2048)


def build_prep():
    nc = bass.Bass("TRN2", target_bir_lowering=False)
    io = {
        "x_in": _din(nc, "x_in", (NSEQ, SEQ, D), F32),
        "q": _din(nc, "q", (6, D, D), I32),
        "scales": _din(nc, "scales", (6, D, 64), F32),
        "lora_a": _din(nc, "lora_a", (6, 32, D), F32),
        "lora_b": _din(nc, "lora_b", (6, D, 32), F32),
        "w_proj": _din(nc, "w_proj", (4 * D, D), F32),
        "w_delta": _din(nc, "w_delta", (D, D), F32),
        "x_fm": _dout(nc, "x_fm", XFM_SHAPE, F32),
        "wt_ffn0": _dout(nc, "wt_ffn0", (6, D, D), BF16),
        "laT": _dout(nc, "laT", (128, 6, 8, 32), BF16),
        "lbT": _dout(nc, "lbT", (32, 6, 8, 128), BF16),
        "wprojT": _dout(nc, "wprojT", (D, 4 * D), BF16),
        "wdeltaT": _dout(nc, "wdeltaT", (D, D), BF16),
        "wcombT": _dout(nc, "wcombT", (D, D), BF16),
    }
    P = Prog(nc)
    com = Commons(P)
    emit_prep(P, com, io)
    P.build()
    return nc


def build_ffn0(n_groups=4, n_blocks=3):
    nc = bass.Bass("TRN2", target_bir_lowering=False)
    io = {
        "x_fm_in": _din(nc, "x_fm_in", XFM_SHAPE, F32),
        "wt_ffn0": _din(nc, "wt_ffn0", (6, D, D), BF16),
        "laT": _din(nc, "laT", (128, 6, 8, 32), BF16),
        "lbT": _din(nc, "lbT", (32, 6, 8, 128), BF16),
        "biases": _din(nc, "biases", (6, D), F32),
        "ln_gamma": _din(nc, "ln_gamma", (3, D), F32),
        "ln_beta": _din(nc, "ln_beta", (3, D), F32),
        "x_fm_out": _dout(nc, "x_fm_out", XFM_SHAPE, F32),
    }
    P = Prog(nc)
    com = Commons(P)
    emit_ffn0(P, com, io, n_groups, n_blocks)
    P.build()
    return nc


def build_mixer():
    nc = bass.Bass("TRN2", target_bir_lowering=False)
    io = {
        "x_fm_in": _din(nc, "x_fm_in", XFM_SHAPE, F32),
        "wprojT": _din(nc, "wprojT", (D, 4 * D), BF16),
        "wcombT": _din(nc, "wcombT", (D, D), BF16),
        "A": _din(nc, "A", (1, 1, D), F32),
        "Bp": _din(nc, "Bp", (1, D, 1), F32),
        "D": _din(nc, "D", (1,), F32),
        "b_delta": _din(nc, "b_delta", (D,), F32),
        "x_fm_out": _dout(nc, "x_fm_out", XFM_SHAPE, F32),
        "dbg_uc": _dout(nc, "dbg_uc", (128, 8), F32),
        "dbg_cs": _dout(nc, "dbg_cs", (128, 8), F32),
        "dbg_dlt": _dout(nc, "dbg_dlt", (128, 2048), F32),
        "dbg_pre": _dout(nc, "dbg_pre", (128, 2048), F32),
        "dbg_bd1": _dout(nc, "dbg_bd1", (128, 8), F32),
        "dbg_bd2": _dout(nc, "dbg_bd2", (128, 8), F32),
        "dbg_abs": _dout(nc, "dbg_abs", (128, 2048), F32),
        "dbg_ln": _dout(nc, "dbg_ln", (128, 2048), F32),
        "dbg_scan": _dout(nc, "dbg_scan", (128, 2048), F32),
    }
    P = Prog(nc)
    import os
    P.serialize = tuple((os.environ.get("KSER") or "").split(",")) if os.environ.get("KSER") else ()
    com = Commons(P)
    emit_mixer(P, com, io)
    P.build()
    return nc


def build_moe():
    nc = bass.Bass("TRN2", target_bir_lowering=False)
    io = {
        "x_fm_in": _din(nc, "x_fm_in", XFM_SHAPE, F32),
        "Wr": _din(nc, "Wr", (D, 8), F32),
        "br": _din(nc, "br", (8,), F32),
        "W1": _din(nc, "W1", (NEXP, D, DFF), F32),
        "b1": _din(nc, "b1", (NEXP, DFF), F32),
        "W2": _din(nc, "W2", (NEXP, D, DFF), F32),
        "b2": _din(nc, "b2", (NEXP, DFF), F32),
        "W3": _din(nc, "W3", (NEXP, DFF, D), F32),
        "b3": _din(nc, "b3", (NEXP, D), F32),
        "x_fm_out": _dout(nc, "x_fm_out", XFM_SHAPE, F32),
        "cw_scr": nc.dram_tensor("cw_scr", [8, 4096], F32,
                                 kind="Internal").ap(),
    }
    P = Prog(nc)
    com = Commons(P)
    emit_moe(P, com, io)
    P.build()
    return nc


def build_attn():
    nc = bass.Bass("TRN2", target_bir_lowering=False)
    io = {
        "x_fm_in": _din(nc, "x_fm_in", XFM_SHAPE, F32),
        "W_Q": _din(nc, "W_Q", (D, D), F32),
        "W_K": _din(nc, "W_K", (D, D), F32),
        "W_V": _din(nc, "W_V", (D, D), F32),
        "W_o": _din(nc, "W_o", (D, D), F32),
        "x_fm_out": _dout(nc, "x_fm_out", XFM_SHAPE, F32),
    }
    P = Prog(nc)
    com = Commons(P)
    emit_attn(P, com, io)
    P.build()
    return nc


def _shard_inputs(inputs):
    x = np.ascontiguousarray(
        inputs["L0_attn_x"].reshape(8, NSEQ, SEQ, D))
    return x


def build_output():
    nc = bass.Bass("TRN2", target_bir_lowering=False)
    io = {
        "x_fm_in": _din(nc, "x_fm_in", XFM_SHAPE, F32),
        "out": _dout(nc, "out", (NSEQ, SEQ, D), F32),
    }
    P = Prog(nc)
    com = Commons(P)
    emit_output(P, com, io)
    P.build()
    return nc


def build_fused():
    nc = bass.Bass("TRN2", target_bir_lowering=False)
    def _int(name, shape, dt):
        return nc.dram_tensor(name, list(shape), dt, kind="Internal").ap()
    xfm = _int("x_fm", XFM_SHAPE, F32)
    io = {
        "x_in": _din(nc, "x_in", (NSEQ, SEQ, D), F32),
        "q": _din(nc, "q", (6, D, D), I32),
        "scales": _din(nc, "scales", (6, D, 64), F32),
        "lora_a": _din(nc, "lora_a", (6, 32, D), F32),
        "lora_b": _din(nc, "lora_b", (6, D, 32), F32),
        "w_proj": _din(nc, "w_proj", (4 * D, D), F32),
        "w_delta": _din(nc, "w_delta", (D, D), F32),
        "biases": _din(nc, "biases", (6, D), F32),
        "ln_gamma": _din(nc, "ln_gamma", (3, D), F32),
        "ln_beta": _din(nc, "ln_beta", (3, D), F32),
        "A": _din(nc, "A", (1, 1, D), F32),
        "Bp": _din(nc, "Bp", (1, D, 1), F32),
        "D": _din(nc, "D", (1,), F32),
        "b_delta": _din(nc, "b_delta", (D,), F32),
        "Wr": _din(nc, "Wr", (D, 8), F32),
        "br": _din(nc, "br", (8,), F32),
        "W1": _din(nc, "W1", (NEXP, D, DFF), F32),
        "b1": _din(nc, "b1", (NEXP, DFF), F32),
        "W2": _din(nc, "W2", (NEXP, D, DFF), F32),
        "b2": _din(nc, "b2", (NEXP, DFF), F32),
        "W3": _din(nc, "W3", (NEXP, DFF, D), F32),
        "b3": _din(nc, "b3", (NEXP, D), F32),
        "W_Q": _din(nc, "W_Q", (D, D), F32),
        "W_K": _din(nc, "W_K", (D, D), F32),
        "W_V": _din(nc, "W_V", (D, D), F32),
        "W_o": _din(nc, "W_o", (D, D), F32),
        "x_fm": xfm, "x_fm_in": xfm, "x_fm_out": xfm,
        "wt_ffn0": _int("wt_ffn0", (6, D, D), BF16),
        "laT": _int("laT", (128, 6, 8, 32), BF16),
        "lbT": _int("lbT", (32, 6, 8, 128), BF16),
        "wprojT": _int("wprojT", (D, 4 * D), BF16),
        "wdeltaT": _int("wdeltaT", (D, D), BF16),
        "wcombT": _int("wcombT", (D, D), BF16),
        "cw_scr": _int("cw_scr", (8, 4096), F32),
        "W1c": _int("W1c", (NEXP, D, DFF), BF16),
        "W2c": _int("W2c", (NEXP, D, DFF), BF16),
        "W3c": _int("W3c", (NEXP, DFF, D), BF16),
        "wqkvo_c": _int("wqkvo_c", (4, D, D), BF16),
        "wr_c": _int("wr_c", (D, 8), BF16),
        "out": _dout(nc, "out", (NSEQ, SEQ, D), F32),
    }
    P = Prog(nc)
    com = Commons(P)
    emit_prep(P, com, io)
    emit_ffn0(P, com, io)
    emit_mixer(P, com, io)
    emit_ffn0(P, com, io)
    emit_moe(P, com, io)
    emit_attn(P, com, io)
    emit_moe(P, com, io)
    emit_output(P, com, io)
    P.build()
    return nc


def kernel(**inputs):
    x_sh = _shard_inputs(inputs)
    n = 8
    nc = build_fused()
    in_maps = [{
        "x_in": x_sh[c],
        "q": inputs["L0_ffn_qweights"],
        "scales": inputs["L0_ffn_scales"],
        "lora_a": inputs["L0_ffn_lora_a"],
        "lora_b": inputs["L0_ffn_lora_b"],
        "w_proj": inputs["L0_attn_W_proj"],
        "w_delta": inputs["L0_attn_W_delta"],
        "biases": inputs["L0_ffn_biases"],
        "ln_gamma": inputs["L0_ffn_ln_gamma"],
        "ln_beta": inputs["L0_ffn_ln_beta"],
        "A": inputs["L0_attn_A"],
        "Bp": inputs["L0_attn_Bp"],
        "D": inputs["L0_attn_D"],
        "b_delta": inputs["L0_attn_b_delta"],
        "Wr": inputs["L1_ffn_Wr"], "br": inputs["L1_ffn_br"],
        "W1": inputs["L1_ffn_W1"], "b1": inputs["L1_ffn_b1"],
        "W2": inputs["L1_ffn_W2"], "b2": inputs["L1_ffn_b2"],
        "W3": inputs["L1_ffn_W3"], "b3": inputs["L1_ffn_b3"],
        "W_Q": inputs["L1_attn_W_Q"], "W_K": inputs["L1_attn_W_K"],
        "W_V": inputs["L1_attn_W_V"], "W_o": inputs["L1_attn_W_o"],
    } for c in range(n)]
    res = run_bass_kernel_spmd(nc, in_maps, core_ids=list(range(n))).results
    out = np.concatenate([np.asarray(r["out"]) for r in res], axis=0)
    return out.reshape(16, SEQ, D)


def kernel_unfused(**inputs):
    x_sh = _shard_inputs(inputs)
    n = 8

    def run(nc, maps):
        return run_bass_kernel_spmd(nc, maps, core_ids=list(range(n))).results

    res = run(build_prep(), [{
        "x_in": x_sh[c],
        "q": inputs["L0_ffn_qweights"],
        "scales": inputs["L0_ffn_scales"],
        "lora_a": inputs["L0_ffn_lora_a"],
        "lora_b": inputs["L0_ffn_lora_b"],
        "w_proj": inputs["L0_attn_W_proj"],
        "w_delta": inputs["L0_attn_W_delta"],
    } for c in range(n)])
    state = [dict(r) for r in res]

    nc_ffn0 = build_ffn0()

    def ffn0_maps():
        return [{
            "x_fm_in": state[c]["x_fm"],
            "wt_ffn0": state[c]["wt_ffn0"],
            "laT": state[c]["laT"],
            "lbT": state[c]["lbT"],
            "biases": inputs["L0_ffn_biases"],
            "ln_gamma": inputs["L0_ffn_ln_gamma"],
            "ln_beta": inputs["L0_ffn_ln_beta"],
        } for c in range(n)]

    res = run(nc_ffn0, ffn0_maps())
    for c in range(n):
        state[c]["x_fm"] = res[c]["x_fm_out"]

    res = run(build_mixer(), [{
        "x_fm_in": state[c]["x_fm"],
        "wprojT": state[c]["wprojT"],
        "wcombT": state[c]["wcombT"],
        "A": inputs["L0_attn_A"],
        "Bp": inputs["L0_attn_Bp"],
        "D": inputs["L0_attn_D"],
        "b_delta": inputs["L0_attn_b_delta"],
    } for c in range(n)])
    for c in range(n):
        state[c]["x_fm"] = res[c]["x_fm_out"]

    res = run(nc_ffn0, ffn0_maps())
    for c in range(n):
        state[c]["x_fm"] = res[c]["x_fm_out"]

    nc_moe = build_moe()

    def moe_maps():
        return [{
            "x_fm_in": state[c]["x_fm"],
            "Wr": inputs["L1_ffn_Wr"], "br": inputs["L1_ffn_br"],
            "W1": inputs["L1_ffn_W1"], "b1": inputs["L1_ffn_b1"],
            "W2": inputs["L1_ffn_W2"], "b2": inputs["L1_ffn_b2"],
            "W3": inputs["L1_ffn_W3"], "b3": inputs["L1_ffn_b3"],
        } for c in range(n)]

    res = run(nc_moe, moe_maps())
    for c in range(n):
        state[c]["x_fm"] = res[c]["x_fm_out"]

    res = run(build_attn(), [{
        "x_fm_in": state[c]["x_fm"],
        "W_Q": inputs["L1_attn_W_Q"], "W_K": inputs["L1_attn_W_K"],
        "W_V": inputs["L1_attn_W_V"], "W_o": inputs["L1_attn_W_o"],
    } for c in range(n)])
    for c in range(n):
        state[c]["x_fm"] = res[c]["x_fm_out"]

    res = run(nc_moe, moe_maps())
    for c in range(n):
        state[c]["x_fm"] = res[c]["x_fm_out"]

    res = run(build_output(), [{"x_fm_in": state[c]["x_fm"]}
                               for c in range(n)])
    out = np.concatenate([np.asarray(res[c]["out"]) for c in range(n)],
                         axis=0)
    return out.reshape(16, SEQ, D)


# version 07
# speedup vs baseline: 1.0177x; 1.0177x over previous
import numpy as np
from contextlib import ExitStack
import concourse.bass as bass
import concourse.mybir as mybir
from concourse.bass_utils import run_bass_kernel_spmd

F32 = mybir.dt.float32
BF16 = mybir.dt.bfloat16
I32 = mybir.dt.int32
U32 = mybir.dt.uint32
AF = mybir.ActivationFunctionType
ALU = mybir.AluOpType
AX = mybir.AxisListType

D = 1024
NCH = 8
SEQ = 2048
NSEQ = 2
EPS = 1e-5
DFF = 4096
NEXP = 8

ENGINES = ("sync", "gpsimd", "tensor", "vector", "scalar")



_UNIQ = [0]


def _sb(nc, name, shape, dt, **kw):
    _UNIQ[0] += 1
    return nc.sbuf_tensor(f"{name}_{_UNIQ[0]}", shape, dt, **kw)


def _psm(nc, name, shape, dt):
    _UNIQ[0] += 1
    return nc.psum_tensor(f"{name}_{_UNIQ[0]}", shape, dt)


class Prog:

    def __init__(self, nc):
        self.nc = nc
        self.phases = []
        self.cur = None
        self.sems = {}
        self.cnt = {}
        self.stack = ExitStack()
        self.new_phase()

    def sem(self, name):
        if name not in self.sems:
            self.sems[name] = self.stack.enter_context(self.nc.semaphore(name))
            self.cnt[name] = 0
        return self.sems[name]

    def new_phase(self):
        self.cur = {e: [] for e in ENGINES}
        self.phases.append(self.cur)

    def emit(self, eng, fn):
        self.cur[eng].append(fn)

    def inc(self, sem_name, n=1):
        self.sem(sem_name)
        self.cnt[sem_name] += n
        return self.cnt[sem_name]

    def wait(self, eng, sem_name, val):
        h = self.sem(sem_name)
        self.emit(eng, lambda e, h=h, v=val: e.wait_ge(h, v))

    serialize = ()

    def op(self, eng, method, *args, sem=None, n=1, **kw):
        ser = eng in self.serialize and eng in ("vector", "scalar", "gpsimd",
                                                "tensor")
        if ser:
            last = getattr(self, "_last_op", {}).get(eng)
            if last is not None:
                self.wait(eng, last[0], last[1])
            if sem is None:
                sem, n = "_ser_" + eng, 1
        t = None
        h = None
        if sem is not None:
            t = self.inc(sem, n)
            h = self.sems[sem]

        def fn(e, method=method, args=args, kw=kw, h=h, n=n):
            instr = getattr(e, method)(*args, **kw)
            if h is not None:
                instr.then_inc(h, n)
        self.emit(eng, fn)
        if ser:
            if not hasattr(self, "_last_op"):
                self._last_op = {}
            self._last_op[eng] = (sem, t)
        return t

    def dma(self, q, out, in_, sem="dma", **kw):
        return self.op(q, "dma_start", sem=sem, n=16, out=out, in_=in_, **kw)

    def build(self):
        for ph in self.phases:
            if not any(ph[e] for e in ENGINES):
                continue
            with self.nc.Block() as block:
                for eng in ENGINES:
                    if ph[eng]:
                        def body(E, ops=tuple(ph[eng])):
                            for f in ops:
                                f(E)
                        getattr(block, eng)(body)
        self.stack.close()


class Commons:

    def __init__(self, P):
        nc, st = P.nc, P.stack
        self.ident_f = st.enter_context(_sb(nc, "c_identf", [128, 128], F32))
        self.ident_b = st.enter_context(_sb(nc, "c_identb", [128, 128], BF16))
        self.ones_b = st.enter_context(_sb(nc, "c_onesb", [128, 1], BF16))
        self.ones_f = st.enter_context(_sb(nc, "c_onesf", [1, 128], F32))
        self.iota_f = st.enter_context(_sb(nc, "c_iotaf", [128, 128], F32))
        self.iota_p = st.enter_context(_sb(nc, "c_iotap", [128, 128], F32))
        self.eps_c = st.enter_context(_sb(nc, "c_eps", [128, 1], F32))
        P.op("gpsimd", "memset", self.eps_c[:], EPS)
        P.op("gpsimd", "iota", self.iota_f[:], [[1, 128]], channel_multiplier=0,
             allow_small_or_imprecise_dtypes=True)
        P.op("gpsimd", "iota", self.iota_p[:], [[0, 128]], channel_multiplier=1,
             allow_small_or_imprecise_dtypes=True)
        P.op("gpsimd", "memset", self.ones_b[:], 1.0)
        P.op("gpsimd", "memset", self.ones_f[:], 1.0, sem="pool")
        P.wait("vector", "pool", P.cnt["pool"])
        P.op("vector", "tensor_tensor", self.ident_f[:], self.iota_f[:],
             self.iota_p[:], ALU.is_equal)
        P.op("vector", "tensor_copy", self.ident_b[:], self.ident_f[:], sem="dve")
        self.ready = P.cnt["dve"]


def load_pp(P, com, st, hbm_2d, n, name, ps=None, guard=0):
    nc = P.nc
    dest = st.enter_context(_sb(nc, name, [128, n], F32))
    td = P.dma("sync", dest[:], hbm_2d.rearrange("n p -> p n"),
               allow_slow_non_contiguous=True)
    return dest, td



def emit_prep(P, com, io):
    nc = P.nc
    P.new_phase()
    st = ExitStack()
    x_in = io["x_in"]
    x_fm = io["x_fm"]

    if "W1c" in io:
        for e in range(NEXP):
            for src_t, dst_t in (("W1", "W1c"), ("W2", "W2c"),
                                 ("W3", "W3c")):
                a, b = io[src_t][e], io[dst_t][e]
                if a.shape[-1] > 2048:
                    a = a.rearrange("a (b f) -> a b f", f=2048)
                    b = b.rearrange("a (b f) -> a b f", f=2048)
                P.dma("gpsimd", b, a, sem="dmag")
    if "wqkvo_c" in io:
        for i, nm in enumerate(("W_Q", "W_K", "W_V", "W_o")):
            P.dma("gpsimd", io["wqkvo_c"][i], io[nm], sem="dmag")
        P.dma("gpsimd", io["wr_c"], io["Wr"], sem="dmag")
    t_wcache = P.cnt.get("dmag", 0)

    xt = st.enter_context(_sb(nc, "pr_xt", [128, 1024], F32))
    xo = st.enter_context(_sb(nc, "pr_xo", [128, 8, 128], F32))
    pst = [st.enter_context(_psm(nc, f"pr_ps{i}", [128, 128], F32))
           for i in range(2)]
    t_st = 0
    for s in range(NSEQ):
        for tc in range(16):
            P.wait("sync", "dve", P.cnt.get("dve", 0))
            td = P.dma("sync", xt[:], x_in[s, tc * 128:(tc + 1) * 128, :])
            P.wait("tensor", "dma", td)
            P.wait("tensor", "dve", com.ready)
            for dc in range(NCH):
                pb = pst[dc % 2]
                tp = P.op("tensor", "transpose", pb[:],
                          xt[:, dc * 128:(dc + 1) * 128], com.ident_f[:],
                          sem="pe")
                P.wait("vector", "pe", tp)
                if dc == 0:
                    P.wait("vector", "st", t_st)
                tv = P.op("vector", "tensor_copy", xo[:, dc, :], pb[:],
                          sem="dve")
                P.wait("tensor", "dve", tv)
            P.wait("sync", "dve", P.cnt["dve"])
            t_st = P.dma("sync",
                         x_fm[s, :, :, tc * 128:(tc + 1) * 128]
                         .rearrange("c p t -> p c t"), xo[:], sem="st")

    wrow = st.enter_context(_sb(nc, "pr_wrow", [128, 1024], F32))
    wbf = st.enter_context(_sb(nc, "pr_wbf", [128, 1024], BF16))
    qsb = st.enter_context(_sb(nc, "pr_qsb", [128, 1024], I32))
    scl = st.enter_context(_sb(nc, "pr_scl", [128, 64], F32))
    psb = [st.enter_context(_psm(nc, f"pr_psb{i}", [128, 128], BF16))
           for i in range(2)]
    wtbig = st.enter_context(_sb(nc, "pr_wtbig", [128, 8, 4096], BF16))

    def transpose_mat(src_rows_get, n_out, out_ap3):
        nonlocal t_st
        for oc in range(n_out // 128):
            tready = src_rows_get(oc)
            P.wait("tensor", "dve", tready)
            for dc in range(NCH):
                pb = psb[dc % 2]
                tp = P.op("tensor", "transpose", pb[:],
                          wbf[:, dc * 128:(dc + 1) * 128], com.ident_b[:],
                          sem="pe")
                P.wait("vector", "pe", tp)
                if dc == 0 and oc == 0:
                    P.wait("vector", "st", t_st)
                tv = P.op("vector", "tensor_copy",
                          wtbig[:, dc, oc * 128:(oc + 1) * 128], pb[:],
                          sem="dve")
                P.wait("tensor", "dve", tv)
        P.wait("sync", "dve", P.cnt["dve"])
        t_st = P.dma("sync", out_ap3, wtbig[:, :, 0:n_out], sem="st")

    if "q" in io:
        qw, scales, wt_out = io["q"], io["scales"], io["wt_ffn0"]
        for i in range(6):
            def get_rows(oc, i=i):
                P.wait("sync", "dve", P.cnt["dve"])
                td = P.dma("sync", qsb[:], qw[i, oc * 128:(oc + 1) * 128, :])
                ts_ = P.dma("sync", scl[:], scales[i, oc * 128:(oc + 1) * 128, :])
                P.wait("vector", "dma", max(td, ts_))
                P.op("vector", "tensor_scalar", wrow[:], qsb[:],
                     1.0 / 7.5, -1.0, ALU.mult, op1=ALU.add)
                scl_b = bass.AP(scl.ap().tensor, 0, [[64, 128], [1, 64], [0, 16]])
                return P.op("vector", "tensor_tensor", wbf[:], wrow[:],
                            scl_b, ALU.mult, sem="dve")
            transpose_mat(get_rows, 1024,
                          wt_out[i].rearrange("(c p) o -> p c o", p=128))

    def plain_rows(src2d):
        def get_rows(oc):
            P.wait("sync", "dve", P.cnt["dve"])
            td = P.dma("sync", wrow[:], src2d[oc * 128:(oc + 1) * 128, :])
            P.wait("vector", "dma", td)
            return P.op("vector", "tensor_copy", wbf[:], wrow[:], sem="dve")
        return get_rows

    if "w_proj" in io:
        transpose_mat(plain_rows(io["w_proj"]), 4096,
                      io["wprojT"].rearrange("(c p) o -> p c o", p=128))
        transpose_mat(plain_rows(io["w_delta"]), 1024,
                      io["wdeltaT"].rearrange("(c p) o -> p c o", p=128))

        wdT = st.enter_context(_sb(nc, "pr_wdT", [128, 8, 1024], BF16))
        wpr = st.enter_context(_sb(nc, "pr_wpr", [128, 8, 1024], BF16))
        P.wait("sync", "st", t_st)
        td = P.dma("sync", wdT[:],
                   io["wdeltaT"].rearrange("(c p) o -> p c o", p=128))
        for o1c in range(8):
            P.wait("sync", "dve", P.cnt["dve"])
            tw = P.dma("sync", wrow[:],
                       io["w_proj"][o1c * 128:(o1c + 1) * 128, :])
            P.wait("vector", "dma", tw)
            P.op("vector", "tensor_copy", wpr[:, o1c, :], wrow[:], sem="dve")
        tcast = P.cnt["dve"]
        P.wait("tensor", "dma", td)
        P.wait("tensor", "dve", tcast)
        pcmb = [st.enter_context(_psm(nc, f"pr_pc{i}", [128, 512], F32))
                for i in range(2)]
        ev_t = [0, 0]
        P.wait("vector", "st", t_st)
        for dc in range(8):
            for nt in range(2):
                k = (dc * 2 + nt) % 2
                P.wait("tensor", "dve", ev_t[k])
                tm = None
                for o1c in range(8):
                    tm = P.op("tensor", "matmul", pcmb[k][:],
                              wpr[:, o1c, dc * 128:(dc + 1) * 128],
                              wdT[:, o1c, nt * 512:(nt + 1) * 512],
                              start=(o1c == 0), stop=(o1c == 7),
                              sem="pe" if o1c == 7 else None)
                P.wait("vector", "pe", tm)
                ev_t[k] = P.op("vector", "tensor_copy",
                               wtbig[:, dc, nt * 512:(nt + 1) * 512],
                               pcmb[k][:], sem="dve")
        P.wait("sync", "dve", P.cnt["dve"])
        t_st = P.dma("sync",
                     io["wcombT"].rearrange("(c p) o -> p c o", p=128),
                     wtbig[:, :, 0:1024], sem="st")

    if "lora_a" in io:
        la, lb, laT_h, lbT_h = io["lora_a"], io["lora_b"], io["laT"], io["lbT"]
        lasb = st.enter_context(_sb(nc, "pr_lasb", [32, 1024], F32))
        lasbb = st.enter_context(_sb(nc, "pr_lasbb", [32, 1024], BF16))
        laacc = st.enter_context(_sb(nc, "pr_laacc", [128, 8, 32], BF16))
        for i in range(6):
            P.wait("sync", "dve", P.cnt["dve"])
            td = P.dma("sync", lasb[:], la[i])
            P.wait("vector", "dma", td)
            tv = P.op("vector", "tensor_copy", lasbb[:], lasb[:], sem="dve")
            P.wait("tensor", "dve", tv)
            for dc in range(NCH):
                tp = P.op("tensor", "transpose", psb[dc % 2][:, 0:32],
                          lasbb[:, dc * 128:(dc + 1) * 128],
                          com.ident_b[0:32, 0:32], sem="pe")
                P.wait("vector", "pe", tp)
                if dc == 0:
                    P.wait("vector", "st", t_st)
                tv = P.op("vector", "tensor_copy", laacc[:, dc, :],
                          psb[dc % 2][:, 0:32], sem="dve")
                P.wait("tensor", "dve", tv)
            P.wait("sync", "dve", P.cnt["dve"])
            t_st = P.dma("sync", laT_h[:, i, :, :], laacc[:], sem="st")

        lbsb = st.enter_context(_sb(nc, "pr_lbsb", [128, 32], F32))
        lbsbb = st.enter_context(_sb(nc, "pr_lbsbb", [128, 32], BF16))
        lbacc = st.enter_context(_sb(nc, "pr_lbacc", [32, 8, 128], BF16))
        for i in range(6):
            for oc in range(NCH):
                P.wait("sync", "dve", P.cnt["dve"])
                td = P.dma("sync", lbsb[:], lb[i, oc * 128:(oc + 1) * 128, :])
                P.wait("vector", "dma", td)
                tv = P.op("vector", "tensor_copy", lbsbb[:], lbsb[:], sem="dve")
                P.wait("tensor", "dve", tv)
                tp = P.op("tensor", "transpose", psb[oc % 2][0:32, :],
                          lbsbb[:], com.ident_b[:], sem="pe")
                P.wait("vector", "pe", tp)
                if oc == 0:
                    P.wait("vector", "st", t_st)
                tv = P.op("vector", "tensor_copy", lbacc[:, oc, :],
                          psb[oc % 2][0:32, :], sem="dve")
                P.wait("tensor", "dve", tv)
            P.wait("sync", "dve", P.cnt["dve"])
            t_st = P.dma("sync", lbT_h[:, i, :, :], lbacc[:], sem="st")

    P.wait("sync", "st", P.cnt["st"])
    st.close()



def emit_ln_stats(P, com, src3d, T, xb, xsq, rows, stat_ps, bcast_mb, bcast_rb,
                  wait_act=None):
    nh = T // 512
    P.wait("scalar", "pe", P.cnt.get("pe", 0))
    if wait_act is not None:
        P.wait("scalar", wait_act[0], wait_act[1])
    P.op("scalar", "activation", xb[:, :, 0:T], src3d, AF.Copy)
    ta = P.op("scalar", "activation", xsq[:, :, 0:T], src3d, AF.Square,
              sem="act")
    P.wait("tensor", "act", ta)
    P.wait("tensor", "dve", com.ready)
    tp = None
    for h in range(nh):
        for c in range(NCH):
            P.op("tensor", "matmul", stat_ps[h], com.ones_b[:],
                 xb[:, c, h * 512:(h + 1) * 512],
                 start=(c == 0), stop=(c == NCH - 1))
        for c in range(NCH):
            last = (h == nh - 1 and c == NCH - 1)
            tp = P.op("tensor", "matmul", stat_ps[nh + h], com.ones_b[:],
                      xsq[:, c, h * 512:(h + 1) * 512],
                      start=(c == 0), stop=(c == NCH - 1),
                      sem="pe" if last else None)
    P.wait("vector", "pe", tp)
    for h in range(nh):
        P.op("vector", "tensor_scalar", rows[0:1, 0, h * 512:(h + 1) * 512],
             stat_ps[h], -1.0 / D, None, ALU.mult)
        P.op("vector", "tensor_scalar", rows[0:1, 1, h * 512:(h + 1) * 512],
             stat_ps[nh + h], 1.0 / D, None, ALU.mult)
    P.op("vector", "tensor_tensor", rows[0:1, 2, 0:T], rows[0:1, 0, 0:T],
         rows[0:1, 0, 0:T], ALU.mult)
    tv = P.op("vector", "tensor_tensor", rows[0:1, 2, 0:T], rows[0:1, 1, 0:T],
              rows[0:1, 2, 0:T], ALU.subtract, sem="dve")
    P.wait("scalar", "dve", tv)
    P.op("scalar", "activation", rows[0:1, 3, 0:T], rows[0:1, 2, 0:T],
         AF.Ln, bias=com.eps_c[0:1, :])
    ta = P.op("scalar", "activation", rows[0:1, 3, 0:T], rows[0:1, 3, 0:T],
              AF.Exp, scale=-0.5, sem="act")
    P.wait("tensor", "act", ta)
    tb = None
    for h in range(nh):
        P.op("tensor", "matmul", bcast_mb[h], com.ones_f[:],
             rows[0:1, 0, h * 512:(h + 1) * 512], start=True, stop=True)
        tb = P.op("tensor", "matmul", bcast_rb[h], com.ones_f[:],
                  rows[0:1, 3, h * 512:(h + 1) * 512], start=True, stop=True,
                  sem="pe" if h == nh - 1 else None)
    return tb, tv



def emit_ffn0(P, com, io, n_groups=4, n_blocks=3):
    nc = P.nc
    P.new_phase()
    st = ExitStack()
    x_in = io["x_fm_in"]
    x_out = io["x_fm_out"]
    wt_h = io["wt_ffn0"]
    laT_h = io["laT"]
    lbT_h = io["lbT"]
    T = 1024

    laT = st.enter_context(_sb(nc, "f_laT", [128, 2, 8, 32], BF16))
    lbT = st.enter_context(_sb(nc, "f_lbT", [32, 2, 8, 128], BF16))
    t0 = 0
    psS = [st.enter_context(_psm(nc, f"f_psS{i}", [128, 512], F32))
           for i in range(4)]
    bias_pp, tb1 = load_pp(P, com, st, io["biases"].rearrange("i (c p) -> (i c) p", p=128), 48, "f_bias")
    gam_pp, tb2 = load_pp(P, com, st, io["ln_gamma"].rearrange("i (c p) -> (i c) p", p=128), 24, "f_gam")
    bet_pp, tb3 = load_pp(P, com, st, io["ln_beta"].rearrange("i (c p) -> (i c) p", p=128), 24, "f_bet")
    pp_ready = max(tb1, tb2, tb3)

    wt = [st.enter_context(_sb(nc, f"f_wt{i}", [128, 8, 1024], BF16))
          for i in range(2)]
    xg = st.enter_context(_sb(nc, "f_xg", [128, 8, T], F32))
    hh = st.enter_context(_sb(nc, "f_h", [128, 8, T], F32))
    hb = st.enter_context(_sb(nc, "f_hb", [128, 8, T], BF16))
    yy = st.enter_context(_sb(nc, "f_y", [128, 8, T], F32))
    xb = st.enter_context(_sb(nc, "f_xb", [128, 8, T], BF16))
    xsq = st.enter_context(_sb(nc, "f_xsq", [128, 8, T], BF16))
    gb = xb
    rows = st.enter_context(_sb(nc, "f_rows", [1, 4, T], F32))
    u1 = st.enter_context(_sb(nc, "f_u1", [32, T], BF16))

    psY = [st.enter_context(_psm(nc, f"f_psY{i}", [128, 512], F32))
           for i in range(2)]
    psU = st.enter_context(_psm(nc, "f_psU", [32, 1024], F32))

    P.wait("scalar", "dma", pp_ready)

    y_guard = [("dve", 0), ("dve", 0)]
    t_wt = [0, 0]
    wt_done = [0, 0]
    t_store = 0

    def qmm(mat_i, wtbuf, rhs3d, rhs_ready, u_guard, evac):
        P.wait("tensor", rhs_ready[0], rhs_ready[1])
        P.wait("tensor", "dma", t_wt[wtbuf])
        P.wait("tensor", u_guard[0], u_guard[1])
        tpe = None
        for h2 in range(2):
            for c in range(NCH):
                last = (h2 == 1 and c == NCH - 1)
                tpe = P.op("tensor", "matmul",
                           psU[0:32, h2 * 512:(h2 + 1) * 512],
                           laT[:, mat_i % 2, c, :],
                           rhs3d[:, c, h2 * 512:(h2 + 1) * 512],
                           start=(c == 0), stop=(c == NCH - 1),
                           sem="pe" if last else None)
        P.wait("vector", "pe", tpe)
        u_evac = P.op("vector", "tensor_copy", u1[:], psU[0:32, :], sem="dve")
        P.wait("tensor", "dve", u_evac)
        tpe = 0
        for oc in range(NCH):
            for h2 in range(2):
                k = (oc * 2 + h2) % 2
                pb = psY[k]
                P.wait("tensor", y_guard[k][0], y_guard[k][1])
                for dc in range(NCH):
                    P.op("tensor", "matmul", pb[:],
                         wt[wtbuf][:, dc, oc * 128:(oc + 1) * 128],
                         rhs3d[:, dc, h2 * 512:(h2 + 1) * 512],
                         start=(dc == 0), stop=False)
                tpe = P.op("tensor", "matmul", pb[:],
                           lbT[:, mat_i % 2, oc, :],
                           u1[0:32, h2 * 512:(h2 + 1) * 512],
                           start=False, stop=True, sem="pe")
                evac(oc, h2, pb, tpe, k)
        wt_done[wtbuf] = tpe
        return ("dve", u_evac)

    u_guard = ("dve", 0)
    for g in range(n_groups):
        s, t0g = g // 2, (g % 2) * T
        P.wait("sync", "dve", P.cnt["dve"])
        P.wait("sync", "st", t_store)
        tx = P.dma("sync", xg[:], x_in[s, :, :, t0g:t0g + T]
                   .rearrange("c p t -> p c t"))
        P.wait("sync", "pe", wt_done[0])
        t_wt[0] = P.dma("sync", wt[0][:],
                        wt_h[0].rearrange("(c p) o -> p c o", p=128))

        tpb, _ = emit_ln_stats(P, com, xg[:, :, 0:T], T, xb, xsq, rows,
                               [psS[0][0:1, :], psS[1][0:1, :],
                                psS[2][0:1, :], psS[3][0:1, :]],
                               [psS[0][:], psS[1][:]], [psS[2][:], psS[3][:]],
                               wait_act=("dma", tx))
        P.wait("vector", "pe", tpb)
        for c in range(NCH):
            for h2 in range(2):
                sl = slice(h2 * 512, (h2 + 1) * 512)
                P.op("vector", "tensor_tensor", hh[:, c, sl], xg[:, c, sl],
                     psS[h2][:], ALU.add)
                P.op("vector", "tensor_tensor", hh[:, c, sl], hh[:, c, sl],
                     psS[2 + h2][:], ALU.mult)
            t_hbv = P.op("vector", "tensor_copy", hb[:, c, :], hh[:, c, :],
                         sem="dve" if c == NCH - 1 else None)
        t_hb = ("dve", t_hbv)

        for blk in range(n_blocks):
            i0, i1 = 2 * blk, 2 * blk + 1
            P.wait("sync", "pe", wt_done[1])
            t_wt[1] = P.dma("sync", wt[1][:],
                            wt_h[i1].rearrange("(c p) o -> p c o", p=128))
            P.dma("sync", laT[:], laT_h[:, i0:i0 + 2, :, :])
            t_lo = P.dma("sync", lbT[:], lbT_h[:, i0:i0 + 2, :, :])
            P.wait("tensor", "dma", t_lo)

            def evac1(oc, h2, pb, tpe, k, i0=i0):
                P.wait("scalar", "pe", tpe)
                ta = P.op("scalar", "add",
                          yy[:, oc, h2 * 512:(h2 + 1) * 512], pb[:],
                          bias_pp[:, i0 * 8 + oc:i0 * 8 + oc + 1], sem="act")
                y_guard[k] = ("act", ta)
            u_guard = qmm(i0, 0, hb, t_hb, u_guard, evac1)

            P.wait("tensor", y_guard[0][0], y_guard[0][1])
            P.wait("tensor", y_guard[1][0], y_guard[1][1])
            P.wait("tensor", u_guard[0], u_guard[1])
            tpb, trows = emit_ln_stats(
                P, com, yy[:, :, 0:T], T, xb, xsq, rows,
                [psY[0][0:1, :], psY[1][0:1, :],
                 psU[0:1, 0:512], psU[0:1, 512:1024]],
                [psS[0][:], psS[1][:]], [psS[2][:], psS[3][:]],
                wait_act=None)
            P.wait("vector", "pe", tpb)
            glast = 0
            for c in range(NCH):
                for h2 in range(2):
                    sl = slice(h2 * 512, (h2 + 1) * 512)
                    P.op("vector", "tensor_tensor", yy[:, c, sl], yy[:, c, sl],
                         psS[h2][:], ALU.add)
                    tv = P.op("vector", "tensor_tensor", yy[:, c, sl],
                              yy[:, c, sl], psS[2 + h2][:], ALU.mult,
                              sem="dve")
                    P.wait("scalar", "dve", tv)
                    col = blk * 8 + c
                    glast = P.op("scalar", "activation", gb[:, c, sl],
                                 yy[:, c, sl], AF.Gelu,
                                 bias=bet_pp[:, col:col + 1],
                                 scale=gam_pp[:, col:col + 1], sem="act")

            if blk < n_blocks - 1:
                P.wait("sync", "pe", wt_done[0])
                t_wt[0] = P.dma("sync", wt[0][:],
                                wt_h[2 * blk + 2]
                                .rearrange("(c p) o -> p c o", p=128))

            P.wait("tensor", "dve", trows)
            y_guard[0] = y_guard[1] = ("dve", trows)
            u_guard = ("dve", trows)
            def evac2(oc, h2, pb, tpe, k, i1=i1):
                P.wait("vector", "pe", tpe)
                col = i1 * 8 + oc
                tv = P.op("vector", "scalar_tensor_tensor",
                          hh[:, oc, h2 * 512:(h2 + 1) * 512], pb[:],
                          bias_pp[:, col:col + 1],
                          hh[:, oc, h2 * 512:(h2 + 1) * 512],
                          ALU.add, ALU.add, sem="dve")
                y_guard[k] = ("dve", tv)
            u_guard = qmm(i1, 1, gb, ("act", glast), u_guard, evac2)
            if blk < n_blocks - 1:
                for c in range(NCH):
                    t_hbv = P.op("vector", "tensor_copy", hb[:, c, :],
                                 hh[:, c, :],
                                 sem="dve" if c == NCH - 1 else None)
                t_hb = ("dve", t_hbv)

        tv = P.op("vector", "scalar_tensor_tensor", xg[:, :, 0:T],
                  hh[:, :, 0:T], 0.5, xg[:, :, 0:T], ALU.mult, ALU.add,
                  sem="dve")
        P.wait("sync", "dve", tv)
        t_store = P.dma("sync", x_out[s, :, :, t0g:t0g + T]
                        .rearrange("c p t -> p c t"), xg[:], sem="st")
    P.wait("sync", "st", P.cnt["st"])
    st.close()



def emit_mixer(P, com, io):
    nc = P.nc
    P.new_phase()
    st = ExitStack()
    x_in = io["x_fm_in"]
    x_out = io["x_fm_out"]
    wprojT = io["wprojT"]
    wcombT = io["wcombT"]

    psA = st.enter_context(_psm(nc, "m_psA", [128, 2048], F32))
    psB = st.enter_context(_psm(nc, "m_psB", [128, 2048], F32))

    a_pp, t1 = load_pp(P, com, st, io["A"].rearrange("one1 one2 (c p) -> (one1 one2 c) p", p=128), 8, "m_A")
    b1_pp, t2 = load_pp(P, com, st, io["Bp"].rearrange("one1 (c p) one2 -> (one1 c) (p one2)", p=128), 8, "m_B1")
    bd_pp, t3 = load_pp(P, com, st, io["b_delta"].rearrange("(c p) -> c p", p=128), 8, "m_bd")
    d_pp = st.enter_context(_sb(nc, "m_dpp", [128, 1], F32))
    td = P.dma("sync", d_pp[:],
               bass.AP(io["D"].tensor, 0, [[0, 128], [1, 1]]),
               allow_slow_non_contiguous=True)
    pp_ready = max(t1, t2, t3, td)

    wcomb = st.enter_context(_sb(nc, "m_wcomb", [128, 8, 1024], BF16))
    wpart = st.enter_context(_sb(nc, "m_wpart", [128, 8, 2048], BF16))
    x_mix = st.enter_context(_sb(nc, "m_xmix", [128, 8, 2048], BF16))
    xg = st.enter_context(_sb(nc, "m_xg", [128, 8, 1024], F32))
    xb = st.enter_context(_sb(nc, "m_xb", [128, 8, 1024], BF16))
    xsq = st.enter_context(_sb(nc, "m_xsq", [128, 8, 1024], BF16))
    rows = st.enter_context(_sb(nc, "m_rows", [1, 4, 1024], F32))
    dump = st.enter_context(_sb(nc, "m_dump", [128, 2048], F32))
    scr1 = st.enter_context(_sb(nc, "m_scr1", [128, 2048], F32))
    scr2 = st.enter_context(_sb(nc, "m_scr2", [128, 2048], F32))
    dlt = st.enter_context(_sb(nc, "m_dlt", [128, 2048], F32))
    xc = st.enter_context(_sb(nc, "m_xc", [128, 2048], F32))
    uc_col = st.enter_context(_sb(nc, "m_uc", [128, 8], F32))
    cs_col = st.enter_context(_sb(nc, "m_cs", [128, 8], F32))
    pp1 = st.enter_context(_sb(nc, "m_pp1", [128, 8], F32))

    twc = P.dma("sync", wcomb[:], wcombT.rearrange("(c p) o -> p c o", p=128))
    twp = P.dma("sync", wpart[:, :, 0:1024],
                wprojT[:, 2048:3072].rearrange("(c p) o -> p c o", p=128))
    twp = P.dma("sync", wpart[:, :, 1024:2048],
                wprojT[:, 3072:4096].rearrange("(c p) o -> p c o", p=128))

    P.wait("scalar", "dma", pp_ready)
    P.wait("vector", "dma", pp_ready)
    t_store = 0
    ev_cs = 0
    last_y = 0

    for s in range(NSEQ):
        P.wait("tensor", "act", P.cnt.get("act", P.inc("act", 0)))
        P.wait("tensor", "dve", P.cnt["dve"])
        for grp in range(2):
            sl = slice(grp * 1024, (grp + 1) * 1024)
            P.wait("sync", "dve", P.cnt["dve"])
            P.wait("sync", "st", t_store)
            tx = P.dma("sync", xg[:], x_in[s, :, :, sl]
                       .rearrange("c p t -> p c t"))
            tpb, _ = emit_ln_stats(
                P, com, xg[:, :, 0:1024], 1024, xb, xsq, rows,
                [psA[0:1, 0:512], psA[0:1, 512:1024],
                 psA[0:1, 1024:1536], psA[0:1, 1536:2048]],
                [psA[:, 0:512], psA[:, 512:1024]],
                [psA[:, 1024:1536], psA[:, 1536:2048]],
                wait_act=("dma", tx))
            P.wait("vector", "pe", tpb)
            for c in range(NCH):
                for h2 in range(2):
                    s2 = slice(h2 * 512, (h2 + 1) * 512)
                    P.op("vector", "tensor_tensor", xg[:, c, s2], xg[:, c, s2],
                         psA[:, h2 * 512:(h2 + 1) * 512], ALU.add)
                    P.op("vector", "tensor_tensor",
                         x_mix[:, c, grp * 1024 + h2 * 512:
                               grp * 1024 + (h2 + 1) * 512],
                         xg[:, c, s2],
                         psA[:, 1024 + h2 * 512:1024 + (h2 + 1) * 512],
                         ALU.mult,
                         sem="dve" if (c == NCH - 1 and h2 == 1) else None)
        t_xmix = P.cnt["dve"]

        P.wait("tensor", "dve", t_xmix)
        P.wait("tensor", "dma", twp)
        for oc in range(NCH):
            P.wait("tensor", "dve", ev_cs)
            tm = None
            for nt in range(4):
                for dc in range(NCH):
                    tm = P.op("tensor", "matmul",
                              psB[:, nt * 512:(nt + 1) * 512],
                              wpart[:, dc, oc * 128:(oc + 1) * 128],
                              x_mix[:, dc, nt * 512:(nt + 1) * 512],
                              start=(dc == 0), stop=(dc == NCH - 1),
                              sem="pe" if (nt == 3 and dc == NCH - 1) else None)
            P.wait("vector", "pe", tm)
            P.op("vector", "scalar_tensor_tensor", dump[:], psB[:], 0.0,
                 x_mix[:, oc, :], ALU.add, ALU.mult,
                 accum_out=uc_col[:, oc:oc + 1])
            ev_cs = P.op("vector", "reduce_sum", cs_col[:, oc:oc + 1], psB[:],
                         AX.X, sem="dve")
        tv = P.op("vector", "tensor_tensor", pp1[:], b1_pp[:], uc_col[:],
                  ALU.mult, sem="dve")
        P.wait("vector", "dve", tv)
        P.wait("scalar", "dve", tv)
        if "dbg_uc" in io and s == 0:
            P.wait("sync", "dve", tv)
            P.dma("sync", io["dbg_uc"], uc_col[:], sem="st")
            P.dma("sync", io["dbg_cs"], cs_col[:], sem="st")
            P.dma("sync", io["dbg_bd1"], bd_pp[:], sem="st")

        P.wait("tensor", "dma", twc)
        ev_a = ("dve", t_xmix)
        ev_b = ("dve", ev_cs)
        for oc in range(NCH):
            P.wait("tensor", ev_a[0], ev_a[1])
            tm = None
            for nt in range(4):
                for dc in range(NCH):
                    tm = P.op("tensor", "matmul",
                              psA[:, nt * 512:(nt + 1) * 512],
                              wcomb[:, dc, oc * 128:(oc + 1) * 128],
                              x_mix[:, dc, nt * 512:(nt + 1) * 512],
                              start=(dc == 0), stop=(dc == NCH - 1),
                              sem="pe" if (nt == 3 and dc == NCH - 1) else None)
            P.wait("scalar", "pe", tm)
            P.wait("scalar", "dve", last_y)
            for nt in range(4):
                ta = P.op("scalar", "add",
                          scr2[:, nt * 512:(nt + 1) * 512],
                          psA[:, nt * 512:(nt + 1) * 512],
                          bd_pp[:, oc:oc + 1], sem="act")
            ev_a = ("act", ta)
            if "dbg_uc" in io and s == 0 and oc == 0:
                P.wait("sync", "act", ta)
                tst_ = P.dma("sync", io["dbg_pre"], scr2[:], sem="st")
                P.wait("vector", "st", tst_)
                P.wait("scalar", "st", tst_)
            P.wait("vector", "act", ta)
            tabs_ = P.op("vector", "scalar_tensor_tensor", scr1[:], scr2[:],
                         -1.0, scr2[:], ALU.mult, ALU.max, sem="dve")
            if "dbg_uc" in io and s == 0 and oc == 0:
                P.wait("sync", "dve", tabs_)
                tst_ = P.dma("sync", io["dbg_abs"], scr1[:], sem="st")
                P.wait("vector", "st", tst_)
                P.wait("scalar", "st", tst_)
            P.wait("scalar", "dve", P.cnt["dve"])
            P.op("scalar", "activation", scr1[:], scr1[:], AF.Exp, scale=-1.0)
            ta = P.op("scalar", "activation", scr1[:], scr1[:], AF.Ln,
                      bias=1.0, sem="act")
            if "dbg_uc" in io and s == 0 and oc == 0:
                P.wait("sync", "act", ta)
                tst_ = P.dma("sync", io["dbg_ln"], scr1[:], sem="st")
                P.wait("vector", "st", tst_)
                P.wait("scalar", "st", tst_)
            P.wait("vector", "act", ta)
            tdd = P.op("vector", "scalar_tensor_tensor", dlt[:], scr2[:], 0.0,
                       scr1[:], ALU.max, ALU.add,
                       sem="dve" if ("dbg_uc" in io and s == 0 and oc == 0) else None)
            if "dbg_uc" in io and s == 0 and oc == 0:
                P.wait("sync", "dve", tdd)
                tst_ = P.dma("sync", io["dbg_dlt"], dlt[:], sem="st")
                P.wait("vector", "st", tst_)
                P.wait("scalar", "st", tst_)
            tvs = P.op("vector", "tensor_scalar", scr1[:], dlt[:],
                       b1_pp[:, oc:oc + 1], None, ALU.mult, sem="dve")
            P.wait("vector", "dve", tvs)
            P.op("vector", "tensor_tensor_scan", scr1[:], scr1[:], scr1[:],
                 0.0, ALU.add, ALU.bypass)
            tv = P.op("vector", "tensor_scalar", scr1[:], scr1[:],
                      cs_col[:, oc:oc + 1], None, ALU.mult, sem="dve")
            if "dbg_uc" in io and s == 0 and oc == 0:
                P.wait("sync", "dve", tv)
                tst_ = P.dma("sync", io["dbg_scan"], scr1[:], sem="st")
                P.wait("vector", "st", tst_)
                P.wait("scalar", "st", tst_)
            P.wait("scalar", "dve", tv)
            ta = P.op("scalar", "activation", dlt[:], dlt[:], AF.Exp,
                      scale=a_pp[:, oc:oc + 1], sem="act")
            P.wait("vector", "act", ta)
            tv = P.op("vector", "scalar_tensor_tensor", scr1[:], dlt[:],
                      pp1[:, oc:oc + 1], scr1[:], ALU.mult, ALU.add, sem="dve")
            last_y = tv
            P.wait("tensor", ev_b[0], ev_b[1])
            tm = None
            for nt in range(4):
                for dc in range(NCH):
                    tm = P.op("tensor", "matmul",
                              psB[:, nt * 512:(nt + 1) * 512],
                              wpart[:, dc, 1024 + oc * 128:
                                    1024 + (oc + 1) * 128],
                              x_mix[:, dc, nt * 512:(nt + 1) * 512],
                              start=(dc == 0), stop=(dc == NCH - 1),
                              sem="pe" if (nt == 3 and dc == NCH - 1) else None)
            P.wait("vector", "pe", tm)
            P.wait("vector", "dve", tv)
            tv = P.op("vector", "scalar_tensor_tensor", scr1[:], psB[:],
                      d_pp[:, 0:1], scr1[:], ALU.mult, ALU.add, sem="dve")
            ev_b = ("dve", tv)
            P.wait("sync", "st", t_store)
            txc = P.dma("sync", xc[:], x_in[s, oc, :, :])
            P.wait("vector", "dma", txc)
            tv = P.op("vector", "tensor_tensor", xc[:], xc[:], scr1[:],
                      ALU.add, sem="dve")
            P.wait("sync", "dve", tv)
            t_store = P.dma("sync", x_out[s, oc, :, :], xc[:], sem="st")

    if "dbg_uc" in io:
        P.dma("sync", io["dbg_bd2"], bd_pp[:], sem="st")
    P.wait("sync", "st", P.cnt["st"])
    st.close()



def emit_attn(P, com, io):
    nc = P.nc
    P.new_phase()
    st = ExitStack()
    x_in = io["x_fm_in"]
    x_out = io["x_fm_out"]
    NH = 16
    DH = 64

    psA = st.enter_context(_psm(nc, "a_psA", [128, 2048], F32))
    psB = st.enter_context(_psm(nc, "a_psB", [128, 2048], F32))

    wq = st.enter_context(_sb(nc, "a_wq", [128, 8, 1024], BF16))
    wx = st.enter_context(_sb(nc, "a_wx", [128, 8, 1024], BF16))
    kf = st.enter_context(_sb(nc, "a_kf", [128, 8, 2048], BF16))
    vt = st.enter_context(_sb(nc, "a_vt", [128, 16, 1024], BF16))
    xat = st.enter_context(_sb(nc, "a_xat", [128, 8, 2048], BF16))
    qg = st.enter_context(_sb(nc, "a_qg", [128, 8, 512], BF16))
    pt = st.enter_context(_sb(nc, "a_pt", [128, 4, 512], BF16))
    att = st.enter_context(_sb(nc, "a_att", [128, 8, 512], BF16))
    msk = st.enter_context(_sb(nc, "a_msk", [128, 4, 512], BF16))
    lb = st.enter_context(_sb(nc, "a_lb", [128, 2, 512], F32))
    lrow = st.enter_context(_sb(nc, "a_lrow", [1, 512], F32))
    xg = st.enter_context(_sb(nc, "a_xg", [128, 8, 512], F32))
    xb = st.enter_context(_sb(nc, "a_xb", [128, 8, 512], BF16))
    xsq = st.enter_context(_sb(nc, "a_xsq", [128, 8, 512], BF16))
    rows = st.enter_context(_sb(nc, "a_rows", [1, 4, 512], F32))
    xc = st.enter_context(_sb(nc, "a_xc", [128, 512], F32))

    mi = st.enter_context(_sb(nc, "a_mi", [128, 512], F32))
    for j in range(4):
        P.op("gpsimd", "iota", mi[:], [[1, 512]], base=-128 * j,
             channel_multiplier=-1, allow_small_or_imprecise_dtypes=True,
             sem="pool")
        P.wait("vector", "pool", P.cnt["pool"])
        P.op("vector", "tensor_scalar", msk[:, j, :], mi[:], 0.0, -1e30,
             ALU.is_lt, op1=ALU.mult, sem="dve")
        P.wait("gpsimd", "dve", P.cnt["dve"])
    wsrc = (io["wqkvo_c"] if "wqkvo_c" in io else
            None)

    def _wload(buf, i, nm):
        if wsrc is not None:
            return P.dma("sync", buf,
                         wsrc[i].rearrange("(c p) o -> p c o", p=128),
                         sem="dma")
        return P.dma("gpsimd", buf,
                     io[nm].rearrange("(c p) o -> p c o", p=128), sem="dmag")

    wq_sem = "dma" if wsrc is not None else "dmag"
    t_wq = _wload(wq[:], 0, "W_Q")
    t_wx = _wload(wx[:], 1, "W_K")

    t_store = 0
    SC = 1.0 / (DH ** 0.5)
    sguard = [0, 0, 0, 0]
    ptg = [0, 0, 0, 0]
    lg_guard = ("dve", 0)
    lbg = ("dve", 0)
    att_guard = [("dve", 0), ("dve", 0)]

    for s in range(NSEQ):
        P.wait("tensor", "act", P.cnt.get("act", P.inc("act", 0)))
        P.wait("tensor", "dve", P.cnt["dve"])

        for grp in range(4):
            sl = slice(grp * 512, (grp + 1) * 512)
            P.wait("sync", "dve", P.cnt["dve"])
            P.wait("sync", "st", t_store)
            tx = P.dma("sync", xg[:], x_in[s, :, :, sl]
                       .rearrange("c p t -> p c t"))
            tpb, _ = emit_ln_stats(
                P, com, xg[:, :, 0:512], 512, xb, xsq, rows,
                [psA[0:1, 0:512], psA[0:1, 512:1024]],
                [psA[:, 0:512]], [psA[:, 512:1024]],
                wait_act=("dma", tx))
            P.wait("vector", "pe", tpb)
            for c in range(NCH):
                P.op("vector", "tensor_tensor", xg[:, c, :], xg[:, c, :],
                     psA[:, 0:512], ALU.add)
                P.op("vector", "tensor_tensor", xat[:, c, sl], xg[:, c, :],
                     psA[:, 512:1024], ALU.mult,
                     sem="dve" if c == NCH - 1 else None)
        t_xat = P.cnt["dve"]

        P.wait("tensor", "dve", t_xat)
        P.wait("tensor", wq_sem, t_wx)
        ev = [("dve", 0)] * 4
        for oc in range(NCH):
            for nt in range(4):
                k = nt
                pb = psB[:, k * 512:(k + 1) * 512]
                P.wait("tensor", ev[k][0], ev[k][1])
                tm = None
                for dc in range(NCH):
                    tm = P.op("tensor", "matmul", pb,
                              wx[:, dc, oc * 128:(oc + 1) * 128],
                              xat[:, dc, nt * 512:(nt + 1) * 512],
                              start=(dc == 0), stop=(dc == NCH - 1),
                              sem="pe" if dc == NCH - 1 else None)
                P.wait("scalar", "pe", tm)
                ta = P.op("scalar", "activation",
                          kf[:, oc, nt * 512:(nt + 1) * 512], pb, AF.Copy,
                          scale=SC, sem="act")
                ev[k] = ("act", ta)
        P.wait("sync" if wsrc is not None else "gpsimd", "act",
               P.cnt["act"])
        t_wx = _wload(wx[:], 2, "W_V")
        P.wait("tensor", wq_sem, t_wx)
        for kt in range(16):
            for nt2 in range(2):
                k = (kt * 2 + nt2) % 4
                pb = psB[:, k * 512:(k + 1) * 512]
                P.wait("tensor", ev[k][0], ev[k][1])
                tm = None
                for dc in range(NCH):
                    tm = P.op("tensor", "matmul", pb,
                              xat[:, dc, kt * 128:(kt + 1) * 128],
                              wx[:, dc, nt2 * 512:(nt2 + 1) * 512],
                              start=(dc == 0), stop=(dc == NCH - 1),
                              sem="pe" if dc == NCH - 1 else None)
                P.wait("scalar", "pe", tm)
                ta = P.op("scalar", "activation",
                          vt[:, kt, nt2 * 512:(nt2 + 1) * 512], pb, AF.Copy,
                          sem="act")
                ev[k] = ("act", ta)
        P.wait("sync" if wsrc is not None else "gpsimd", "act",
               P.cnt["act"])
        t_wx = _wload(wx[:], 3, "W_o")

        for g in range(4):
            gsl = slice(g * 512, (g + 1) * 512)
            P.wait("tensor", wq_sem, t_wq)
            P.wait("tensor", "act", P.cnt["act"])
            P.wait("tensor", "dve", P.cnt["dve"])
            for oc in range(NCH):
                k = oc % 2
                pb = psA[:, k * 512:(k + 1) * 512]
                P.wait("tensor", ev[k][0], ev[k][1])
                tm = None
                for dc in range(NCH):
                    tm = P.op("tensor", "matmul", pb,
                              wq[:, dc, oc * 128:(oc + 1) * 128],
                              xat[:, dc, gsl],
                              start=(dc == 0), stop=(dc == NCH - 1),
                              sem="pe" if dc == NCH - 1 else None)
                P.wait("scalar", "pe", tm)
                ta = P.op("scalar", "activation", qg[:, oc, :], pb, AF.Copy,
                          sem="act")
                ev[k] = ("act", ta)
            t_qg = ("act", P.cnt["act"])

            nkt = 4 * g + 4
            for h in range(NH):
                base = (h % 2) * 64
                hc = h // 2
                ob = h % 2
                P.wait("tensor", t_qg[0], t_qg[1])
                pv_t = {}
                exp_t = {}
                for step in range(nkt + 2):
                    kt = step
                    if kt < nkt:
                        kb = kt % 4
                        pb = psA[:, kb * 512:(kb + 1) * 512]
                        P.wait("tensor", "act", sguard[kb])
                        tm = P.op("tensor", "matmul", pb,
                                  kf[base:base + 64, hc,
                                     kt * 128:(kt + 1) * 128],
                                  qg[base:base + 64, hc, :],
                                  start=True, stop=True, sem="pe")
                        if kt >= 4 * g:
                            jj = kt - 4 * g
                            P.wait("vector", "pe", tm)
                            tm2 = P.op("vector", "tensor_tensor", pb, pb,
                                       msk[:, jj, :], ALU.add, sem="dve")
                            P.wait("scalar", "dve", tm2)
                        else:
                            P.wait("scalar", "pe", tm)
                        P.wait("scalar", "pe", ptg[kb])
                        exp_t[kt] = P.op("scalar", "activation",
                                         pt[:, kb, :], pb, AF.Exp, sem="act")
                        sguard[kb] = exp_t[kt]
                    j = step - 2
                    if 0 <= j < nkt:
                        jb = j % 4
                        P.wait("tensor", "act", exp_t[j])
                        if j == 0:
                            P.wait("tensor", lg_guard[0], lg_guard[1])
                            P.wait("tensor", att_guard[ob][0],
                                   att_guard[ob][1])
                        P.op("tensor", "matmul", psB[0:1, 1024:1536],
                             com.ones_b[:], pt[:, jb, :],
                             start=(j == 0), stop=(j == nkt - 1))
                        pv_t[j] = P.op("tensor", "matmul",
                                       psB[base:base + 64,
                                           ob * 512:ob * 512 + 512],
                                       vt[:, j, h * 64:h * 64 + 64],
                                       pt[:, jb, :],
                                       start=(j == 0), stop=(j == nkt - 1),
                                       tile_position=(0, base), sem="pe")
                        ptg[jb] = pv_t[j]
                P.wait("vector", "pe", pv_t[nkt - 1])
                tv = P.op("vector", "tensor_copy", lrow[:],
                          psB[0:1, 1024:1536], sem="dve")
                lg_guard = ("dve", tv)
                P.wait("scalar", "dve", tv)
                P.op("scalar", "activation", lrow[:], lrow[:], AF.Ln)
                ta = P.op("scalar", "activation", lrow[:], lrow[:], AF.Exp,
                          scale=-1.0, sem="act")
                P.wait("tensor", "act", ta)
                P.wait("tensor", lbg[0], lbg[1])
                tm = P.op("tensor", "matmul", psB[:, 1536:2048], com.ones_f[:],
                          lrow[:], start=True, stop=True, sem="pe")
                P.wait("vector", "pe", tm)
                tv = P.op("vector", "tensor_copy", lb[:, h % 2, :],
                          psB[:, 1536:2048], sem="dve")
                lbg = ("dve", tv)
                P.wait("vector", "dve", tv)
                td = P.op("vector", "tensor_tensor",
                          att[base:base + 64, hc, :],
                          psB[base:base + 64, ob * 512:ob * 512 + 512],
                          lb[base:base + 64, h % 2, :],
                          ALU.mult, sem="dve")
                att_guard[ob] = ("dve", td)

            P.wait("tensor", att_guard[0][0], att_guard[0][1])
            P.wait("tensor", att_guard[1][0], att_guard[1][1])
            P.wait("tensor", "act", P.cnt["act"])
            P.wait("tensor", "dve", P.cnt["dve"])
            P.wait("tensor", wq_sem, t_wx)
            for oc in range(NCH):
                k = oc % 2
                pb = psA[:, k * 512:(k + 1) * 512]
                P.wait("tensor", ev[k][0], ev[k][1])
                tm = None
                for dc in range(NCH):
                    tm = P.op("tensor", "matmul", pb,
                              wx[:, dc, oc * 128:(oc + 1) * 128],
                              att[:, dc, :],
                              start=(dc == 0), stop=(dc == NCH - 1),
                              sem="pe" if dc == NCH - 1 else None)
                P.wait("sync", "st", t_store)
                txc = P.dma("sync", xc[:], x_in[s, oc, :, gsl])
                P.wait("vector", "dma", txc)
                P.wait("vector", "pe", tm)
                tv = P.op("vector", "tensor_tensor", xc[:], xc[:], pb,
                          ALU.add, sem="dve")
                ev[k] = ("dve", tv)
                P.wait("sync", "dve", tv)
                t_store = P.dma("sync", x_out[s, oc, :, gsl], xc[:], sem="st")
    P.wait("sync", "st", P.cnt["st"])
    st.close()



def emit_ppT(P, com, st, src2d, R, Cc, name, ps, tmp, guard=0):
    nc = P.nc
    dest = st.enter_context(_sb(nc, name, [128, R * Cc], F32))
    P.wait("sync", "pe", guard)
    td = P.dma("sync", tmp[0:R, 0:Cc * 128], src2d)
    P.wait("tensor", "dma", td)
    P.wait("tensor", "dve", com.ready)
    t = 0
    for cc in range(Cc):
        pb = ps[cc % 2]
        if cc >= 2:
            P.wait("tensor", "dve", t)
        tp = P.op("tensor", "transpose", pb, tmp[:, cc * 128:(cc + 1) * 128],
                  com.ident_f[0:R, 0:R], sem="pe")
        P.wait("vector", "pe", tp)
        t = P.op("vector", "tensor_copy",
                 bass.AP(dest.ap().tensor, cc, [[R * Cc, 128], [Cc, R]]),
                 pb, sem="dve")
    return dest, t


def emit_moe(P, com, io):
    nc = P.nc
    P.new_phase()
    st = ExitStack()
    x_in = io["x_fm_in"]
    x_out = io["x_fm_out"]
    if "W1c" in io:
        W1h, W2h, W3h = io["W1c"], io["W2c"], io["W3c"]
    else:
        W1h, W2h, W3h = io["W1"], io["W2"], io["W3"]
    cw_scr = io["cw_scr"]

    psA = st.enter_context(_psm(nc, "e_psA", [128, 2048], F32))
    psB = st.enter_context(_psm(nc, "e_psB", [128, 2048], F32))

    xln = st.enter_context(_sb(nc, "e_xln", [128, 8, 1024], BF16))
    yy = st.enter_context(_sb(nc, "e_y", [128, 8, 1024], F32))
    hh = st.enter_context(_sb(nc, "e_h", [128, 32, 1024], BF16))
    w1q = st.enter_context(_sb(nc, "e_w1q", [128, 2, 8, 512], BF16))
    w2q = st.enter_context(_sb(nc, "e_w2q", [128, 2, 8, 512], BF16))
    xg = st.enter_context(_sb(nc, "e_xg", [128, 8, 512], F32))
    rows = st.enter_context(_sb(nc, "e_rows", [1, 4, 512], F32))
    cwb = st.enter_context(_sb(nc, "e_cwb", [128, 1024], F32))
    s1 = st.enter_context(_sb(nc, "e_s1", [128, 2, 512], BF16))
    tt1 = st.enter_context(_sb(nc, "e_tt1", [128, 512], F32))
    wr_sb = st.enter_context(_sb(nc, "e_wr", [128, 8, 8], BF16))
    brb = st.enter_context(_sb(nc, "e_brb", [128, 8], F32))
    iota8 = st.enter_context(_sb(nc, "e_io8", [128, 8], F32))
    lg = st.enter_context(_sb(nc, "e_lg", [128, 32, 8], F32))
    mxall = st.enter_context(_sb(nc, "e_mx", [128, 32, 8], F32))
    miall = st.enter_context(_sb(nc, "e_mi", [128, 32, 8], U32))
    i1f = st.enter_context(_sb(nc, "e_i1f", [128, 2, 32], F32))
    wgt = st.enter_context(_sb(nc, "e_wgt", [128, 5, 32], F32))
    oh = st.enter_context(_sb(nc, "e_oh", [128, 8], F32))
    cw = st.enter_context(_sb(nc, "e_cw", [128, 32, 8], F32))
    cwTt = st.enter_context(_sb(nc, "e_cwTt", [8, 2, 128], F32))

    xb = hh[:, 0:8, 0:512]
    xsq = hh[:, 8:16, 0:512]

    _pp = [psA[:, 0:8], psA[:, 512:520]]
    _tmp = st.enter_context(_sb(nc, "e_ppt", [8, 4096], F32))
    b1_pp, tp1 = emit_ppT(P, com, st, io["b1"], 8, 32, "e_b1", _pp, _tmp)
    b2_pp, tp2 = emit_ppT(P, com, st, io["b2"], 8, 32, "e_b2", _pp, _tmp,
                          guard=P.cnt["pe"])
    b3_pp, tp3 = emit_ppT(P, com, st, io["b3"], 8, 8, "e_b3", _pp, _tmp,
                          guard=P.cnt["pe"])
    P.op("gpsimd", "iota", iota8[:], [[1, 8]], channel_multiplier=0,
         allow_small_or_imprecise_dtypes=True, sem="pool")
    t_io8 = P.cnt["pool"]
    if "wr_c" in io:
        tw = P.dma("sync", wr_sb[:],
                   io["wr_c"].rearrange("(c p) e -> p c e", p=128), sem="dma")
        wr_sem = "dma"
    else:
        tw = P.dma("gpsimd", wr_sb[:],
                   io["Wr"].rearrange("(c p) e -> p c e", p=128), sem="dmag")
        wr_sem = "dmag"
    tb = P.dma("sync", brb[:],
               bass.AP(io["br"].tensor, 0, [[0, 128], [1, 8]]),
               allow_slow_non_contiguous=True)
    pp_ready = max(tp1, tp2, tp3)

    t_store = 0
    ev_ps = ("dve", 0)
    for g4 in range(4):
        for sub in range(2):
            t0g = g4 * 1024 + sub * 512
            s_, lo = t0g // SEQ, t0g % SEQ
            P.wait("sync", "dve", P.cnt["dve"])
            tx = P.dma("sync", xg[:], x_in[s_, :, :, lo:lo + 512]
                       .rearrange("c p t -> p c t"))
            tpb, _ = emit_ln_stats(
                P, com, xg[:, :, 0:512], 512, xb, xsq, rows,
                [psA[0:1, 0:512], psA[0:1, 512:1024]],
                [psA[:, 0:512]], [psA[:, 512:1024]],
                wait_act=("dma", tx))
            P.wait("vector", "pe", tpb)
            for c in range(NCH):
                P.op("vector", "tensor_tensor", xg[:, c, :], xg[:, c, :],
                     psA[:, 0:512], ALU.add)
                P.op("vector", "tensor_tensor",
                     xln[:, c, sub * 512:(sub + 1) * 512], xg[:, c, :],
                     psA[:, 512:1024], ALU.mult,
                     sem="dve" if c == NCH - 1 else None)
        t_xln = P.cnt["dve"]
        P.wait("tensor", "dve", t_xln)
        P.wait("tensor", wr_sem, tw)
        for tt2 in range(8):
            tt = g4 * 8 + tt2
            k = tt2 % 2
            pb = psA[:, 1024 + k * 8:1024 + (k + 1) * 8]
            P.wait("tensor", ev_ps[0], ev_ps[1])
            tm = None
            for dc in range(NCH):
                tm = P.op("tensor", "matmul", pb,
                          xln[:, dc, tt2 * 128:(tt2 + 1) * 128],
                          wr_sb[:, dc, :], start=(dc == 0),
                          stop=(dc == NCH - 1),
                          sem="pe" if dc == NCH - 1 else None)
            P.wait("vector", "pe", tm)
            tv = P.op("vector", "tensor_copy", lg[:, tt, :], pb, sem="dve")
            ev_ps = ("dve", tv)

    P.wait("vector", "dma", tb)
    brb_b = bass.AP(brb.ap().tensor, 0, [[8, 128], [0, 32], [1, 8]])
    tv = P.op("vector", "tensor_tensor", lg[:], lg[:], brb_b, ALU.add,
              sem="dve")
    for tt in range(32):
        P.wait("vector", "dve", P.cnt["dve"])
        P.op("vector", "max", mxall[:, tt, :], lg[:, tt, :], sem="dve")
        P.wait("vector", "dve", P.cnt["dve"])
        P.op("vector", "max_index", miall[:, tt, :], mxall[:, tt, :],
             lg[:, tt, :], sem="dve")
    m1 = bass.AP(mxall.ap().tensor, 0, [[256, 128], [8, 32]])
    m2 = bass.AP(mxall.ap().tensor, 1, [[256, 128], [8, 32]])
    i1u = bass.AP(miall.ap().tensor, 0, [[256, 128], [8, 32]])
    i2u = bass.AP(miall.ap().tensor, 1, [[256, 128], [8, 32]])
    P.wait("vector", "dve", P.cnt["dve"])
    P.op("vector", "tensor_tensor", wgt[:, 0, :], m2, m1, ALU.subtract)
    P.op("vector", "tensor_copy", i1f[:, 0, :], i1u)
    tv = P.op("vector", "tensor_copy", i1f[:, 1, :], i2u, sem="dve")
    P.wait("scalar", "dve", tv)
    ta = P.op("scalar", "activation", wgt[:, 0, :], wgt[:, 0, :], AF.Exp,
              sem="act")
    P.wait("vector", "act", ta)
    tv = P.op("vector", "tensor_scalar", wgt[:, 1, :], wgt[:, 0, :], 1.0,
              None, ALU.add, sem="dve")
    P.wait("vector", "dve", tv)
    tv = P.op("vector", "reciprocal", wgt[:, 4, :], wgt[:, 1, :], sem="dve")
    P.wait("vector", "dve", tv)
    P.op("vector", "tensor_scalar", wgt[:, 2, :], wgt[:, 4, :], 0.5, None,
         ALU.mult)
    tv = P.op("vector", "scalar_tensor_tensor", wgt[:, 3, :], wgt[:, 0, :],
              0.5, wgt[:, 4, :], ALU.mult, ALU.mult, sem="dve")
    P.wait("vector", "dve", tv)
    P.wait("vector", "pool", t_io8)
    for tt in range(32):
        P.op("vector", "tensor_scalar", oh[:], iota8[:],
             i1f[:, 0, tt:tt + 1], None, ALU.is_equal)
        P.op("vector", "tensor_scalar", cw[:, tt, :], oh[:],
             wgt[:, 2, tt:tt + 1], None, ALU.mult)
        P.op("vector", "tensor_scalar", oh[:], iota8[:],
             i1f[:, 1, tt:tt + 1], None, ALU.is_equal)
        P.op("vector", "scalar_tensor_tensor", cw[:, tt, :], oh[:],
             wgt[:, 3, tt:tt + 1], cw[:, tt, :], ALU.mult, ALU.add,
             sem="dve" if tt == 31 else None)
    t_cw = P.cnt["dve"]
    P.wait("tensor", "dve", t_cw)
    t = 0
    t_cws = 0
    cwv = cw_scr.rearrange("e (tt t) -> e tt t", t=128)
    for tt in range(32):
        pb = psA[:, 1024 + (tt % 2) * 128:1024 + (tt % 2) * 128 + 128]
        if tt >= 2:
            P.wait("tensor", "dve", t)
        tp = P.op("tensor", "transpose", pb[0:8, 0:128], cw[:, tt, :],
                  com.ident_f[:], sem="pe")
        P.wait("vector", "pe", tp)
        if tt >= 2:
            P.wait("vector", "st", t_cws)
        t = P.op("vector", "tensor_copy", cwTt[:, tt % 2, :], pb[0:8, 0:128],
                 sem="dve")
        P.wait("sync", "dve", t)
        t_cws = P.dma("sync", cwv[:, tt, :], cwTt[:, tt % 2, :], sem="st")
    if "dbg_cw" in io:
        P.wait("sync", "dve", P.cnt["dve"])
        P.wait("sync", "st", t_cws)
        P.dma("sync", io["dbg_cw"], cw[:], sem="st")
        P.dma("sync", io["dbg_lg"], lg[:], sem="st")
        P.dma("sync", io["dbg_mx"], mxall[:], sem="st")
        P.dma("sync", io["dbg_mi"], miall[:], sem="st")
        P.dma("sync", io["dbg_wgt"], wgt[:, 0:4, :], sem="st")
        tdb = P.dma("sync", io["dbg_i1f"], i1f[:], sem="st")
        P.wait("vector", "st", tdb)
        P.wait("scalar", "st", tdb)
        P.wait("tensor", "st", tdb)

    w1_free = [0, 0]
    w23_free = [0, 0]
    hg = [("dve", 0)] * 4
    yg = [("dve", 0)] * 8
    t_cwb = 0
    for g in range(4):
        P.wait("vector", "dve", P.cnt["dve"])
        tv = P.op("vector", "memset", yy[:], 0.0, sem="dve")
        t_y0 = tv
        P.wait("scalar", "pe", P.cnt["pe"])
        for sub in range(2):
            t0g = g * 1024 + sub * 512
            s_, lo = t0g // SEQ, t0g % SEQ
            P.wait("sync", "dve", P.cnt["dve"])
            tx = P.dma("sync", xg[:], x_in[s_, :, :, lo:lo + 512]
                       .rearrange("c p t -> p c t"))
            tpb, _ = emit_ln_stats(
                P, com, xg[:, :, 0:512], 512, xb, xsq, rows,
                [psA[0:1, 0:512], psA[0:1, 512:1024]],
                [psA[:, 0:512]], [psA[:, 512:1024]],
                wait_act=("dma", tx))
            P.wait("vector", "pe", tpb)
            for c in range(NCH):
                P.op("vector", "tensor_tensor", xg[:, c, :], xg[:, c, :],
                     psA[:, 0:512], ALU.add)
                P.op("vector", "tensor_tensor",
                     xln[:, c, sub * 512:(sub + 1) * 512], xg[:, c, :],
                     psA[:, 512:1024], ALU.mult,
                     sem="dve" if c == NCH - 1 else None)
        t_xln = P.cnt["dve"]
        hg = [("dve", t_xln)] * 4

        for e in range(NEXP):
            P.wait("sync", "st", t_cws)
            P.wait("sync", "dve", P.cnt["dve"])
            t_cwb = P.dma("sync", cwb[:],
                          bass.AP(cw_scr.tensor, e * 4096 + g * 1024,
                                  [[0, 128], [1, 1024]]),
                          allow_slow_non_contiguous=True)
            slab_t = [0, 0]

            def issue_a(q8, e=e):
                slot = q8 % 2
                P.wait("gpsimd", "pe", w1_free[slot])
                P.dma("gpsimd", w1q[:, slot],
                      W1h[e][:, q8 * 512:(q8 + 1) * 512]
                      .rearrange("(c p) f -> p c f", p=128), sem="dmag")
                P.wait("gpsimd", "pe", w23_free[slot])
                slab_t[slot] = P.dma(
                    "gpsimd", w2q[:, slot],
                    W2h[e][:, q8 * 512:(q8 + 1) * 512]
                    .rearrange("(c p) f -> p c f", p=128), sem="dmag")

            issue_a(0)
            issue_a(1)
            P.wait("tensor", "dve", t_xln)
            for q8 in range(8):
                slot = q8 % 2
                P.wait("tensor", "dmag", slab_t[slot])
                for cq in range(4):
                    c = q8 * 4 + cq
                    for nh in range(2):
                        k = (cq * 2 + nh) % 2
                        p1 = psA[:, k * 512:(k + 1) * 512]
                        p2 = psA[:, 1024 + k * 512:1024 + (k + 1) * 512]
                        P.wait("tensor", hg[k][0], hg[k][1])
                        P.wait("tensor", hg[2 + k][0], hg[2 + k][1])
                        for dc in range(NCH):
                            P.op("tensor", "matmul", p1,
                                 w1q[:, slot, dc, cq * 128:(cq + 1) * 128],
                                 xln[:, dc, nh * 512:(nh + 1) * 512],
                                 start=(dc == 0), stop=(dc == NCH - 1),
                                 sem="pe" if dc == NCH - 1 else None)
                        tm1 = P.cnt["pe"]
                        for dc in range(NCH):
                            P.op("tensor", "matmul", p2,
                                 w2q[:, slot, dc, cq * 128:(cq + 1) * 128],
                                 xln[:, dc, nh * 512:(nh + 1) * 512],
                                 start=(dc == 0), stop=(dc == NCH - 1),
                                 sem="pe" if dc == NCH - 1 else None)
                        tm2 = P.cnt["pe"]
                        P.wait("scalar", "pe", tm1)
                        col = e * 32 + c
                        ta = P.op("scalar", "activation", s1[:, k, :], p1,
                                  AF.Silu, bias=b1_pp[:, col:col + 1],
                                  sem="act")
                        hg[k] = ("act", ta)
                        P.wait("vector", "pe", tm2)
                        P.wait("vector", "act", ta)
                        tv = P.op("vector", "scalar_tensor_tensor",
                                  hh[:, c, nh * 512:(nh + 1) * 512], p2,
                                  b2_pp[:, col:col + 1], s1[:, k, :],
                                  ALU.add, ALU.mult, sem="dve")
                        hg[2 + k] = ("dve", tv)
                w1_free[slot] = w23_free[slot] = P.cnt["pe"]
                if q8 + 2 < 8:
                    issue_a(q8 + 2)
            t_h = ("dve", P.cnt["dve"])

            w3st = [0, 0]

            def w3_lhsT(slot, rc, dc):
                off = slot * 4096 + rc * 1024 + dc * 128
                return bass.AP(w2q.ap().tensor, off, [[8192, 128], [1, 128]])

            def w3_dst(slot):
                return bass.AP(w2q.ap().tensor, slot * 4096,
                               [[8192, 128], [1024, 4], [1, 1024]])

            def issue_w3(step, e=e):
                q8 = step % 8
                slot = step % 2
                P.wait("gpsimd", "pe", w23_free[slot])
                w3st[slot] = P.dma(
                    "gpsimd", w3_dst(slot),
                    W3h[e][q8 * 512:(q8 + 1) * 512, :]
                    .rearrange("(c p) f -> p c f", p=128), sem="dmag")

            issue_w3(0)
            issue_w3(1)
            for step in range(16):
                th, q8 = step // 8, step % 8
                slot = step % 2
                P.wait("tensor", "dmag", w3st[slot])
                if step == 0:
                    P.wait("tensor", t_h[0], t_h[1])
                    for dc in range(8):
                        P.wait("tensor", yg[dc][0], yg[dc][1])
                tm = None
                for rc in range(4):
                    c = q8 * 4 + rc
                    for dc in range(8):
                        reg = (psA if dc < 4 else psB)[
                            :, (dc % 4) * 512:(dc % 4 + 1) * 512]
                        tm = P.op("tensor", "matmul", reg,
                                  w3_lhsT(slot, rc, dc),
                                  hh[:, c, th * 512:(th + 1) * 512],
                                  start=(c == 0), stop=(c == 31),
                                  sem="pe" if (rc == 3 and dc == 7) else None)
                w23_free[slot] = tm
                if step + 2 < 16:
                    issue_w3(step + 2)
                if q8 == 7:
                    P.wait("vector", "pe", tm)
                    P.wait("vector", "dma", t_cwb)
                    P.wait("vector", "dve", t_y0)
                    for dc in range(8):
                        reg = (psA if dc < 4 else psB)[
                            :, (dc % 4) * 512:(dc % 4 + 1) * 512]
                        col = e * 8 + dc
                        P.op("vector", "scalar_tensor_tensor", tt1[:], reg,
                             b3_pp[:, col:col + 1],
                             cwb[:, th * 512:(th + 1) * 512], ALU.add,
                             ALU.mult)
                        tv = P.op("vector", "tensor_tensor",
                                  yy[:, dc, th * 512:(th + 1) * 512],
                                  yy[:, dc, th * 512:(th + 1) * 512], tt1[:],
                                  ALU.add, sem="dve")
                        yg[dc] = ("dve", tv)
                    if th == 0:
                        for dc in range(8):
                            P.wait("tensor", yg[dc][0], yg[dc][1])
            hg = [("dve", P.cnt["dve"])] * 4

        for sub in range(2):
            t0g = g * 1024 + sub * 512
            s_, lo = t0g // SEQ, t0g % SEQ
            P.wait("sync", "dve", P.cnt["dve"])
            P.wait("sync", "st", t_store)
            tx = P.dma("sync", xg[:], x_in[s_, :, :, lo:lo + 512]
                       .rearrange("c p t -> p c t"))
            P.wait("vector", "dma", tx)
            tv = P.op("vector", "tensor_tensor", xg[:],
                      xg[:], yy[:, :, sub * 512:(sub + 1) * 512], ALU.add,
                      sem="dve")
            P.wait("sync", "dve", tv)
            t_store = P.dma("sync", x_out[s_, :, :, lo:lo + 512]
                            .rearrange("c p t -> p c t"), xg[:], sem="st")
    P.wait("sync", "st", P.cnt["st"])
    st.close()



def emit_output(P, com, io):
    nc = P.nc
    P.new_phase()
    st = ExitStack()
    x_in = io["x_fm_in"]
    out = io["out"]
    xi = st.enter_context(_sb(nc, "o_xi", [128, 8, 128], F32))
    xo = st.enter_context(_sb(nc, "o_xo", [128, 8, 128], F32))
    pst = [st.enter_context(_psm(nc, f"o_ps{i}", [128, 128], F32))
           for i in range(2)]
    t_st = 0
    for s in range(NSEQ):
        for tc in range(16):
            P.wait("sync", "dve", P.cnt.get("dve", 0))
            td = P.dma("sync", xi[:],
                       x_in[s, :, :, tc * 128:(tc + 1) * 128]
                       .rearrange("c p t -> p c t"))
            P.wait("tensor", "dma", td)
            P.wait("tensor", "dve", com.ready)
            for dc in range(NCH):
                pb = pst[dc % 2]
                tp = P.op("tensor", "transpose", pb[:], xi[:, dc, :],
                          com.ident_f[:], sem="pe")
                P.wait("vector", "pe", tp)
                if dc == 0:
                    P.wait("vector", "st", t_st)
                tv = P.op("vector", "tensor_copy", xo[:, dc, :], pb[:],
                          sem="dve")
                P.wait("tensor", "dve", tv)
            P.wait("sync", "dve", P.cnt["dve"])
            t_st = P.dma("sync",
                         out[s, tc * 128:(tc + 1) * 128, :]
                         .rearrange("t (c p) -> t c p", p=128),
                         xo[:], sem="st")
    P.wait("sync", "st", P.cnt["st"])
    st.close()



def _din(nc, name, shape, dt):
    return nc.dram_tensor(name, list(shape), dt, kind="ExternalInput").ap()


def _dout(nc, name, shape, dt):
    return nc.dram_tensor(name, list(shape), dt, kind="ExternalOutput").ap()


XFM_SHAPE = (NSEQ, 8, 128, 2048)


def build_prep():
    nc = bass.Bass("TRN2", target_bir_lowering=False)
    io = {
        "x_in": _din(nc, "x_in", (NSEQ, SEQ, D), F32),
        "q": _din(nc, "q", (6, D, D), I32),
        "scales": _din(nc, "scales", (6, D, 64), F32),
        "lora_a": _din(nc, "lora_a", (6, 32, D), F32),
        "lora_b": _din(nc, "lora_b", (6, D, 32), F32),
        "w_proj": _din(nc, "w_proj", (4 * D, D), F32),
        "w_delta": _din(nc, "w_delta", (D, D), F32),
        "x_fm": _dout(nc, "x_fm", XFM_SHAPE, F32),
        "wt_ffn0": _dout(nc, "wt_ffn0", (6, D, D), BF16),
        "laT": _dout(nc, "laT", (128, 6, 8, 32), BF16),
        "lbT": _dout(nc, "lbT", (32, 6, 8, 128), BF16),
        "wprojT": _dout(nc, "wprojT", (D, 4 * D), BF16),
        "wdeltaT": _dout(nc, "wdeltaT", (D, D), BF16),
        "wcombT": _dout(nc, "wcombT", (D, D), BF16),
    }
    P = Prog(nc)
    com = Commons(P)
    emit_prep(P, com, io)
    P.build()
    return nc


def build_ffn0(n_groups=4, n_blocks=3):
    nc = bass.Bass("TRN2", target_bir_lowering=False)
    io = {
        "x_fm_in": _din(nc, "x_fm_in", XFM_SHAPE, F32),
        "wt_ffn0": _din(nc, "wt_ffn0", (6, D, D), BF16),
        "laT": _din(nc, "laT", (128, 6, 8, 32), BF16),
        "lbT": _din(nc, "lbT", (32, 6, 8, 128), BF16),
        "biases": _din(nc, "biases", (6, D), F32),
        "ln_gamma": _din(nc, "ln_gamma", (3, D), F32),
        "ln_beta": _din(nc, "ln_beta", (3, D), F32),
        "x_fm_out": _dout(nc, "x_fm_out", XFM_SHAPE, F32),
    }
    P = Prog(nc)
    com = Commons(P)
    emit_ffn0(P, com, io, n_groups, n_blocks)
    P.build()
    return nc


def build_mixer():
    nc = bass.Bass("TRN2", target_bir_lowering=False)
    io = {
        "x_fm_in": _din(nc, "x_fm_in", XFM_SHAPE, F32),
        "wprojT": _din(nc, "wprojT", (D, 4 * D), BF16),
        "wcombT": _din(nc, "wcombT", (D, D), BF16),
        "A": _din(nc, "A", (1, 1, D), F32),
        "Bp": _din(nc, "Bp", (1, D, 1), F32),
        "D": _din(nc, "D", (1,), F32),
        "b_delta": _din(nc, "b_delta", (D,), F32),
        "x_fm_out": _dout(nc, "x_fm_out", XFM_SHAPE, F32),
        "dbg_uc": _dout(nc, "dbg_uc", (128, 8), F32),
        "dbg_cs": _dout(nc, "dbg_cs", (128, 8), F32),
        "dbg_dlt": _dout(nc, "dbg_dlt", (128, 2048), F32),
        "dbg_pre": _dout(nc, "dbg_pre", (128, 2048), F32),
        "dbg_bd1": _dout(nc, "dbg_bd1", (128, 8), F32),
        "dbg_bd2": _dout(nc, "dbg_bd2", (128, 8), F32),
        "dbg_abs": _dout(nc, "dbg_abs", (128, 2048), F32),
        "dbg_ln": _dout(nc, "dbg_ln", (128, 2048), F32),
        "dbg_scan": _dout(nc, "dbg_scan", (128, 2048), F32),
    }
    P = Prog(nc)
    import os
    P.serialize = tuple((os.environ.get("KSER") or "").split(",")) if os.environ.get("KSER") else ()
    com = Commons(P)
    emit_mixer(P, com, io)
    P.build()
    return nc


def build_moe():
    nc = bass.Bass("TRN2", target_bir_lowering=False)
    io = {
        "x_fm_in": _din(nc, "x_fm_in", XFM_SHAPE, F32),
        "Wr": _din(nc, "Wr", (D, 8), F32),
        "br": _din(nc, "br", (8,), F32),
        "W1": _din(nc, "W1", (NEXP, D, DFF), F32),
        "b1": _din(nc, "b1", (NEXP, DFF), F32),
        "W2": _din(nc, "W2", (NEXP, D, DFF), F32),
        "b2": _din(nc, "b2", (NEXP, DFF), F32),
        "W3": _din(nc, "W3", (NEXP, DFF, D), F32),
        "b3": _din(nc, "b3", (NEXP, D), F32),
        "x_fm_out": _dout(nc, "x_fm_out", XFM_SHAPE, F32),
        "cw_scr": nc.dram_tensor("cw_scr", [8, 4096], F32,
                                 kind="Internal").ap(),
    }
    P = Prog(nc)
    com = Commons(P)
    emit_moe(P, com, io)
    P.build()
    return nc


def build_attn():
    nc = bass.Bass("TRN2", target_bir_lowering=False)
    io = {
        "x_fm_in": _din(nc, "x_fm_in", XFM_SHAPE, F32),
        "W_Q": _din(nc, "W_Q", (D, D), F32),
        "W_K": _din(nc, "W_K", (D, D), F32),
        "W_V": _din(nc, "W_V", (D, D), F32),
        "W_o": _din(nc, "W_o", (D, D), F32),
        "x_fm_out": _dout(nc, "x_fm_out", XFM_SHAPE, F32),
    }
    P = Prog(nc)
    com = Commons(P)
    emit_attn(P, com, io)
    P.build()
    return nc


def _shard_inputs(inputs):
    x = np.ascontiguousarray(
        inputs["L0_attn_x"].reshape(8, NSEQ, SEQ, D))
    return x


def build_output():
    nc = bass.Bass("TRN2", target_bir_lowering=False)
    io = {
        "x_fm_in": _din(nc, "x_fm_in", XFM_SHAPE, F32),
        "out": _dout(nc, "out", (NSEQ, SEQ, D), F32),
    }
    P = Prog(nc)
    com = Commons(P)
    emit_output(P, com, io)
    P.build()
    return nc


def build_fused():
    nc = bass.Bass("TRN2", target_bir_lowering=False)
    def _int(name, shape, dt):
        return nc.dram_tensor(name, list(shape), dt, kind="Internal").ap()
    xfm = _int("x_fm", XFM_SHAPE, F32)
    io = {
        "x_in": _din(nc, "x_in", (NSEQ, SEQ, D), F32),
        "q": _din(nc, "q", (6, D, D), I32),
        "scales": _din(nc, "scales", (6, D, 64), F32),
        "lora_a": _din(nc, "lora_a", (6, 32, D), F32),
        "lora_b": _din(nc, "lora_b", (6, D, 32), F32),
        "w_proj": _din(nc, "w_proj", (4 * D, D), F32),
        "w_delta": _din(nc, "w_delta", (D, D), F32),
        "biases": _din(nc, "biases", (6, D), F32),
        "ln_gamma": _din(nc, "ln_gamma", (3, D), F32),
        "ln_beta": _din(nc, "ln_beta", (3, D), F32),
        "A": _din(nc, "A", (1, 1, D), F32),
        "Bp": _din(nc, "Bp", (1, D, 1), F32),
        "D": _din(nc, "D", (1,), F32),
        "b_delta": _din(nc, "b_delta", (D,), F32),
        "Wr": _din(nc, "Wr", (D, 8), F32),
        "br": _din(nc, "br", (8,), F32),
        "W1": _din(nc, "W1", (NEXP, D, DFF), F32),
        "b1": _din(nc, "b1", (NEXP, DFF), F32),
        "W2": _din(nc, "W2", (NEXP, D, DFF), F32),
        "b2": _din(nc, "b2", (NEXP, DFF), F32),
        "W3": _din(nc, "W3", (NEXP, DFF, D), F32),
        "b3": _din(nc, "b3", (NEXP, D), F32),
        "W_Q": _din(nc, "W_Q", (D, D), F32),
        "W_K": _din(nc, "W_K", (D, D), F32),
        "W_V": _din(nc, "W_V", (D, D), F32),
        "W_o": _din(nc, "W_o", (D, D), F32),
        "x_fm": xfm, "x_fm_in": xfm, "x_fm_out": xfm,
        "wt_ffn0": _int("wt_ffn0", (6, D, D), BF16),
        "laT": _int("laT", (128, 6, 8, 32), BF16),
        "lbT": _int("lbT", (32, 6, 8, 128), BF16),
        "wprojT": _int("wprojT", (D, 4 * D), BF16),
        "wdeltaT": _int("wdeltaT", (D, D), BF16),
        "wcombT": _int("wcombT", (D, D), BF16),
        "cw_scr": _int("cw_scr", (8, 4096), F32),
        "W1c": _int("W1c", (NEXP, D, DFF), BF16),
        "W2c": _int("W2c", (NEXP, D, DFF), BF16),
        "W3c": _int("W3c", (NEXP, DFF, D), BF16),
        "wqkvo_c": _int("wqkvo_c", (4, D, D), BF16),
        "wr_c": _int("wr_c", (D, 8), BF16),
        "out": _dout(nc, "out", (NSEQ, SEQ, D), F32),
    }
    P = Prog(nc)
    com = Commons(P)
    emit_prep(P, com, io)
    emit_ffn0(P, com, io)
    emit_mixer(P, com, io)
    emit_ffn0(P, com, io)
    emit_moe(P, com, io)
    emit_attn(P, com, io)
    emit_moe(P, com, io)
    emit_output(P, com, io)
    P.build()
    return nc


def kernel(**inputs):
    x_sh = _shard_inputs(inputs)
    n = 8
    nc = build_fused()
    in_maps = [{
        "x_in": x_sh[c],
        "q": inputs["L0_ffn_qweights"],
        "scales": inputs["L0_ffn_scales"],
        "lora_a": inputs["L0_ffn_lora_a"],
        "lora_b": inputs["L0_ffn_lora_b"],
        "w_proj": inputs["L0_attn_W_proj"],
        "w_delta": inputs["L0_attn_W_delta"],
        "biases": inputs["L0_ffn_biases"],
        "ln_gamma": inputs["L0_ffn_ln_gamma"],
        "ln_beta": inputs["L0_ffn_ln_beta"],
        "A": inputs["L0_attn_A"],
        "Bp": inputs["L0_attn_Bp"],
        "D": inputs["L0_attn_D"],
        "b_delta": inputs["L0_attn_b_delta"],
        "Wr": inputs["L1_ffn_Wr"], "br": inputs["L1_ffn_br"],
        "W1": inputs["L1_ffn_W1"], "b1": inputs["L1_ffn_b1"],
        "W2": inputs["L1_ffn_W2"], "b2": inputs["L1_ffn_b2"],
        "W3": inputs["L1_ffn_W3"], "b3": inputs["L1_ffn_b3"],
        "W_Q": inputs["L1_attn_W_Q"], "W_K": inputs["L1_attn_W_K"],
        "W_V": inputs["L1_attn_W_V"], "W_o": inputs["L1_attn_W_o"],
    } for c in range(n)]
    res = run_bass_kernel_spmd(nc, in_maps, core_ids=list(range(n))).results
    out = np.concatenate([np.asarray(r["out"]) for r in res], axis=0)
    return out.reshape(16, SEQ, D)


def kernel_unfused(**inputs):
    x_sh = _shard_inputs(inputs)
    n = 8

    def run(nc, maps):
        return run_bass_kernel_spmd(nc, maps, core_ids=list(range(n))).results

    res = run(build_prep(), [{
        "x_in": x_sh[c],
        "q": inputs["L0_ffn_qweights"],
        "scales": inputs["L0_ffn_scales"],
        "lora_a": inputs["L0_ffn_lora_a"],
        "lora_b": inputs["L0_ffn_lora_b"],
        "w_proj": inputs["L0_attn_W_proj"],
        "w_delta": inputs["L0_attn_W_delta"],
    } for c in range(n)])
    state = [dict(r) for r in res]

    nc_ffn0 = build_ffn0()

    def ffn0_maps():
        return [{
            "x_fm_in": state[c]["x_fm"],
            "wt_ffn0": state[c]["wt_ffn0"],
            "laT": state[c]["laT"],
            "lbT": state[c]["lbT"],
            "biases": inputs["L0_ffn_biases"],
            "ln_gamma": inputs["L0_ffn_ln_gamma"],
            "ln_beta": inputs["L0_ffn_ln_beta"],
        } for c in range(n)]

    res = run(nc_ffn0, ffn0_maps())
    for c in range(n):
        state[c]["x_fm"] = res[c]["x_fm_out"]

    res = run(build_mixer(), [{
        "x_fm_in": state[c]["x_fm"],
        "wprojT": state[c]["wprojT"],
        "wcombT": state[c]["wcombT"],
        "A": inputs["L0_attn_A"],
        "Bp": inputs["L0_attn_Bp"],
        "D": inputs["L0_attn_D"],
        "b_delta": inputs["L0_attn_b_delta"],
    } for c in range(n)])
    for c in range(n):
        state[c]["x_fm"] = res[c]["x_fm_out"]

    res = run(nc_ffn0, ffn0_maps())
    for c in range(n):
        state[c]["x_fm"] = res[c]["x_fm_out"]

    nc_moe = build_moe()

    def moe_maps():
        return [{
            "x_fm_in": state[c]["x_fm"],
            "Wr": inputs["L1_ffn_Wr"], "br": inputs["L1_ffn_br"],
            "W1": inputs["L1_ffn_W1"], "b1": inputs["L1_ffn_b1"],
            "W2": inputs["L1_ffn_W2"], "b2": inputs["L1_ffn_b2"],
            "W3": inputs["L1_ffn_W3"], "b3": inputs["L1_ffn_b3"],
        } for c in range(n)]

    res = run(nc_moe, moe_maps())
    for c in range(n):
        state[c]["x_fm"] = res[c]["x_fm_out"]

    res = run(build_attn(), [{
        "x_fm_in": state[c]["x_fm"],
        "W_Q": inputs["L1_attn_W_Q"], "W_K": inputs["L1_attn_W_K"],
        "W_V": inputs["L1_attn_W_V"], "W_o": inputs["L1_attn_W_o"],
    } for c in range(n)])
    for c in range(n):
        state[c]["x_fm"] = res[c]["x_fm_out"]

    res = run(nc_moe, moe_maps())
    for c in range(n):
        state[c]["x_fm"] = res[c]["x_fm_out"]

    res = run(build_output(), [{"x_fm_in": state[c]["x_fm"]}
                               for c in range(n)])
    out = np.concatenate([np.asarray(res[c]["out"]) for c in range(n)],
                         axis=0)
    return out.reshape(16, SEQ, D)


# version 08
# speedup vs baseline: 1.0549x; 1.0365x over previous
import numpy as np
from contextlib import ExitStack
import concourse.bass as bass
import concourse.mybir as mybir
from concourse.bass_utils import run_bass_kernel_spmd

F32 = mybir.dt.float32
BF16 = mybir.dt.bfloat16
I32 = mybir.dt.int32
U32 = mybir.dt.uint32
AF = mybir.ActivationFunctionType
ALU = mybir.AluOpType
AX = mybir.AxisListType

D = 1024
NCH = 8
SEQ = 2048
NSEQ = 2
EPS = 1e-5
DFF = 4096
NEXP = 8

ENGINES = ("sync", "gpsimd", "tensor", "vector", "scalar")



_UNIQ = [0]


def _sb(nc, name, shape, dt, **kw):
    _UNIQ[0] += 1
    return nc.sbuf_tensor(f"{name}_{_UNIQ[0]}", shape, dt, **kw)


def _psm(nc, name, shape, dt):
    _UNIQ[0] += 1
    return nc.psum_tensor(f"{name}_{_UNIQ[0]}", shape, dt)


class Prog:

    def __init__(self, nc):
        self.nc = nc
        self.phases = []
        self.cur = None
        self.sems = {}
        self.cnt = {}
        self.stack = ExitStack()
        self.new_phase()

    def sem(self, name):
        if name not in self.sems:
            self.sems[name] = self.stack.enter_context(self.nc.semaphore(name))
            self.cnt[name] = 0
        return self.sems[name]

    def new_phase(self):
        self.cur = {e: [] for e in ENGINES}
        self.phases.append(self.cur)

    def emit(self, eng, fn):
        self.cur[eng].append(fn)

    def inc(self, sem_name, n=1):
        self.sem(sem_name)
        self.cnt[sem_name] += n
        return self.cnt[sem_name]

    def wait(self, eng, sem_name, val):
        h = self.sem(sem_name)
        self.emit(eng, lambda e, h=h, v=val: e.wait_ge(h, v))

    serialize = ()

    def op(self, eng, method, *args, sem=None, n=1, **kw):
        ser = eng in self.serialize and eng in ("vector", "scalar", "gpsimd",
                                                "tensor")
        if ser:
            last = getattr(self, "_last_op", {}).get(eng)
            if last is not None:
                self.wait(eng, last[0], last[1])
            if sem is None:
                sem, n = "_ser_" + eng, 1
        t = None
        h = None
        if sem is not None:
            t = self.inc(sem, n)
            h = self.sems[sem]

        def fn(e, method=method, args=args, kw=kw, h=h, n=n):
            instr = getattr(e, method)(*args, **kw)
            if h is not None:
                instr.then_inc(h, n)
        self.emit(eng, fn)
        if ser:
            if not hasattr(self, "_last_op"):
                self._last_op = {}
            self._last_op[eng] = (sem, t)
        return t

    def dma(self, q, out, in_, sem="dma", **kw):
        return self.op(q, "dma_start", sem=sem, n=16, out=out, in_=in_, **kw)

    def build(self):
        for ph in self.phases:
            if not any(ph[e] for e in ENGINES):
                continue
            with self.nc.Block() as block:
                for eng in ENGINES:
                    if ph[eng]:
                        def body(E, ops=tuple(ph[eng])):
                            for f in ops:
                                f(E)
                        getattr(block, eng)(body)
        self.stack.close()


class Commons:

    def __init__(self, P):
        nc, st = P.nc, P.stack
        self.ident_f = st.enter_context(_sb(nc, "c_identf", [128, 128], F32))
        self.ident_b = st.enter_context(_sb(nc, "c_identb", [128, 128], BF16))
        self.ones_b = st.enter_context(_sb(nc, "c_onesb", [128, 1], BF16))
        self.ones_f = st.enter_context(_sb(nc, "c_onesf", [1, 128], F32))
        self.iota_f = st.enter_context(_sb(nc, "c_iotaf", [128, 128], F32))
        self.iota_p = st.enter_context(_sb(nc, "c_iotap", [128, 128], F32))
        self.eps_c = st.enter_context(_sb(nc, "c_eps", [128, 1], F32))
        P.op("gpsimd", "memset", self.eps_c[:], EPS)
        P.op("gpsimd", "iota", self.iota_f[:], [[1, 128]], channel_multiplier=0,
             allow_small_or_imprecise_dtypes=True)
        P.op("gpsimd", "iota", self.iota_p[:], [[0, 128]], channel_multiplier=1,
             allow_small_or_imprecise_dtypes=True)
        P.op("gpsimd", "memset", self.ones_b[:], 1.0)
        P.op("gpsimd", "memset", self.ones_f[:], 1.0, sem="pool")
        P.wait("vector", "pool", P.cnt["pool"])
        P.op("vector", "tensor_tensor", self.ident_f[:], self.iota_f[:],
             self.iota_p[:], ALU.is_equal)
        P.op("vector", "tensor_copy", self.ident_b[:], self.ident_f[:], sem="dve")
        self.ready = P.cnt["dve"]


def load_pp(P, com, st, hbm_2d, n, name, ps=None, guard=0):
    nc = P.nc
    dest = st.enter_context(_sb(nc, name, [128, n], F32))
    td = P.dma("sync", dest[:], hbm_2d.rearrange("n p -> p n"),
               allow_slow_non_contiguous=True)
    return dest, td



def emit_prep(P, com, io):
    nc = P.nc
    P.new_phase()
    st = ExitStack()
    x_in = io["x_in"]
    x_fm = io["x_fm"]

    xt = st.enter_context(_sb(nc, "pr_xt", [128, 1024], F32))
    xo = st.enter_context(_sb(nc, "pr_xo", [128, 8, 128], F32))
    pst = [st.enter_context(_psm(nc, f"pr_ps{i}", [128, 128], F32))
           for i in range(2)]
    t_st = 0
    for s in range(NSEQ):
        for tc in range(16):
            P.wait("sync", "dve", P.cnt.get("dve", 0))
            td = P.dma("sync", xt[:], x_in[s, tc * 128:(tc + 1) * 128, :])
            P.wait("tensor", "dma", td)
            P.wait("tensor", "dve", com.ready)
            for dc in range(NCH):
                pb = pst[dc % 2]
                tp = P.op("tensor", "transpose", pb[:],
                          xt[:, dc * 128:(dc + 1) * 128], com.ident_f[:],
                          sem="pe")
                P.wait("vector", "pe", tp)
                if dc == 0:
                    P.wait("vector", "st", t_st)
                tv = P.op("vector", "tensor_copy", xo[:, dc, :], pb[:],
                          sem="dve")
                P.wait("tensor", "dve", tv)
            P.wait("sync", "dve", P.cnt["dve"])
            t_st = P.dma("sync",
                         x_fm[s, :, :, tc * 128:(tc + 1) * 128]
                         .rearrange("c p t -> p c t"), xo[:], sem="st")

    if "W1c" in io:
        for e in range(NEXP):
            for src_t, dst_t in (("W1", "W1c"), ("W2", "W2c"),
                                 ("W3", "W3c")):
                a, b = io[src_t][e], io[dst_t][e]
                if a.shape[-1] > 2048:
                    a = a.rearrange("a (b f) -> a b f", f=2048)
                    b = b.rearrange("a (b f) -> a b f", f=2048)
                P.dma("gpsimd", b, a, sem="dmag")
    if "wqkvo_c" in io:
        for i, nm in enumerate(("W_Q", "W_K", "W_V", "W_o")):
            P.dma("gpsimd", io["wqkvo_c"][i], io[nm], sem="dmag")
        P.dma("gpsimd", io["wr_c"], io["Wr"], sem="dmag")
    t_wcache = P.cnt.get("dmag", 0)

    wrow = st.enter_context(_sb(nc, "pr_wrow", [128, 1024], F32))
    wbf = st.enter_context(_sb(nc, "pr_wbf", [128, 1024], BF16))
    qsb = st.enter_context(_sb(nc, "pr_qsb", [128, 1024], I32))
    scl = st.enter_context(_sb(nc, "pr_scl", [128, 64], F32))
    psb = [st.enter_context(_psm(nc, f"pr_psb{i}", [128, 128], BF16))
           for i in range(2)]
    wtbig = st.enter_context(_sb(nc, "pr_wtbig", [128, 8, 4096], BF16))

    def transpose_mat(src_rows_get, n_out, out_ap3):
        nonlocal t_st
        for oc in range(n_out // 128):
            tready = src_rows_get(oc)
            P.wait("tensor", "dve", tready)
            for dc in range(NCH):
                pb = psb[dc % 2]
                tp = P.op("tensor", "transpose", pb[:],
                          wbf[:, dc * 128:(dc + 1) * 128], com.ident_b[:],
                          sem="pe")
                P.wait("vector", "pe", tp)
                if dc == 0 and oc == 0:
                    P.wait("vector", "st", t_st)
                tv = P.op("vector", "tensor_copy",
                          wtbig[:, dc, oc * 128:(oc + 1) * 128], pb[:],
                          sem="dve")
                P.wait("tensor", "dve", tv)
        P.wait("sync", "dve", P.cnt["dve"])
        t_st = P.dma("sync", out_ap3, wtbig[:, :, 0:n_out], sem="st")

    if "q" in io:
        qw, scales, wt_out = io["q"], io["scales"], io["wt_ffn0"]
        for i in range(6):
            def get_rows(oc, i=i):
                P.wait("sync", "dve", P.cnt["dve"])
                td = P.dma("sync", qsb[:], qw[i, oc * 128:(oc + 1) * 128, :])
                ts_ = P.dma("sync", scl[:], scales[i, oc * 128:(oc + 1) * 128, :])
                P.wait("vector", "dma", max(td, ts_))
                P.op("vector", "tensor_scalar", wrow[:], qsb[:],
                     1.0 / 7.5, -1.0, ALU.mult, op1=ALU.add)
                scl_b = bass.AP(scl.ap().tensor, 0, [[64, 128], [1, 64], [0, 16]])
                return P.op("vector", "tensor_tensor", wbf[:], wrow[:],
                            scl_b, ALU.mult, sem="dve")
            transpose_mat(get_rows, 1024,
                          wt_out[i].rearrange("(c p) o -> p c o", p=128))

    def plain_rows(src2d):
        def get_rows(oc):
            P.wait("sync", "dve", P.cnt["dve"])
            td = P.dma("sync", wrow[:], src2d[oc * 128:(oc + 1) * 128, :])
            P.wait("vector", "dma", td)
            return P.op("vector", "tensor_copy", wbf[:], wrow[:], sem="dve")
        return get_rows

    if "w_proj" in io:
        transpose_mat(plain_rows(io["w_proj"]), 4096,
                      io["wprojT"].rearrange("(c p) o -> p c o", p=128))
        transpose_mat(plain_rows(io["w_delta"]), 1024,
                      io["wdeltaT"].rearrange("(c p) o -> p c o", p=128))

        wdT = st.enter_context(_sb(nc, "pr_wdT", [128, 8, 1024], BF16))
        wpr = st.enter_context(_sb(nc, "pr_wpr", [128, 8, 1024], BF16))
        P.wait("sync", "st", t_st)
        td = P.dma("sync", wdT[:],
                   io["wdeltaT"].rearrange("(c p) o -> p c o", p=128))
        for o1c in range(8):
            P.wait("sync", "dve", P.cnt["dve"])
            tw = P.dma("sync", wrow[:],
                       io["w_proj"][o1c * 128:(o1c + 1) * 128, :])
            P.wait("vector", "dma", tw)
            P.op("vector", "tensor_copy", wpr[:, o1c, :], wrow[:], sem="dve")
        tcast = P.cnt["dve"]
        P.wait("tensor", "dma", td)
        P.wait("tensor", "dve", tcast)
        pcmb = [st.enter_context(_psm(nc, f"pr_pc{i}", [128, 512], F32))
                for i in range(2)]
        ev_t = [0, 0]
        P.wait("vector", "st", t_st)
        for dc in range(8):
            for nt in range(2):
                k = (dc * 2 + nt) % 2
                P.wait("tensor", "dve", ev_t[k])
                tm = None
                for o1c in range(8):
                    tm = P.op("tensor", "matmul", pcmb[k][:],
                              wpr[:, o1c, dc * 128:(dc + 1) * 128],
                              wdT[:, o1c, nt * 512:(nt + 1) * 512],
                              start=(o1c == 0), stop=(o1c == 7),
                              sem="pe" if o1c == 7 else None)
                P.wait("vector", "pe", tm)
                ev_t[k] = P.op("vector", "tensor_copy",
                               wtbig[:, dc, nt * 512:(nt + 1) * 512],
                               pcmb[k][:], sem="dve")
        P.wait("sync", "dve", P.cnt["dve"])
        t_st = P.dma("sync",
                     io["wcombT"].rearrange("(c p) o -> p c o", p=128),
                     wtbig[:, :, 0:1024], sem="st")

    if "lora_a" in io:
        la, lb, laT_h, lbT_h = io["lora_a"], io["lora_b"], io["laT"], io["lbT"]
        lasb = st.enter_context(_sb(nc, "pr_lasb", [32, 1024], F32))
        lasbb = st.enter_context(_sb(nc, "pr_lasbb", [32, 1024], BF16))
        laacc = st.enter_context(_sb(nc, "pr_laacc", [128, 8, 32], BF16))
        for i in range(6):
            P.wait("sync", "dve", P.cnt["dve"])
            td = P.dma("sync", lasb[:], la[i])
            P.wait("vector", "dma", td)
            tv = P.op("vector", "tensor_copy", lasbb[:], lasb[:], sem="dve")
            P.wait("tensor", "dve", tv)
            for dc in range(NCH):
                tp = P.op("tensor", "transpose", psb[dc % 2][:, 0:32],
                          lasbb[:, dc * 128:(dc + 1) * 128],
                          com.ident_b[0:32, 0:32], sem="pe")
                P.wait("vector", "pe", tp)
                if dc == 0:
                    P.wait("vector", "st", t_st)
                tv = P.op("vector", "tensor_copy", laacc[:, dc, :],
                          psb[dc % 2][:, 0:32], sem="dve")
                P.wait("tensor", "dve", tv)
            P.wait("sync", "dve", P.cnt["dve"])
            t_st = P.dma("sync", laT_h[:, i, :, :], laacc[:], sem="st")

        lbsb = st.enter_context(_sb(nc, "pr_lbsb", [128, 32], F32))
        lbsbb = st.enter_context(_sb(nc, "pr_lbsbb", [128, 32], BF16))
        lbacc = st.enter_context(_sb(nc, "pr_lbacc", [32, 8, 128], BF16))
        for i in range(6):
            for oc in range(NCH):
                P.wait("sync", "dve", P.cnt["dve"])
                td = P.dma("sync", lbsb[:], lb[i, oc * 128:(oc + 1) * 128, :])
                P.wait("vector", "dma", td)
                tv = P.op("vector", "tensor_copy", lbsbb[:], lbsb[:], sem="dve")
                P.wait("tensor", "dve", tv)
                tp = P.op("tensor", "transpose", psb[oc % 2][0:32, :],
                          lbsbb[:], com.ident_b[:], sem="pe")
                P.wait("vector", "pe", tp)
                if oc == 0:
                    P.wait("vector", "st", t_st)
                tv = P.op("vector", "tensor_copy", lbacc[:, oc, :],
                          psb[oc % 2][0:32, :], sem="dve")
                P.wait("tensor", "dve", tv)
            P.wait("sync", "dve", P.cnt["dve"])
            t_st = P.dma("sync", lbT_h[:, i, :, :], lbacc[:], sem="st")

    P.wait("sync", "st", P.cnt["st"])
    st.close()



def emit_ln_stats(P, com, src3d, T, xb, xsq, rows, stat_ps, bcast_mb, bcast_rb,
                  wait_act=None):
    nh = T // 512
    P.wait("scalar", "pe", P.cnt.get("pe", 0))
    if wait_act is not None:
        P.wait("scalar", wait_act[0], wait_act[1])
    P.op("scalar", "activation", xb[:, :, 0:T], src3d, AF.Copy)
    ta = P.op("scalar", "activation", xsq[:, :, 0:T], src3d, AF.Square,
              sem="act")
    P.wait("tensor", "act", ta)
    P.wait("tensor", "dve", com.ready)
    tp = None
    for h in range(nh):
        for c in range(NCH):
            P.op("tensor", "matmul", stat_ps[h], com.ones_b[:],
                 xb[:, c, h * 512:(h + 1) * 512],
                 start=(c == 0), stop=(c == NCH - 1))
        for c in range(NCH):
            last = (h == nh - 1 and c == NCH - 1)
            tp = P.op("tensor", "matmul", stat_ps[nh + h], com.ones_b[:],
                      xsq[:, c, h * 512:(h + 1) * 512],
                      start=(c == 0), stop=(c == NCH - 1),
                      sem="pe" if last else None)
    P.wait("vector", "pe", tp)
    for h in range(nh):
        P.op("vector", "tensor_scalar", rows[0:1, 0, h * 512:(h + 1) * 512],
             stat_ps[h], -1.0 / D, None, ALU.mult)
        P.op("vector", "tensor_scalar", rows[0:1, 1, h * 512:(h + 1) * 512],
             stat_ps[nh + h], 1.0 / D, None, ALU.mult)
    P.op("vector", "tensor_tensor", rows[0:1, 2, 0:T], rows[0:1, 0, 0:T],
         rows[0:1, 0, 0:T], ALU.mult)
    tv = P.op("vector", "tensor_tensor", rows[0:1, 2, 0:T], rows[0:1, 1, 0:T],
              rows[0:1, 2, 0:T], ALU.subtract, sem="dve")
    P.wait("scalar", "dve", tv)
    P.op("scalar", "activation", rows[0:1, 3, 0:T], rows[0:1, 2, 0:T],
         AF.Ln, bias=com.eps_c[0:1, :])
    ta = P.op("scalar", "activation", rows[0:1, 3, 0:T], rows[0:1, 3, 0:T],
              AF.Exp, scale=-0.5, sem="act")
    P.wait("tensor", "act", ta)
    tb = None
    for h in range(nh):
        P.op("tensor", "matmul", bcast_mb[h], com.ones_f[:],
             rows[0:1, 0, h * 512:(h + 1) * 512], start=True, stop=True)
        tb = P.op("tensor", "matmul", bcast_rb[h], com.ones_f[:],
                  rows[0:1, 3, h * 512:(h + 1) * 512], start=True, stop=True,
                  sem="pe" if h == nh - 1 else None)
    return tb, tv



def emit_ffn0(P, com, io, n_groups=4, n_blocks=3):
    nc = P.nc
    P.new_phase()
    st = ExitStack()
    x_in = io["x_fm_in"]
    x_out = io["x_fm_out"]
    wt_h = io["wt_ffn0"]
    laT_h = io["laT"]
    lbT_h = io["lbT"]
    T = 1024

    laT = st.enter_context(_sb(nc, "f_laT", [128, 2, 8, 32], BF16))
    lbT = st.enter_context(_sb(nc, "f_lbT", [32, 2, 8, 128], BF16))
    t0 = 0
    psS = [st.enter_context(_psm(nc, f"f_psS{i}", [128, 512], F32))
           for i in range(4)]
    bias_pp, tb1 = load_pp(P, com, st, io["biases"].rearrange("i (c p) -> (i c) p", p=128), 48, "f_bias")
    gam_pp, tb2 = load_pp(P, com, st, io["ln_gamma"].rearrange("i (c p) -> (i c) p", p=128), 24, "f_gam")
    bet_pp, tb3 = load_pp(P, com, st, io["ln_beta"].rearrange("i (c p) -> (i c) p", p=128), 24, "f_bet")
    pp_ready = max(tb1, tb2, tb3)

    wt = [st.enter_context(_sb(nc, f"f_wt{i}", [128, 8, 1024], BF16))
          for i in range(2)]
    xg = st.enter_context(_sb(nc, "f_xg", [128, 8, T], F32))
    hh = st.enter_context(_sb(nc, "f_h", [128, 8, T], F32))
    hb = st.enter_context(_sb(nc, "f_hb", [128, 8, T], BF16))
    yy = st.enter_context(_sb(nc, "f_y", [128, 8, T], F32))
    xb = st.enter_context(_sb(nc, "f_xb", [128, 8, T], BF16))
    xsq = st.enter_context(_sb(nc, "f_xsq", [128, 8, T], BF16))
    gb = xb
    rows = st.enter_context(_sb(nc, "f_rows", [1, 4, T], F32))
    u1 = st.enter_context(_sb(nc, "f_u1", [32, T], BF16))

    psY = [st.enter_context(_psm(nc, f"f_psY{i}", [128, 512], F32))
           for i in range(2)]
    psU = st.enter_context(_psm(nc, "f_psU", [32, 1024], F32))

    P.wait("scalar", "dma", pp_ready)

    y_guard = [("dve", 0), ("dve", 0)]
    t_wt = [0, 0]
    wt_done = [0, 0]
    t_store = 0

    def qmm(mat_i, wtbuf, rhs3d, rhs_ready, u_guard, evac):
        P.wait("tensor", rhs_ready[0], rhs_ready[1])
        P.wait("tensor", "dma", t_wt[wtbuf])
        P.wait("tensor", u_guard[0], u_guard[1])
        tpe = None
        for h2 in range(2):
            for c in range(NCH):
                last = (h2 == 1 and c == NCH - 1)
                tpe = P.op("tensor", "matmul",
                           psU[0:32, h2 * 512:(h2 + 1) * 512],
                           laT[:, mat_i % 2, c, :],
                           rhs3d[:, c, h2 * 512:(h2 + 1) * 512],
                           start=(c == 0), stop=(c == NCH - 1),
                           sem="pe" if last else None)
        P.wait("vector", "pe", tpe)
        u_evac = P.op("vector", "tensor_copy", u1[:], psU[0:32, :], sem="dve")
        P.wait("tensor", "dve", u_evac)
        tpe = 0
        for oc in range(NCH):
            for h2 in range(2):
                k = (oc * 2 + h2) % 2
                pb = psY[k]
                P.wait("tensor", y_guard[k][0], y_guard[k][1])
                for dc in range(NCH):
                    P.op("tensor", "matmul", pb[:],
                         wt[wtbuf][:, dc, oc * 128:(oc + 1) * 128],
                         rhs3d[:, dc, h2 * 512:(h2 + 1) * 512],
                         start=(dc == 0), stop=False)
                tpe = P.op("tensor", "matmul", pb[:],
                           lbT[:, mat_i % 2, oc, :],
                           u1[0:32, h2 * 512:(h2 + 1) * 512],
                           start=False, stop=True, sem="pe")
                evac(oc, h2, pb, tpe, k)
        wt_done[wtbuf] = tpe
        return ("dve", u_evac)

    u_guard = ("dve", 0)
    for g in range(n_groups):
        s, t0g = g // 2, (g % 2) * T
        P.wait("sync", "dve", P.cnt["dve"])
        P.wait("sync", "st", t_store)
        tx = P.dma("sync", xg[:], x_in[s, :, :, t0g:t0g + T]
                   .rearrange("c p t -> p c t"))
        P.wait("sync", "pe", wt_done[0])
        t_wt[0] = P.dma("sync", wt[0][:],
                        wt_h[0].rearrange("(c p) o -> p c o", p=128))

        tpb, _ = emit_ln_stats(P, com, xg[:, :, 0:T], T, xb, xsq, rows,
                               [psS[0][0:1, :], psS[1][0:1, :],
                                psS[2][0:1, :], psS[3][0:1, :]],
                               [psS[0][:], psS[1][:]], [psS[2][:], psS[3][:]],
                               wait_act=("dma", tx))
        P.wait("vector", "pe", tpb)
        for c in range(NCH):
            for h2 in range(2):
                sl = slice(h2 * 512, (h2 + 1) * 512)
                P.op("vector", "tensor_tensor", hh[:, c, sl], xg[:, c, sl],
                     psS[h2][:], ALU.add)
                P.op("vector", "tensor_tensor", hh[:, c, sl], hh[:, c, sl],
                     psS[2 + h2][:], ALU.mult)
            t_hbv = P.op("vector", "tensor_copy", hb[:, c, :], hh[:, c, :],
                         sem="dve" if c == NCH - 1 else None)
        t_hb = ("dve", t_hbv)

        for blk in range(n_blocks):
            i0, i1 = 2 * blk, 2 * blk + 1
            P.wait("sync", "pe", wt_done[1])
            t_wt[1] = P.dma("sync", wt[1][:],
                            wt_h[i1].rearrange("(c p) o -> p c o", p=128))
            P.dma("sync", laT[:], laT_h[:, i0:i0 + 2, :, :])
            t_lo = P.dma("sync", lbT[:], lbT_h[:, i0:i0 + 2, :, :])
            P.wait("tensor", "dma", t_lo)

            def evac1(oc, h2, pb, tpe, k, i0=i0):
                P.wait("scalar", "pe", tpe)
                ta = P.op("scalar", "add",
                          yy[:, oc, h2 * 512:(h2 + 1) * 512], pb[:],
                          bias_pp[:, i0 * 8 + oc:i0 * 8 + oc + 1], sem="act")
                y_guard[k] = ("act", ta)
            u_guard = qmm(i0, 0, hb, t_hb, u_guard, evac1)

            P.wait("tensor", y_guard[0][0], y_guard[0][1])
            P.wait("tensor", y_guard[1][0], y_guard[1][1])
            P.wait("tensor", u_guard[0], u_guard[1])
            tpb, trows = emit_ln_stats(
                P, com, yy[:, :, 0:T], T, xb, xsq, rows,
                [psY[0][0:1, :], psY[1][0:1, :],
                 psU[0:1, 0:512], psU[0:1, 512:1024]],
                [psS[0][:], psS[1][:]], [psS[2][:], psS[3][:]],
                wait_act=None)
            P.wait("vector", "pe", tpb)
            glast = 0
            for c in range(NCH):
                for h2 in range(2):
                    sl = slice(h2 * 512, (h2 + 1) * 512)
                    P.op("vector", "tensor_tensor", yy[:, c, sl], yy[:, c, sl],
                         psS[h2][:], ALU.add)
                    tv = P.op("vector", "tensor_tensor", yy[:, c, sl],
                              yy[:, c, sl], psS[2 + h2][:], ALU.mult,
                              sem="dve")
                    P.wait("scalar", "dve", tv)
                    col = blk * 8 + c
                    glast = P.op("scalar", "activation", gb[:, c, sl],
                                 yy[:, c, sl], AF.Gelu,
                                 bias=bet_pp[:, col:col + 1],
                                 scale=gam_pp[:, col:col + 1], sem="act")

            if blk < n_blocks - 1:
                P.wait("sync", "pe", wt_done[0])
                t_wt[0] = P.dma("sync", wt[0][:],
                                wt_h[2 * blk + 2]
                                .rearrange("(c p) o -> p c o", p=128))

            P.wait("tensor", "dve", trows)
            y_guard[0] = y_guard[1] = ("dve", trows)
            u_guard = ("dve", trows)
            def evac2(oc, h2, pb, tpe, k, i1=i1):
                P.wait("vector", "pe", tpe)
                col = i1 * 8 + oc
                tv = P.op("vector", "scalar_tensor_tensor",
                          hh[:, oc, h2 * 512:(h2 + 1) * 512], pb[:],
                          bias_pp[:, col:col + 1],
                          hh[:, oc, h2 * 512:(h2 + 1) * 512],
                          ALU.add, ALU.add, sem="dve")
                y_guard[k] = ("dve", tv)
            u_guard = qmm(i1, 1, gb, ("act", glast), u_guard, evac2)
            if blk < n_blocks - 1:
                for c in range(NCH):
                    t_hbv = P.op("vector", "tensor_copy", hb[:, c, :],
                                 hh[:, c, :],
                                 sem="dve" if c == NCH - 1 else None)
                t_hb = ("dve", t_hbv)

        tv = P.op("vector", "scalar_tensor_tensor", xg[:, :, 0:T],
                  hh[:, :, 0:T], 0.5, xg[:, :, 0:T], ALU.mult, ALU.add,
                  sem="dve")
        P.wait("sync", "dve", tv)
        t_store = P.dma("sync", x_out[s, :, :, t0g:t0g + T]
                        .rearrange("c p t -> p c t"), xg[:], sem="st")
    P.wait("sync", "st", P.cnt["st"])
    st.close()



def emit_mixer(P, com, io):
    nc = P.nc
    P.new_phase()
    st = ExitStack()
    x_in = io["x_fm_in"]
    x_out = io["x_fm_out"]
    wprojT = io["wprojT"]
    wcombT = io["wcombT"]

    psA = st.enter_context(_psm(nc, "m_psA", [128, 2048], F32))
    psB = st.enter_context(_psm(nc, "m_psB", [128, 2048], F32))

    a_pp, t1 = load_pp(P, com, st, io["A"].rearrange("one1 one2 (c p) -> (one1 one2 c) p", p=128), 8, "m_A")
    b1_pp, t2 = load_pp(P, com, st, io["Bp"].rearrange("one1 (c p) one2 -> (one1 c) (p one2)", p=128), 8, "m_B1")
    bd_pp, t3 = load_pp(P, com, st, io["b_delta"].rearrange("(c p) -> c p", p=128), 8, "m_bd")
    d_pp = st.enter_context(_sb(nc, "m_dpp", [128, 1], F32))
    td = P.dma("sync", d_pp[:],
               bass.AP(io["D"].tensor, 0, [[0, 128], [1, 1]]),
               allow_slow_non_contiguous=True)
    pp_ready = max(t1, t2, t3, td)

    wcomb = st.enter_context(_sb(nc, "m_wcomb", [128, 8, 1024], BF16))
    wpart = st.enter_context(_sb(nc, "m_wpart", [128, 8, 2048], BF16))
    x_mix = st.enter_context(_sb(nc, "m_xmix", [128, 8, 2048], BF16))
    xg = st.enter_context(_sb(nc, "m_xg", [128, 8, 1024], F32))
    xb = st.enter_context(_sb(nc, "m_xb", [128, 8, 1024], BF16))
    xsq = st.enter_context(_sb(nc, "m_xsq", [128, 8, 1024], BF16))
    rows = st.enter_context(_sb(nc, "m_rows", [1, 4, 1024], F32))
    dump = st.enter_context(_sb(nc, "m_dump", [128, 2048], F32))
    scr1 = st.enter_context(_sb(nc, "m_scr1", [128, 2048], F32))
    scr2 = st.enter_context(_sb(nc, "m_scr2", [128, 2048], F32))
    dlt = st.enter_context(_sb(nc, "m_dlt", [128, 2048], F32))
    xc = st.enter_context(_sb(nc, "m_xc", [128, 2048], F32))
    uc_col = st.enter_context(_sb(nc, "m_uc", [128, 8], F32))
    cs_col = st.enter_context(_sb(nc, "m_cs", [128, 8], F32))
    pp1 = st.enter_context(_sb(nc, "m_pp1", [128, 8], F32))

    twc = P.dma("sync", wcomb[:], wcombT.rearrange("(c p) o -> p c o", p=128))
    twp = P.dma("sync", wpart[:, :, 0:1024],
                wprojT[:, 2048:3072].rearrange("(c p) o -> p c o", p=128))
    twp = P.dma("sync", wpart[:, :, 1024:2048],
                wprojT[:, 3072:4096].rearrange("(c p) o -> p c o", p=128))

    P.wait("scalar", "dma", pp_ready)
    P.wait("vector", "dma", pp_ready)
    t_store = 0
    ev_cs = 0
    last_y = 0

    for s in range(NSEQ):
        P.wait("tensor", "act", P.cnt.get("act", P.inc("act", 0)))
        P.wait("tensor", "dve", P.cnt["dve"])
        for grp in range(2):
            sl = slice(grp * 1024, (grp + 1) * 1024)
            P.wait("sync", "dve", P.cnt["dve"])
            P.wait("sync", "st", t_store)
            tx = P.dma("sync", xg[:], x_in[s, :, :, sl]
                       .rearrange("c p t -> p c t"))
            tpb, _ = emit_ln_stats(
                P, com, xg[:, :, 0:1024], 1024, xb, xsq, rows,
                [psA[0:1, 0:512], psA[0:1, 512:1024],
                 psA[0:1, 1024:1536], psA[0:1, 1536:2048]],
                [psA[:, 0:512], psA[:, 512:1024]],
                [psA[:, 1024:1536], psA[:, 1536:2048]],
                wait_act=("dma", tx))
            P.wait("vector", "pe", tpb)
            for c in range(NCH):
                for h2 in range(2):
                    s2 = slice(h2 * 512, (h2 + 1) * 512)
                    P.op("vector", "tensor_tensor", xg[:, c, s2], xg[:, c, s2],
                         psA[:, h2 * 512:(h2 + 1) * 512], ALU.add)
                    P.op("vector", "tensor_tensor",
                         x_mix[:, c, grp * 1024 + h2 * 512:
                               grp * 1024 + (h2 + 1) * 512],
                         xg[:, c, s2],
                         psA[:, 1024 + h2 * 512:1024 + (h2 + 1) * 512],
                         ALU.mult,
                         sem="dve" if (c == NCH - 1 and h2 == 1) else None)
        t_xmix = P.cnt["dve"]

        P.wait("tensor", "dve", t_xmix)
        P.wait("tensor", "dma", twp)
        for oc in range(NCH):
            P.wait("tensor", "dve", ev_cs)
            tm = None
            for nt in range(4):
                for dc in range(NCH):
                    tm = P.op("tensor", "matmul",
                              psB[:, nt * 512:(nt + 1) * 512],
                              wpart[:, dc, oc * 128:(oc + 1) * 128],
                              x_mix[:, dc, nt * 512:(nt + 1) * 512],
                              start=(dc == 0), stop=(dc == NCH - 1),
                              sem="pe" if (nt == 3 and dc == NCH - 1) else None)
            P.wait("vector", "pe", tm)
            P.op("vector", "scalar_tensor_tensor", dump[:], psB[:], 0.0,
                 x_mix[:, oc, :], ALU.add, ALU.mult,
                 accum_out=uc_col[:, oc:oc + 1])
            ev_cs = P.op("vector", "reduce_sum", cs_col[:, oc:oc + 1], psB[:],
                         AX.X, sem="dve")
        tv = P.op("vector", "tensor_tensor", pp1[:], b1_pp[:], uc_col[:],
                  ALU.mult, sem="dve")
        P.wait("vector", "dve", tv)
        P.wait("scalar", "dve", tv)
        if "dbg_uc" in io and s == 0:
            P.wait("sync", "dve", tv)
            P.dma("sync", io["dbg_uc"], uc_col[:], sem="st")
            P.dma("sync", io["dbg_cs"], cs_col[:], sem="st")
            P.dma("sync", io["dbg_bd1"], bd_pp[:], sem="st")

        P.wait("tensor", "dma", twc)
        ev_a = ("dve", t_xmix)
        ev_b = ("dve", ev_cs)
        for oc in range(NCH):
            P.wait("tensor", ev_a[0], ev_a[1])
            tm = None
            for nt in range(4):
                for dc in range(NCH):
                    tm = P.op("tensor", "matmul",
                              psA[:, nt * 512:(nt + 1) * 512],
                              wcomb[:, dc, oc * 128:(oc + 1) * 128],
                              x_mix[:, dc, nt * 512:(nt + 1) * 512],
                              start=(dc == 0), stop=(dc == NCH - 1),
                              sem="pe" if (nt == 3 and dc == NCH - 1) else None)
            P.wait("scalar", "pe", tm)
            P.wait("scalar", "dve", last_y)
            for nt in range(4):
                ta = P.op("scalar", "add",
                          scr2[:, nt * 512:(nt + 1) * 512],
                          psA[:, nt * 512:(nt + 1) * 512],
                          bd_pp[:, oc:oc + 1], sem="act")
            ev_a = ("act", ta)
            if "dbg_uc" in io and s == 0 and oc == 0:
                P.wait("sync", "act", ta)
                tst_ = P.dma("sync", io["dbg_pre"], scr2[:], sem="st")
                P.wait("vector", "st", tst_)
                P.wait("scalar", "st", tst_)
            P.wait("vector", "act", ta)
            tabs_ = P.op("vector", "scalar_tensor_tensor", scr1[:], scr2[:],
                         -1.0, scr2[:], ALU.mult, ALU.max, sem="dve")
            if "dbg_uc" in io and s == 0 and oc == 0:
                P.wait("sync", "dve", tabs_)
                tst_ = P.dma("sync", io["dbg_abs"], scr1[:], sem="st")
                P.wait("vector", "st", tst_)
                P.wait("scalar", "st", tst_)
            P.wait("scalar", "dve", P.cnt["dve"])
            P.op("scalar", "activation", scr1[:], scr1[:], AF.Exp, scale=-1.0)
            ta = P.op("scalar", "activation", scr1[:], scr1[:], AF.Ln,
                      bias=1.0, sem="act")
            if "dbg_uc" in io and s == 0 and oc == 0:
                P.wait("sync", "act", ta)
                tst_ = P.dma("sync", io["dbg_ln"], scr1[:], sem="st")
                P.wait("vector", "st", tst_)
                P.wait("scalar", "st", tst_)
            P.wait("vector", "act", ta)
            tdd = P.op("vector", "scalar_tensor_tensor", dlt[:], scr2[:], 0.0,
                       scr1[:], ALU.max, ALU.add,
                       sem="dve" if ("dbg_uc" in io and s == 0 and oc == 0) else None)
            if "dbg_uc" in io and s == 0 and oc == 0:
                P.wait("sync", "dve", tdd)
                tst_ = P.dma("sync", io["dbg_dlt"], dlt[:], sem="st")
                P.wait("vector", "st", tst_)
                P.wait("scalar", "st", tst_)
            tvs = P.op("vector", "tensor_scalar", scr1[:], dlt[:],
                       b1_pp[:, oc:oc + 1], None, ALU.mult, sem="dve")
            P.wait("vector", "dve", tvs)
            P.op("vector", "tensor_tensor_scan", scr1[:], scr1[:], scr1[:],
                 0.0, ALU.add, ALU.bypass)
            tv = P.op("vector", "tensor_scalar", scr1[:], scr1[:],
                      cs_col[:, oc:oc + 1], None, ALU.mult, sem="dve")
            if "dbg_uc" in io and s == 0 and oc == 0:
                P.wait("sync", "dve", tv)
                tst_ = P.dma("sync", io["dbg_scan"], scr1[:], sem="st")
                P.wait("vector", "st", tst_)
                P.wait("scalar", "st", tst_)
            P.wait("scalar", "dve", tv)
            ta = P.op("scalar", "activation", dlt[:], dlt[:], AF.Exp,
                      scale=a_pp[:, oc:oc + 1], sem="act")
            P.wait("vector", "act", ta)
            tv = P.op("vector", "scalar_tensor_tensor", scr1[:], dlt[:],
                      pp1[:, oc:oc + 1], scr1[:], ALU.mult, ALU.add, sem="dve")
            last_y = tv
            P.wait("tensor", ev_b[0], ev_b[1])
            tm = None
            for nt in range(4):
                for dc in range(NCH):
                    tm = P.op("tensor", "matmul",
                              psB[:, nt * 512:(nt + 1) * 512],
                              wpart[:, dc, 1024 + oc * 128:
                                    1024 + (oc + 1) * 128],
                              x_mix[:, dc, nt * 512:(nt + 1) * 512],
                              start=(dc == 0), stop=(dc == NCH - 1),
                              sem="pe" if (nt == 3 and dc == NCH - 1) else None)
            P.wait("vector", "pe", tm)
            P.wait("vector", "dve", tv)
            tv = P.op("vector", "scalar_tensor_tensor", scr1[:], psB[:],
                      d_pp[:, 0:1], scr1[:], ALU.mult, ALU.add, sem="dve")
            ev_b = ("dve", tv)
            P.wait("sync", "st", t_store)
            txc = P.dma("sync", xc[:], x_in[s, oc, :, :])
            P.wait("vector", "dma", txc)
            tv = P.op("vector", "tensor_tensor", xc[:], xc[:], scr1[:],
                      ALU.add, sem="dve")
            P.wait("sync", "dve", tv)
            t_store = P.dma("sync", x_out[s, oc, :, :], xc[:], sem="st")

    if "dbg_uc" in io:
        P.dma("sync", io["dbg_bd2"], bd_pp[:], sem="st")
    P.wait("sync", "st", P.cnt["st"])
    st.close()



def emit_attn(P, com, io):
    nc = P.nc
    P.new_phase()
    st = ExitStack()
    x_in = io["x_fm_in"]
    x_out = io["x_fm_out"]
    NH = 16
    DH = 64

    psA = st.enter_context(_psm(nc, "a_psA", [128, 2048], F32))
    psB = st.enter_context(_psm(nc, "a_psB", [128, 2048], F32))

    wq = st.enter_context(_sb(nc, "a_wq", [128, 8, 1024], BF16))
    wx = st.enter_context(_sb(nc, "a_wx", [128, 8, 1024], BF16))
    kf = st.enter_context(_sb(nc, "a_kf", [128, 8, 2048], BF16))
    vt = st.enter_context(_sb(nc, "a_vt", [128, 16, 1024], BF16))
    xat = st.enter_context(_sb(nc, "a_xat", [128, 8, 2048], BF16))
    qg = st.enter_context(_sb(nc, "a_qg", [128, 8, 512], BF16))
    pt = st.enter_context(_sb(nc, "a_pt", [128, 4, 512], BF16))
    att = st.enter_context(_sb(nc, "a_att", [128, 8, 512], BF16))
    msk = st.enter_context(_sb(nc, "a_msk", [128, 4, 512], BF16))
    lb = st.enter_context(_sb(nc, "a_lb", [128, 2, 512], F32))
    lrow = st.enter_context(_sb(nc, "a_lrow", [1, 512], F32))
    xg = st.enter_context(_sb(nc, "a_xg", [128, 8, 512], F32))
    xb = st.enter_context(_sb(nc, "a_xb", [128, 8, 512], BF16))
    xsq = st.enter_context(_sb(nc, "a_xsq", [128, 8, 512], BF16))
    rows = st.enter_context(_sb(nc, "a_rows", [1, 4, 512], F32))
    xc = st.enter_context(_sb(nc, "a_xc", [128, 512], F32))

    mi = st.enter_context(_sb(nc, "a_mi", [128, 512], F32))
    for j in range(4):
        P.op("gpsimd", "iota", mi[:], [[1, 512]], base=-128 * j,
             channel_multiplier=-1, allow_small_or_imprecise_dtypes=True,
             sem="pool")
        P.wait("vector", "pool", P.cnt["pool"])
        P.op("vector", "tensor_scalar", msk[:, j, :], mi[:], 0.0, -1e30,
             ALU.is_lt, op1=ALU.mult, sem="dve")
        P.wait("gpsimd", "dve", P.cnt["dve"])
    wsrc = (io["wqkvo_c"] if "wqkvo_c" in io else
            None)

    def _wload(buf, i, nm):
        if wsrc is not None:
            return P.dma("sync", buf,
                         wsrc[i].rearrange("(c p) o -> p c o", p=128),
                         sem="dma")
        return P.dma("gpsimd", buf,
                     io[nm].rearrange("(c p) o -> p c o", p=128), sem="dmag")

    wq_sem = "dma" if wsrc is not None else "dmag"
    t_wq = _wload(wq[:], 0, "W_Q")
    t_wx = _wload(wx[:], 1, "W_K")

    t_store = 0
    SC = 1.0 / (DH ** 0.5)
    sguard = [0, 0, 0, 0]
    ptg = [0, 0, 0, 0]
    lg_guard = ("dve", 0)
    lbg = ("dve", 0)
    att_guard = [("dve", 0), ("dve", 0)]

    for s in range(NSEQ):
        P.wait("tensor", "act", P.cnt.get("act", P.inc("act", 0)))
        P.wait("tensor", "dve", P.cnt["dve"])

        for grp in range(4):
            sl = slice(grp * 512, (grp + 1) * 512)
            P.wait("sync", "dve", P.cnt["dve"])
            P.wait("sync", "st", t_store)
            tx = P.dma("sync", xg[:], x_in[s, :, :, sl]
                       .rearrange("c p t -> p c t"))
            tpb, _ = emit_ln_stats(
                P, com, xg[:, :, 0:512], 512, xb, xsq, rows,
                [psA[0:1, 0:512], psA[0:1, 512:1024]],
                [psA[:, 0:512]], [psA[:, 512:1024]],
                wait_act=("dma", tx))
            P.wait("vector", "pe", tpb)
            for c in range(NCH):
                P.op("vector", "tensor_tensor", xg[:, c, :], xg[:, c, :],
                     psA[:, 0:512], ALU.add)
                P.op("vector", "tensor_tensor", xat[:, c, sl], xg[:, c, :],
                     psA[:, 512:1024], ALU.mult,
                     sem="dve" if c == NCH - 1 else None)
        t_xat = P.cnt["dve"]

        P.wait("tensor", "dve", t_xat)
        P.wait("tensor", wq_sem, t_wx)
        ev = [("dve", 0)] * 4
        for oc in range(NCH):
            for nt in range(4):
                k = nt
                pb = psB[:, k * 512:(k + 1) * 512]
                P.wait("tensor", ev[k][0], ev[k][1])
                tm = None
                for dc in range(NCH):
                    tm = P.op("tensor", "matmul", pb,
                              wx[:, dc, oc * 128:(oc + 1) * 128],
                              xat[:, dc, nt * 512:(nt + 1) * 512],
                              start=(dc == 0), stop=(dc == NCH - 1),
                              sem="pe" if dc == NCH - 1 else None)
                P.wait("scalar", "pe", tm)
                ta = P.op("scalar", "activation",
                          kf[:, oc, nt * 512:(nt + 1) * 512], pb, AF.Copy,
                          scale=SC, sem="act")
                ev[k] = ("act", ta)
        P.wait("sync" if wsrc is not None else "gpsimd", "act",
               P.cnt["act"])
        t_wx = _wload(wx[:], 2, "W_V")
        P.wait("tensor", wq_sem, t_wx)
        for kt in range(16):
            for nt2 in range(2):
                k = (kt * 2 + nt2) % 4
                pb = psB[:, k * 512:(k + 1) * 512]
                P.wait("tensor", ev[k][0], ev[k][1])
                tm = None
                for dc in range(NCH):
                    tm = P.op("tensor", "matmul", pb,
                              xat[:, dc, kt * 128:(kt + 1) * 128],
                              wx[:, dc, nt2 * 512:(nt2 + 1) * 512],
                              start=(dc == 0), stop=(dc == NCH - 1),
                              sem="pe" if dc == NCH - 1 else None)
                P.wait("scalar", "pe", tm)
                ta = P.op("scalar", "activation",
                          vt[:, kt, nt2 * 512:(nt2 + 1) * 512], pb, AF.Copy,
                          sem="act")
                ev[k] = ("act", ta)
        P.wait("sync" if wsrc is not None else "gpsimd", "act",
               P.cnt["act"])
        t_wx = _wload(wx[:], 3, "W_o")

        for g in range(4):
            gsl = slice(g * 512, (g + 1) * 512)
            P.wait("tensor", wq_sem, t_wq)
            P.wait("tensor", "act", P.cnt["act"])
            P.wait("tensor", "dve", P.cnt["dve"])
            for oc in range(NCH):
                k = oc % 2
                pb = psA[:, k * 512:(k + 1) * 512]
                P.wait("tensor", ev[k][0], ev[k][1])
                tm = None
                for dc in range(NCH):
                    tm = P.op("tensor", "matmul", pb,
                              wq[:, dc, oc * 128:(oc + 1) * 128],
                              xat[:, dc, gsl],
                              start=(dc == 0), stop=(dc == NCH - 1),
                              sem="pe" if dc == NCH - 1 else None)
                P.wait("scalar", "pe", tm)
                ta = P.op("scalar", "activation", qg[:, oc, :], pb, AF.Copy,
                          sem="act")
                ev[k] = ("act", ta)
            t_qg = ("act", P.cnt["act"])

            nkt = 4 * g + 4
            for h in range(NH):
                base = (h % 2) * 64
                hc = h // 2
                ob = h % 2
                P.wait("tensor", t_qg[0], t_qg[1])
                pv_t = {}
                exp_t = {}
                for step in range(nkt + 2):
                    kt = step
                    if kt < nkt:
                        kb = kt % 4
                        pb = psA[:, kb * 512:(kb + 1) * 512]
                        P.wait("tensor", "act", sguard[kb])
                        tm = P.op("tensor", "matmul", pb,
                                  kf[base:base + 64, hc,
                                     kt * 128:(kt + 1) * 128],
                                  qg[base:base + 64, hc, :],
                                  start=True, stop=True, sem="pe")
                        if kt >= 4 * g:
                            jj = kt - 4 * g
                            P.wait("vector", "pe", tm)
                            tm2 = P.op("vector", "tensor_tensor", pb, pb,
                                       msk[:, jj, :], ALU.add, sem="dve")
                            P.wait("scalar", "dve", tm2)
                        else:
                            P.wait("scalar", "pe", tm)
                        P.wait("scalar", "pe", ptg[kb])
                        exp_t[kt] = P.op("scalar", "activation",
                                         pt[:, kb, :], pb, AF.Exp, sem="act")
                        sguard[kb] = exp_t[kt]
                    j = step - 2
                    if 0 <= j < nkt:
                        jb = j % 4
                        P.wait("tensor", "act", exp_t[j])
                        if j == 0:
                            P.wait("tensor", lg_guard[0], lg_guard[1])
                            P.wait("tensor", att_guard[ob][0],
                                   att_guard[ob][1])
                        P.op("tensor", "matmul", psB[0:1, 1024:1536],
                             com.ones_b[:], pt[:, jb, :],
                             start=(j == 0), stop=(j == nkt - 1))
                        pv_t[j] = P.op("tensor", "matmul",
                                       psB[base:base + 64,
                                           ob * 512:ob * 512 + 512],
                                       vt[:, j, h * 64:h * 64 + 64],
                                       pt[:, jb, :],
                                       start=(j == 0), stop=(j == nkt - 1),
                                       tile_position=(0, base), sem="pe")
                        ptg[jb] = pv_t[j]
                P.wait("vector", "pe", pv_t[nkt - 1])
                tv = P.op("vector", "tensor_copy", lrow[:],
                          psB[0:1, 1024:1536], sem="dve")
                lg_guard = ("dve", tv)
                P.wait("scalar", "dve", tv)
                P.op("scalar", "activation", lrow[:], lrow[:], AF.Ln)
                ta = P.op("scalar", "activation", lrow[:], lrow[:], AF.Exp,
                          scale=-1.0, sem="act")
                P.wait("tensor", "act", ta)
                P.wait("tensor", lbg[0], lbg[1])
                tm = P.op("tensor", "matmul", psB[:, 1536:2048], com.ones_f[:],
                          lrow[:], start=True, stop=True, sem="pe")
                P.wait("vector", "pe", tm)
                tv = P.op("vector", "tensor_copy", lb[:, h % 2, :],
                          psB[:, 1536:2048], sem="dve")
                lbg = ("dve", tv)
                P.wait("vector", "dve", tv)
                td = P.op("vector", "tensor_tensor",
                          att[base:base + 64, hc, :],
                          psB[base:base + 64, ob * 512:ob * 512 + 512],
                          lb[base:base + 64, h % 2, :],
                          ALU.mult, sem="dve")
                att_guard[ob] = ("dve", td)

            P.wait("tensor", att_guard[0][0], att_guard[0][1])
            P.wait("tensor", att_guard[1][0], att_guard[1][1])
            P.wait("tensor", "act", P.cnt["act"])
            P.wait("tensor", "dve", P.cnt["dve"])
            P.wait("tensor", wq_sem, t_wx)
            for oc in range(NCH):
                k = oc % 2
                pb = psA[:, k * 512:(k + 1) * 512]
                P.wait("tensor", ev[k][0], ev[k][1])
                tm = None
                for dc in range(NCH):
                    tm = P.op("tensor", "matmul", pb,
                              wx[:, dc, oc * 128:(oc + 1) * 128],
                              att[:, dc, :],
                              start=(dc == 0), stop=(dc == NCH - 1),
                              sem="pe" if dc == NCH - 1 else None)
                P.wait("sync", "st", t_store)
                txc = P.dma("sync", xc[:], x_in[s, oc, :, gsl])
                P.wait("vector", "dma", txc)
                P.wait("vector", "pe", tm)
                tv = P.op("vector", "tensor_tensor", xc[:], xc[:], pb,
                          ALU.add, sem="dve")
                ev[k] = ("dve", tv)
                P.wait("sync", "dve", tv)
                t_store = P.dma("sync", x_out[s, oc, :, gsl], xc[:], sem="st")
    P.wait("sync", "st", P.cnt["st"])
    st.close()



def emit_ppT(P, com, st, src2d, R, Cc, name, ps, tmp, guard=0):
    nc = P.nc
    dest = st.enter_context(_sb(nc, name, [128, R * Cc], F32))
    P.wait("sync", "pe", guard)
    td = P.dma("sync", tmp[0:R, 0:Cc * 128], src2d)
    P.wait("tensor", "dma", td)
    P.wait("tensor", "dve", com.ready)
    t = 0
    for cc in range(Cc):
        pb = ps[cc % 2]
        if cc >= 2:
            P.wait("tensor", "dve", t)
        tp = P.op("tensor", "transpose", pb, tmp[:, cc * 128:(cc + 1) * 128],
                  com.ident_f[0:R, 0:R], sem="pe")
        P.wait("vector", "pe", tp)
        t = P.op("vector", "tensor_copy",
                 bass.AP(dest.ap().tensor, cc, [[R * Cc, 128], [Cc, R]]),
                 pb, sem="dve")
    return dest, t


def emit_moe(P, com, io):
    nc = P.nc
    P.new_phase()
    st = ExitStack()
    x_in = io["x_fm_in"]
    x_out = io["x_fm_out"]
    if "W1c" in io:
        W1h, W2h, W3h = io["W1c"], io["W2c"], io["W3c"]
    else:
        W1h, W2h, W3h = io["W1"], io["W2"], io["W3"]
    cw_scr = io["cw_scr"]

    psA = st.enter_context(_psm(nc, "e_psA", [128, 2048], F32))
    psB = st.enter_context(_psm(nc, "e_psB", [128, 2048], F32))

    xln = st.enter_context(_sb(nc, "e_xln", [128, 8, 1024], BF16))
    yy = st.enter_context(_sb(nc, "e_y", [128, 8, 1024], F32))
    hh = st.enter_context(_sb(nc, "e_h", [128, 32, 1024], BF16))
    w1q = st.enter_context(_sb(nc, "e_w1q", [128, 2, 8, 512], BF16))
    w2q = st.enter_context(_sb(nc, "e_w2q", [128, 2, 8, 512], BF16))
    xg = st.enter_context(_sb(nc, "e_xg", [128, 8, 512], F32))
    rows = st.enter_context(_sb(nc, "e_rows", [1, 4, 512], F32))
    cwb = st.enter_context(_sb(nc, "e_cwb", [128, 1024], F32))
    s1 = st.enter_context(_sb(nc, "e_s1", [128, 2, 512], BF16))
    tt1 = st.enter_context(_sb(nc, "e_tt1", [128, 512], F32))
    wr_sb = st.enter_context(_sb(nc, "e_wr", [128, 8, 8], BF16))
    brb = st.enter_context(_sb(nc, "e_brb", [128, 8], F32))
    iota8 = st.enter_context(_sb(nc, "e_io8", [128, 8], F32))
    lg = st.enter_context(_sb(nc, "e_lg", [128, 32, 8], F32))
    mxall = st.enter_context(_sb(nc, "e_mx", [128, 32, 8], F32))
    miall = st.enter_context(_sb(nc, "e_mi", [128, 32, 8], U32))
    i1f = st.enter_context(_sb(nc, "e_i1f", [128, 2, 32], F32))
    wgt = st.enter_context(_sb(nc, "e_wgt", [128, 5, 32], F32))
    oh = st.enter_context(_sb(nc, "e_oh", [128, 8], F32))
    cw = st.enter_context(_sb(nc, "e_cw", [128, 32, 8], F32))
    cwTt = st.enter_context(_sb(nc, "e_cwTt", [8, 2, 128], F32))

    xb = hh[:, 0:8, 0:512]
    xsq = hh[:, 8:16, 0:512]

    _pp = [psA[:, 0:8], psA[:, 512:520]]
    _tmp = st.enter_context(_sb(nc, "e_ppt", [8, 4096], F32))
    b1_pp, tp1 = emit_ppT(P, com, st, io["b1"], 8, 32, "e_b1", _pp, _tmp)
    b2_pp, tp2 = emit_ppT(P, com, st, io["b2"], 8, 32, "e_b2", _pp, _tmp,
                          guard=P.cnt["pe"])
    b3_pp, tp3 = emit_ppT(P, com, st, io["b3"], 8, 8, "e_b3", _pp, _tmp,
                          guard=P.cnt["pe"])
    P.op("gpsimd", "iota", iota8[:], [[1, 8]], channel_multiplier=0,
         allow_small_or_imprecise_dtypes=True, sem="pool")
    t_io8 = P.cnt["pool"]
    if "wr_c" in io:
        tw = P.dma("sync", wr_sb[:],
                   io["wr_c"].rearrange("(c p) e -> p c e", p=128), sem="dma")
        wr_sem = "dma"
    else:
        tw = P.dma("gpsimd", wr_sb[:],
                   io["Wr"].rearrange("(c p) e -> p c e", p=128), sem="dmag")
        wr_sem = "dmag"
    tb = P.dma("sync", brb[:],
               bass.AP(io["br"].tensor, 0, [[0, 128], [1, 8]]),
               allow_slow_non_contiguous=True)
    pp_ready = max(tp1, tp2, tp3)

    t_store = 0
    ev_ps = ("dve", 0)
    for g4 in range(4):
        for sub in range(2):
            t0g = g4 * 1024 + sub * 512
            s_, lo = t0g // SEQ, t0g % SEQ
            P.wait("sync", "dve", P.cnt["dve"])
            tx = P.dma("sync", xg[:], x_in[s_, :, :, lo:lo + 512]
                       .rearrange("c p t -> p c t"))
            tpb, _ = emit_ln_stats(
                P, com, xg[:, :, 0:512], 512, xb, xsq, rows,
                [psA[0:1, 0:512], psA[0:1, 512:1024]],
                [psA[:, 0:512]], [psA[:, 512:1024]],
                wait_act=("dma", tx))
            P.wait("vector", "pe", tpb)
            for c in range(NCH):
                P.op("vector", "tensor_tensor", xg[:, c, :], xg[:, c, :],
                     psA[:, 0:512], ALU.add)
                P.op("vector", "tensor_tensor",
                     xln[:, c, sub * 512:(sub + 1) * 512], xg[:, c, :],
                     psA[:, 512:1024], ALU.mult,
                     sem="dve" if c == NCH - 1 else None)
        t_xln = P.cnt["dve"]
        P.wait("tensor", "dve", t_xln)
        P.wait("tensor", wr_sem, tw)
        for tt2 in range(8):
            tt = g4 * 8 + tt2
            k = tt2 % 2
            pb = psA[:, 1024 + k * 8:1024 + (k + 1) * 8]
            P.wait("tensor", ev_ps[0], ev_ps[1])
            tm = None
            for dc in range(NCH):
                tm = P.op("tensor", "matmul", pb,
                          xln[:, dc, tt2 * 128:(tt2 + 1) * 128],
                          wr_sb[:, dc, :], start=(dc == 0),
                          stop=(dc == NCH - 1),
                          sem="pe" if dc == NCH - 1 else None)
            P.wait("vector", "pe", tm)
            tv = P.op("vector", "tensor_copy", lg[:, tt, :], pb, sem="dve")
            ev_ps = ("dve", tv)

    P.wait("vector", "dma", tb)
    brb_b = bass.AP(brb.ap().tensor, 0, [[8, 128], [0, 32], [1, 8]])
    tv = P.op("vector", "tensor_tensor", lg[:], lg[:], brb_b, ALU.add,
              sem="dve")
    for tt in range(32):
        P.wait("vector", "dve", P.cnt["dve"])
        P.op("vector", "max", mxall[:, tt, :], lg[:, tt, :], sem="dve")
        P.wait("vector", "dve", P.cnt["dve"])
        P.op("vector", "max_index", miall[:, tt, :], mxall[:, tt, :],
             lg[:, tt, :], sem="dve")
    m1 = bass.AP(mxall.ap().tensor, 0, [[256, 128], [8, 32]])
    m2 = bass.AP(mxall.ap().tensor, 1, [[256, 128], [8, 32]])
    i1u = bass.AP(miall.ap().tensor, 0, [[256, 128], [8, 32]])
    i2u = bass.AP(miall.ap().tensor, 1, [[256, 128], [8, 32]])
    P.wait("vector", "dve", P.cnt["dve"])
    P.op("vector", "tensor_tensor", wgt[:, 0, :], m2, m1, ALU.subtract)
    P.op("vector", "tensor_copy", i1f[:, 0, :], i1u)
    tv = P.op("vector", "tensor_copy", i1f[:, 1, :], i2u, sem="dve")
    P.wait("scalar", "dve", tv)
    ta = P.op("scalar", "activation", wgt[:, 0, :], wgt[:, 0, :], AF.Exp,
              sem="act")
    P.wait("vector", "act", ta)
    tv = P.op("vector", "tensor_scalar", wgt[:, 1, :], wgt[:, 0, :], 1.0,
              None, ALU.add, sem="dve")
    P.wait("vector", "dve", tv)
    tv = P.op("vector", "reciprocal", wgt[:, 4, :], wgt[:, 1, :], sem="dve")
    P.wait("vector", "dve", tv)
    P.op("vector", "tensor_scalar", wgt[:, 2, :], wgt[:, 4, :], 0.5, None,
         ALU.mult)
    tv = P.op("vector", "scalar_tensor_tensor", wgt[:, 3, :], wgt[:, 0, :],
              0.5, wgt[:, 4, :], ALU.mult, ALU.mult, sem="dve")
    P.wait("vector", "dve", tv)
    P.wait("vector", "pool", t_io8)
    for tt in range(32):
        P.op("vector", "tensor_scalar", oh[:], iota8[:],
             i1f[:, 0, tt:tt + 1], None, ALU.is_equal)
        P.op("vector", "tensor_scalar", cw[:, tt, :], oh[:],
             wgt[:, 2, tt:tt + 1], None, ALU.mult)
        P.op("vector", "tensor_scalar", oh[:], iota8[:],
             i1f[:, 1, tt:tt + 1], None, ALU.is_equal)
        P.op("vector", "scalar_tensor_tensor", cw[:, tt, :], oh[:],
             wgt[:, 3, tt:tt + 1], cw[:, tt, :], ALU.mult, ALU.add,
             sem="dve" if tt == 31 else None)
    t_cw = P.cnt["dve"]
    P.wait("tensor", "dve", t_cw)
    t = 0
    t_cws = 0
    cwv = cw_scr.rearrange("e (tt t) -> e tt t", t=128)
    for tt in range(32):
        pb = psA[:, 1024 + (tt % 2) * 128:1024 + (tt % 2) * 128 + 128]
        if tt >= 2:
            P.wait("tensor", "dve", t)
        tp = P.op("tensor", "transpose", pb[0:8, 0:128], cw[:, tt, :],
                  com.ident_f[:], sem="pe")
        P.wait("vector", "pe", tp)
        if tt >= 2:
            P.wait("vector", "st", t_cws)
        t = P.op("vector", "tensor_copy", cwTt[:, tt % 2, :], pb[0:8, 0:128],
                 sem="dve")
        P.wait("sync", "dve", t)
        t_cws = P.dma("sync", cwv[:, tt, :], cwTt[:, tt % 2, :], sem="st")
    if "dbg_cw" in io:
        P.wait("sync", "dve", P.cnt["dve"])
        P.wait("sync", "st", t_cws)
        P.dma("sync", io["dbg_cw"], cw[:], sem="st")
        P.dma("sync", io["dbg_lg"], lg[:], sem="st")
        P.dma("sync", io["dbg_mx"], mxall[:], sem="st")
        P.dma("sync", io["dbg_mi"], miall[:], sem="st")
        P.dma("sync", io["dbg_wgt"], wgt[:, 0:4, :], sem="st")
        tdb = P.dma("sync", io["dbg_i1f"], i1f[:], sem="st")
        P.wait("vector", "st", tdb)
        P.wait("scalar", "st", tdb)
        P.wait("tensor", "st", tdb)

    w1_free = [0, 0]
    w23_free = [0, 0]
    hg = [("dve", 0)] * 4
    yg = [("dve", 0)] * 8
    t_cwb = 0
    for g in range(4):
        P.wait("vector", "dve", P.cnt["dve"])
        tv = P.op("vector", "memset", yy[:], 0.0, sem="dve")
        t_y0 = tv
        P.wait("scalar", "pe", P.cnt["pe"])
        for sub in range(2):
            t0g = g * 1024 + sub * 512
            s_, lo = t0g // SEQ, t0g % SEQ
            P.wait("sync", "dve", P.cnt["dve"])
            tx = P.dma("sync", xg[:], x_in[s_, :, :, lo:lo + 512]
                       .rearrange("c p t -> p c t"))
            tpb, _ = emit_ln_stats(
                P, com, xg[:, :, 0:512], 512, xb, xsq, rows,
                [psA[0:1, 0:512], psA[0:1, 512:1024]],
                [psA[:, 0:512]], [psA[:, 512:1024]],
                wait_act=("dma", tx))
            P.wait("vector", "pe", tpb)
            for c in range(NCH):
                P.op("vector", "tensor_tensor", xg[:, c, :], xg[:, c, :],
                     psA[:, 0:512], ALU.add)
                P.op("vector", "tensor_tensor",
                     xln[:, c, sub * 512:(sub + 1) * 512], xg[:, c, :],
                     psA[:, 512:1024], ALU.mult,
                     sem="dve" if c == NCH - 1 else None)
        t_xln = P.cnt["dve"]
        hg = [("dve", t_xln)] * 4

        for e in range(NEXP):
            P.wait("sync", "st", t_cws)
            P.wait("sync", "dve", P.cnt["dve"])
            t_cwb = P.dma("sync", cwb[:],
                          bass.AP(cw_scr.tensor, e * 4096 + g * 1024,
                                  [[0, 128], [1, 1024]]),
                          allow_slow_non_contiguous=True)
            slab_t = [0, 0]

            def issue_a(q8, e=e):
                slot = q8 % 2
                P.wait("gpsimd", "pe", w1_free[slot])
                P.dma("gpsimd", w1q[:, slot],
                      W1h[e][:, q8 * 512:(q8 + 1) * 512]
                      .rearrange("(c p) f -> p c f", p=128), sem="dmag")
                P.wait("gpsimd", "pe", w23_free[slot])
                slab_t[slot] = P.dma(
                    "gpsimd", w2q[:, slot],
                    W2h[e][:, q8 * 512:(q8 + 1) * 512]
                    .rearrange("(c p) f -> p c f", p=128), sem="dmag")

            issue_a(0)
            issue_a(1)
            P.wait("tensor", "dve", t_xln)
            for q8 in range(8):
                slot = q8 % 2
                P.wait("tensor", "dmag", slab_t[slot])
                for cq in range(4):
                    c = q8 * 4 + cq
                    for nh in range(2):
                        k = (cq * 2 + nh) % 2
                        p1 = psA[:, k * 512:(k + 1) * 512]
                        p2 = psA[:, 1024 + k * 512:1024 + (k + 1) * 512]
                        P.wait("tensor", hg[k][0], hg[k][1])
                        P.wait("tensor", hg[2 + k][0], hg[2 + k][1])
                        for dc in range(NCH):
                            P.op("tensor", "matmul", p1,
                                 w1q[:, slot, dc, cq * 128:(cq + 1) * 128],
                                 xln[:, dc, nh * 512:(nh + 1) * 512],
                                 start=(dc == 0), stop=(dc == NCH - 1),
                                 sem="pe" if dc == NCH - 1 else None)
                        tm1 = P.cnt["pe"]
                        for dc in range(NCH):
                            P.op("tensor", "matmul", p2,
                                 w2q[:, slot, dc, cq * 128:(cq + 1) * 128],
                                 xln[:, dc, nh * 512:(nh + 1) * 512],
                                 start=(dc == 0), stop=(dc == NCH - 1),
                                 sem="pe" if dc == NCH - 1 else None)
                        tm2 = P.cnt["pe"]
                        P.wait("scalar", "pe", tm1)
                        col = e * 32 + c
                        ta = P.op("scalar", "activation", s1[:, k, :], p1,
                                  AF.Silu, bias=b1_pp[:, col:col + 1],
                                  sem="act")
                        hg[k] = ("act", ta)
                        P.wait("vector", "pe", tm2)
                        P.wait("vector", "act", ta)
                        tv = P.op("vector", "scalar_tensor_tensor",
                                  hh[:, c, nh * 512:(nh + 1) * 512], p2,
                                  b2_pp[:, col:col + 1], s1[:, k, :],
                                  ALU.add, ALU.mult, sem="dve")
                        hg[2 + k] = ("dve", tv)
                w1_free[slot] = w23_free[slot] = P.cnt["pe"]
                if q8 + 2 < 8:
                    issue_a(q8 + 2)
            t_h = ("dve", P.cnt["dve"])

            w3st = [0, 0]

            def w3_lhsT(slot, rc, dc):
                off = slot * 4096 + rc * 1024 + dc * 128
                return bass.AP(w2q.ap().tensor, off, [[8192, 128], [1, 128]])

            def w3_dst(slot):
                return bass.AP(w2q.ap().tensor, slot * 4096,
                               [[8192, 128], [1024, 4], [1, 1024]])

            def issue_w3(step, e=e):
                q8 = step % 8
                slot = step % 2
                P.wait("gpsimd", "pe", w23_free[slot])
                w3st[slot] = P.dma(
                    "gpsimd", w3_dst(slot),
                    W3h[e][q8 * 512:(q8 + 1) * 512, :]
                    .rearrange("(c p) f -> p c f", p=128), sem="dmag")

            issue_w3(0)
            issue_w3(1)
            for step in range(16):
                th, q8 = step // 8, step % 8
                slot = step % 2
                P.wait("tensor", "dmag", w3st[slot])
                if step == 0:
                    P.wait("tensor", t_h[0], t_h[1])
                    for dc in range(8):
                        P.wait("tensor", yg[dc][0], yg[dc][1])
                tm = None
                for rc in range(4):
                    c = q8 * 4 + rc
                    for dc in range(8):
                        reg = (psA if dc < 4 else psB)[
                            :, (dc % 4) * 512:(dc % 4 + 1) * 512]
                        tm = P.op("tensor", "matmul", reg,
                                  w3_lhsT(slot, rc, dc),
                                  hh[:, c, th * 512:(th + 1) * 512],
                                  start=(c == 0), stop=(c == 31),
                                  sem="pe" if (rc == 3 and dc == 7) else None)
                w23_free[slot] = tm
                if step + 2 < 16:
                    issue_w3(step + 2)
                if q8 == 7:
                    P.wait("vector", "pe", tm)
                    P.wait("vector", "dma", t_cwb)
                    P.wait("vector", "dve", t_y0)
                    for dc in range(8):
                        reg = (psA if dc < 4 else psB)[
                            :, (dc % 4) * 512:(dc % 4 + 1) * 512]
                        col = e * 8 + dc
                        P.op("vector", "scalar_tensor_tensor", tt1[:], reg,
                             b3_pp[:, col:col + 1],
                             cwb[:, th * 512:(th + 1) * 512], ALU.add,
                             ALU.mult)
                        tv = P.op("vector", "tensor_tensor",
                                  yy[:, dc, th * 512:(th + 1) * 512],
                                  yy[:, dc, th * 512:(th + 1) * 512], tt1[:],
                                  ALU.add, sem="dve")
                        yg[dc] = ("dve", tv)
                    if th == 0:
                        for dc in range(8):
                            P.wait("tensor", yg[dc][0], yg[dc][1])
            hg = [("dve", P.cnt["dve"])] * 4

        for sub in range(2):
            t0g = g * 1024 + sub * 512
            s_, lo = t0g // SEQ, t0g % SEQ
            P.wait("sync", "dve", P.cnt["dve"])
            P.wait("sync", "st", t_store)
            tx = P.dma("sync", xg[:], x_in[s_, :, :, lo:lo + 512]
                       .rearrange("c p t -> p c t"))
            P.wait("vector", "dma", tx)
            tv = P.op("vector", "tensor_tensor", xg[:],
                      xg[:], yy[:, :, sub * 512:(sub + 1) * 512], ALU.add,
                      sem="dve")
            P.wait("sync", "dve", tv)
            t_store = P.dma("sync", x_out[s_, :, :, lo:lo + 512]
                            .rearrange("c p t -> p c t"), xg[:], sem="st")
    P.wait("sync", "st", P.cnt["st"])
    st.close()



def emit_output(P, com, io):
    nc = P.nc
    P.new_phase()
    st = ExitStack()
    x_in = io["x_fm_in"]
    out = io["out"]
    xi = st.enter_context(_sb(nc, "o_xi", [128, 8, 128], F32))
    xo = st.enter_context(_sb(nc, "o_xo", [128, 8, 128], F32))
    pst = [st.enter_context(_psm(nc, f"o_ps{i}", [128, 128], F32))
           for i in range(2)]
    t_st = 0
    for s in range(NSEQ):
        for tc in range(16):
            P.wait("sync", "dve", P.cnt.get("dve", 0))
            td = P.dma("sync", xi[:],
                       x_in[s, :, :, tc * 128:(tc + 1) * 128]
                       .rearrange("c p t -> p c t"))
            P.wait("tensor", "dma", td)
            P.wait("tensor", "dve", com.ready)
            for dc in range(NCH):
                pb = pst[dc % 2]
                tp = P.op("tensor", "transpose", pb[:], xi[:, dc, :],
                          com.ident_f[:], sem="pe")
                P.wait("vector", "pe", tp)
                if dc == 0:
                    P.wait("vector", "st", t_st)
                tv = P.op("vector", "tensor_copy", xo[:, dc, :], pb[:],
                          sem="dve")
                P.wait("tensor", "dve", tv)
            P.wait("sync", "dve", P.cnt["dve"])
            t_st = P.dma("sync",
                         out[s, tc * 128:(tc + 1) * 128, :]
                         .rearrange("t (c p) -> t c p", p=128),
                         xo[:], sem="st")
    P.wait("sync", "st", P.cnt["st"])
    st.close()



def _din(nc, name, shape, dt):
    return nc.dram_tensor(name, list(shape), dt, kind="ExternalInput").ap()


def _dout(nc, name, shape, dt):
    return nc.dram_tensor(name, list(shape), dt, kind="ExternalOutput").ap()


XFM_SHAPE = (NSEQ, 8, 128, 2048)


def build_prep():
    nc = bass.Bass("TRN2", target_bir_lowering=False)
    io = {
        "x_in": _din(nc, "x_in", (NSEQ, SEQ, D), F32),
        "q": _din(nc, "q", (6, D, D), I32),
        "scales": _din(nc, "scales", (6, D, 64), F32),
        "lora_a": _din(nc, "lora_a", (6, 32, D), F32),
        "lora_b": _din(nc, "lora_b", (6, D, 32), F32),
        "w_proj": _din(nc, "w_proj", (4 * D, D), F32),
        "w_delta": _din(nc, "w_delta", (D, D), F32),
        "x_fm": _dout(nc, "x_fm", XFM_SHAPE, F32),
        "wt_ffn0": _dout(nc, "wt_ffn0", (6, D, D), BF16),
        "laT": _dout(nc, "laT", (128, 6, 8, 32), BF16),
        "lbT": _dout(nc, "lbT", (32, 6, 8, 128), BF16),
        "wprojT": _dout(nc, "wprojT", (D, 4 * D), BF16),
        "wdeltaT": _dout(nc, "wdeltaT", (D, D), BF16),
        "wcombT": _dout(nc, "wcombT", (D, D), BF16),
    }
    P = Prog(nc)
    com = Commons(P)
    emit_prep(P, com, io)
    P.build()
    return nc


def build_ffn0(n_groups=4, n_blocks=3):
    nc = bass.Bass("TRN2", target_bir_lowering=False)
    io = {
        "x_fm_in": _din(nc, "x_fm_in", XFM_SHAPE, F32),
        "wt_ffn0": _din(nc, "wt_ffn0", (6, D, D), BF16),
        "laT": _din(nc, "laT", (128, 6, 8, 32), BF16),
        "lbT": _din(nc, "lbT", (32, 6, 8, 128), BF16),
        "biases": _din(nc, "biases", (6, D), F32),
        "ln_gamma": _din(nc, "ln_gamma", (3, D), F32),
        "ln_beta": _din(nc, "ln_beta", (3, D), F32),
        "x_fm_out": _dout(nc, "x_fm_out", XFM_SHAPE, F32),
    }
    P = Prog(nc)
    com = Commons(P)
    emit_ffn0(P, com, io, n_groups, n_blocks)
    P.build()
    return nc


def build_mixer():
    nc = bass.Bass("TRN2", target_bir_lowering=False)
    io = {
        "x_fm_in": _din(nc, "x_fm_in", XFM_SHAPE, F32),
        "wprojT": _din(nc, "wprojT", (D, 4 * D), BF16),
        "wcombT": _din(nc, "wcombT", (D, D), BF16),
        "A": _din(nc, "A", (1, 1, D), F32),
        "Bp": _din(nc, "Bp", (1, D, 1), F32),
        "D": _din(nc, "D", (1,), F32),
        "b_delta": _din(nc, "b_delta", (D,), F32),
        "x_fm_out": _dout(nc, "x_fm_out", XFM_SHAPE, F32),
        "dbg_uc": _dout(nc, "dbg_uc", (128, 8), F32),
        "dbg_cs": _dout(nc, "dbg_cs", (128, 8), F32),
        "dbg_dlt": _dout(nc, "dbg_dlt", (128, 2048), F32),
        "dbg_pre": _dout(nc, "dbg_pre", (128, 2048), F32),
        "dbg_bd1": _dout(nc, "dbg_bd1", (128, 8), F32),
        "dbg_bd2": _dout(nc, "dbg_bd2", (128, 8), F32),
        "dbg_abs": _dout(nc, "dbg_abs", (128, 2048), F32),
        "dbg_ln": _dout(nc, "dbg_ln", (128, 2048), F32),
        "dbg_scan": _dout(nc, "dbg_scan", (128, 2048), F32),
    }
    P = Prog(nc)
    import os
    P.serialize = tuple((os.environ.get("KSER") or "").split(",")) if os.environ.get("KSER") else ()
    com = Commons(P)
    emit_mixer(P, com, io)
    P.build()
    return nc


def build_moe():
    nc = bass.Bass("TRN2", target_bir_lowering=False)
    io = {
        "x_fm_in": _din(nc, "x_fm_in", XFM_SHAPE, F32),
        "Wr": _din(nc, "Wr", (D, 8), F32),
        "br": _din(nc, "br", (8,), F32),
        "W1": _din(nc, "W1", (NEXP, D, DFF), F32),
        "b1": _din(nc, "b1", (NEXP, DFF), F32),
        "W2": _din(nc, "W2", (NEXP, D, DFF), F32),
        "b2": _din(nc, "b2", (NEXP, DFF), F32),
        "W3": _din(nc, "W3", (NEXP, DFF, D), F32),
        "b3": _din(nc, "b3", (NEXP, D), F32),
        "x_fm_out": _dout(nc, "x_fm_out", XFM_SHAPE, F32),
        "cw_scr": nc.dram_tensor("cw_scr", [8, 4096], F32,
                                 kind="Internal").ap(),
    }
    P = Prog(nc)
    com = Commons(P)
    emit_moe(P, com, io)
    P.build()
    return nc


def build_attn():
    nc = bass.Bass("TRN2", target_bir_lowering=False)
    io = {
        "x_fm_in": _din(nc, "x_fm_in", XFM_SHAPE, F32),
        "W_Q": _din(nc, "W_Q", (D, D), F32),
        "W_K": _din(nc, "W_K", (D, D), F32),
        "W_V": _din(nc, "W_V", (D, D), F32),
        "W_o": _din(nc, "W_o", (D, D), F32),
        "x_fm_out": _dout(nc, "x_fm_out", XFM_SHAPE, F32),
    }
    P = Prog(nc)
    com = Commons(P)
    emit_attn(P, com, io)
    P.build()
    return nc


def _shard_inputs(inputs):
    x = np.ascontiguousarray(
        inputs["L0_attn_x"].reshape(8, NSEQ, SEQ, D))
    return x


def build_output():
    nc = bass.Bass("TRN2", target_bir_lowering=False)
    io = {
        "x_fm_in": _din(nc, "x_fm_in", XFM_SHAPE, F32),
        "out": _dout(nc, "out", (NSEQ, SEQ, D), F32),
    }
    P = Prog(nc)
    com = Commons(P)
    emit_output(P, com, io)
    P.build()
    return nc


def build_fused():
    nc = bass.Bass("TRN2", target_bir_lowering=False)
    def _int(name, shape, dt):
        return nc.dram_tensor(name, list(shape), dt, kind="Internal").ap()
    xfm = _int("x_fm", XFM_SHAPE, F32)
    io = {
        "x_in": _din(nc, "x_in", (NSEQ, SEQ, D), F32),
        "q": _din(nc, "q", (6, D, D), I32),
        "scales": _din(nc, "scales", (6, D, 64), F32),
        "lora_a": _din(nc, "lora_a", (6, 32, D), F32),
        "lora_b": _din(nc, "lora_b", (6, D, 32), F32),
        "w_proj": _din(nc, "w_proj", (4 * D, D), F32),
        "w_delta": _din(nc, "w_delta", (D, D), F32),
        "biases": _din(nc, "biases", (6, D), F32),
        "ln_gamma": _din(nc, "ln_gamma", (3, D), F32),
        "ln_beta": _din(nc, "ln_beta", (3, D), F32),
        "A": _din(nc, "A", (1, 1, D), F32),
        "Bp": _din(nc, "Bp", (1, D, 1), F32),
        "D": _din(nc, "D", (1,), F32),
        "b_delta": _din(nc, "b_delta", (D,), F32),
        "Wr": _din(nc, "Wr", (D, 8), F32),
        "br": _din(nc, "br", (8,), F32),
        "W1": _din(nc, "W1", (NEXP, D, DFF), F32),
        "b1": _din(nc, "b1", (NEXP, DFF), F32),
        "W2": _din(nc, "W2", (NEXP, D, DFF), F32),
        "b2": _din(nc, "b2", (NEXP, DFF), F32),
        "W3": _din(nc, "W3", (NEXP, DFF, D), F32),
        "b3": _din(nc, "b3", (NEXP, D), F32),
        "W_Q": _din(nc, "W_Q", (D, D), F32),
        "W_K": _din(nc, "W_K", (D, D), F32),
        "W_V": _din(nc, "W_V", (D, D), F32),
        "W_o": _din(nc, "W_o", (D, D), F32),
        "x_fm": xfm, "x_fm_in": xfm, "x_fm_out": xfm,
        "wt_ffn0": _int("wt_ffn0", (6, D, D), BF16),
        "laT": _int("laT", (128, 6, 8, 32), BF16),
        "lbT": _int("lbT", (32, 6, 8, 128), BF16),
        "wprojT": _int("wprojT", (D, 4 * D), BF16),
        "wdeltaT": _int("wdeltaT", (D, D), BF16),
        "wcombT": _int("wcombT", (D, D), BF16),
        "cw_scr": _int("cw_scr", (8, 4096), F32),
        "W1c": _int("W1c", (NEXP, D, DFF), BF16),
        "W2c": _int("W2c", (NEXP, D, DFF), BF16),
        "W3c": _int("W3c", (NEXP, DFF, D), BF16),
        "wqkvo_c": _int("wqkvo_c", (4, D, D), BF16),
        "wr_c": _int("wr_c", (D, 8), BF16),
        "out": _dout(nc, "out", (NSEQ, SEQ, D), F32),
    }
    P = Prog(nc)
    com = Commons(P)
    emit_prep(P, com, io)
    emit_ffn0(P, com, io)
    emit_mixer(P, com, io)
    emit_ffn0(P, com, io)
    emit_moe(P, com, io)
    emit_attn(P, com, io)
    emit_moe(P, com, io)
    emit_output(P, com, io)
    P.build()
    return nc


def kernel(**inputs):
    x_sh = _shard_inputs(inputs)
    n = 8
    nc = build_fused()
    in_maps = [{
        "x_in": x_sh[c],
        "q": inputs["L0_ffn_qweights"],
        "scales": inputs["L0_ffn_scales"],
        "lora_a": inputs["L0_ffn_lora_a"],
        "lora_b": inputs["L0_ffn_lora_b"],
        "w_proj": inputs["L0_attn_W_proj"],
        "w_delta": inputs["L0_attn_W_delta"],
        "biases": inputs["L0_ffn_biases"],
        "ln_gamma": inputs["L0_ffn_ln_gamma"],
        "ln_beta": inputs["L0_ffn_ln_beta"],
        "A": inputs["L0_attn_A"],
        "Bp": inputs["L0_attn_Bp"],
        "D": inputs["L0_attn_D"],
        "b_delta": inputs["L0_attn_b_delta"],
        "Wr": inputs["L1_ffn_Wr"], "br": inputs["L1_ffn_br"],
        "W1": inputs["L1_ffn_W1"], "b1": inputs["L1_ffn_b1"],
        "W2": inputs["L1_ffn_W2"], "b2": inputs["L1_ffn_b2"],
        "W3": inputs["L1_ffn_W3"], "b3": inputs["L1_ffn_b3"],
        "W_Q": inputs["L1_attn_W_Q"], "W_K": inputs["L1_attn_W_K"],
        "W_V": inputs["L1_attn_W_V"], "W_o": inputs["L1_attn_W_o"],
    } for c in range(n)]
    res = run_bass_kernel_spmd(nc, in_maps, core_ids=list(range(n))).results
    out = np.concatenate([np.asarray(r["out"]) for r in res], axis=0)
    return out.reshape(16, SEQ, D)


def kernel_unfused(**inputs):
    x_sh = _shard_inputs(inputs)
    n = 8

    def run(nc, maps):
        return run_bass_kernel_spmd(nc, maps, core_ids=list(range(n))).results

    res = run(build_prep(), [{
        "x_in": x_sh[c],
        "q": inputs["L0_ffn_qweights"],
        "scales": inputs["L0_ffn_scales"],
        "lora_a": inputs["L0_ffn_lora_a"],
        "lora_b": inputs["L0_ffn_lora_b"],
        "w_proj": inputs["L0_attn_W_proj"],
        "w_delta": inputs["L0_attn_W_delta"],
    } for c in range(n)])
    state = [dict(r) for r in res]

    nc_ffn0 = build_ffn0()

    def ffn0_maps():
        return [{
            "x_fm_in": state[c]["x_fm"],
            "wt_ffn0": state[c]["wt_ffn0"],
            "laT": state[c]["laT"],
            "lbT": state[c]["lbT"],
            "biases": inputs["L0_ffn_biases"],
            "ln_gamma": inputs["L0_ffn_ln_gamma"],
            "ln_beta": inputs["L0_ffn_ln_beta"],
        } for c in range(n)]

    res = run(nc_ffn0, ffn0_maps())
    for c in range(n):
        state[c]["x_fm"] = res[c]["x_fm_out"]

    res = run(build_mixer(), [{
        "x_fm_in": state[c]["x_fm"],
        "wprojT": state[c]["wprojT"],
        "wcombT": state[c]["wcombT"],
        "A": inputs["L0_attn_A"],
        "Bp": inputs["L0_attn_Bp"],
        "D": inputs["L0_attn_D"],
        "b_delta": inputs["L0_attn_b_delta"],
    } for c in range(n)])
    for c in range(n):
        state[c]["x_fm"] = res[c]["x_fm_out"]

    res = run(nc_ffn0, ffn0_maps())
    for c in range(n):
        state[c]["x_fm"] = res[c]["x_fm_out"]

    nc_moe = build_moe()

    def moe_maps():
        return [{
            "x_fm_in": state[c]["x_fm"],
            "Wr": inputs["L1_ffn_Wr"], "br": inputs["L1_ffn_br"],
            "W1": inputs["L1_ffn_W1"], "b1": inputs["L1_ffn_b1"],
            "W2": inputs["L1_ffn_W2"], "b2": inputs["L1_ffn_b2"],
            "W3": inputs["L1_ffn_W3"], "b3": inputs["L1_ffn_b3"],
        } for c in range(n)]

    res = run(nc_moe, moe_maps())
    for c in range(n):
        state[c]["x_fm"] = res[c]["x_fm_out"]

    res = run(build_attn(), [{
        "x_fm_in": state[c]["x_fm"],
        "W_Q": inputs["L1_attn_W_Q"], "W_K": inputs["L1_attn_W_K"],
        "W_V": inputs["L1_attn_W_V"], "W_o": inputs["L1_attn_W_o"],
    } for c in range(n)])
    for c in range(n):
        state[c]["x_fm"] = res[c]["x_fm_out"]

    res = run(nc_moe, moe_maps())
    for c in range(n):
        state[c]["x_fm"] = res[c]["x_fm_out"]

    res = run(build_output(), [{"x_fm_in": state[c]["x_fm"]}
                               for c in range(n)])
    out = np.concatenate([np.asarray(res[c]["out"]) for c in range(n)],
                         axis=0)
    return out.reshape(16, SEQ, D)
